# version 09
# speedup vs baseline: 1.4335x; 1.1934x over previous
import numpy as np
import contextlib
import concourse.bass as bass
import concourse.mybir as mybir
from concourse.bass_utils import run_bass_kernel_spmd

F32 = mybir.dt.float32
BF16 = mybir.dt.bfloat16
AF = mybir.ActivationFunctionType
ALU = mybir.AluOpType
AX = mybir.AxisListType

NCORES = 8
T = 1024
NT = 8
D = 2048
KC = 16
NE = 32
I_FF = 768
IC = 6
NH = 16
HD = 128
BLK = 256
NB = 4
EPS_LN = 1e-5
EPS_RMS = 1e-6
SPARSE = True
CAP = 384
RW = 2080
NG = CAP // 128


class S:
    __slots__ = ("h", "c")

    def __init__(self, h):
        self.h = h
        self.c = 0


class Em:

    ENGS = ("tensor", "vector", "scalar", "gpsimd", "sync")

    def __init__(self, nc):
        self.nc = nc
        self.items = {k: [] for k in self.ENGS}

    def on(self, eng, fn, waits=(), inc=None, inc_n=1):
        ws = tuple((s.h, int(v)) for (s, v) in waits if v > 0)
        pair = None
        if inc is not None:
            inc.c += inc_n
            pair = (inc.h, inc_n)

        def run(e, fn=fn, ws=ws, pair=pair):
            for h, v in ws:
                e.wait_ge(h, v)
            r = fn(e)
            if pair is not None:
                r.then_inc(*pair)
        self.items[eng].append(run)

    def wait(self, eng, waits):
        ws = tuple((s.h, int(v)) for (s, v) in waits if v > 0)

        def run(e, ws=ws):
            for h, v in ws:
                e.wait_ge(h, v)
        self.items[eng].append(run)

    def flush(self):
        with self.nc.Block() as b:
            for eng in self.ENGS:
                lst = self.items[eng]
                if lst:
                    getattr(b, eng)(lambda e, lst=list(lst): [f(e) for f in lst])
        self.items = {k: [] for k in self.ENGS}


def build_nc(stage="full"):
    nc = bass.Bass("TRN2", target_bir_lowering=False)

    d_x = nc.dram_tensor("x_in", [T, D], F32, kind="ExternalInput").ap()
    d_wqkvuT = nc.dram_tensor("wqkvuT", [D, 4 * D], F32, kind="ExternalInput").ap()
    d_woutT = nc.dram_tensor("woutT", [D, D], F32, kind="ExternalInput").ap()
    d_rT = nc.dram_tensor("rT", [D, NE], F32, kind="ExternalInput").ap()
    d_wg = nc.dram_tensor("w_gate", [NE, D, I_FF], F32, kind="ExternalInput").ap()
    d_wu = nc.dram_tensor("w_up", [NE, D, I_FF], F32, kind="ExternalInput").ap()
    d_wd = nc.dram_tensor("w_down", [NE, I_FF, D], F32, kind="ExternalInput").ap()
    d_slope = nc.dram_tensor("slope_bc", [128, NH], F32, kind="ExternalInput").ap()
    d_arr = nc.dram_tensor("arr_b", [128, BLK], F32, kind="ExternalInput").ap()
    d_pos = nc.dram_tensor("pos_b", [128, T], F32, kind="ExternalInput").ap()
    d_idxT = nc.dram_tensor("idxT_b", [128, 2, BLK], F32, kind="ExternalInput").ap()
    d_krev = nc.dram_tensor("krev_b", [128, 2], F32, kind="ExternalInput").ap()
    d_c256 = nc.dram_tensor("c256_b", [128, 1], F32, kind="ExternalInput").ap()
    d_delta = nc.dram_tensor("delta_b", [128, NCORES], F32, kind="ExternalInput").ap()
    d_maskg = nc.dram_tensor("maskg_b", [128, NCORES], F32, kind="ExternalInput").ap()
    d_ident = nc.dram_tensor("ident", [128, 128], BF16, kind="ExternalInput").ap()

    d_out = nc.dram_tensor("out", [T, D], F32, kind="ExternalOutput").ap()

    d_x2s = nc.dram_tensor("x2s", [T, D], F32, kind="Internal").ap()
    d_x3s = nc.dram_tensor("x3s", [T, D], F32, kind="Internal").ap()
    d_eg = nc.dram_tensor("eg", [NE * CAP, RW], BF16, kind="Internal").ap()
    d_eo = nc.dram_tensor("eo", [NE * CAP, D], BF16, kind="Internal").ap()
    d_ltri = nc.dram_tensor("ltri", [128, 128], F32, kind="ExternalInput").ap()
    d_erange = nc.dram_tensor("erange_b", [128, NE], F32, kind="ExternalInput").ap()
    d_g128p = nc.dram_tensor("g128p_b", [128, NG], F32, kind="ExternalInput").ap()
    d_tokhi = nc.dram_tensor("tokhi_b", [128, NT], F32, kind="ExternalInput").ap()
    d_toklo = nc.dram_tensor("toklo_b", [128, NT], F32, kind="ExternalInput").ap()
    d_ones2 = nc.dram_tensor("ones2_b", [128, 2], F32, kind="ExternalInput").ap()
    d_onesrow = nc.dram_tensor("onesrow_b", [1, 128], F32, kind="ExternalInput").ap()
    d_ecap = nc.dram_tensor("ecap_b", [1, NE], F32, kind="ExternalInput").ap()
    d_ccin = nc.dram_tensor("ccin", [NH * HD, HD], F32, kind="Internal").ap()
    d_ccout = nc.dram_tensor("ccout", [NCORES * NH * HD, HD], F32, kind="Internal").ap()

    x_r = d_x.rearrange("(m p) d -> m p d", p=128)
    x2_r = d_x2s.rearrange("(m p) d -> m p d", p=128)
    x3_r = d_x3s.rearrange("(m p) d -> m p d", p=128)
    out_r = d_out.rearrange("(m p) d -> m p d", p=128)
    wg_r = d_wg.rearrange("e (k p) j -> e p k j", p=128)
    wu_r = d_wu.rearrange("e (k p) j -> e p k j", p=128)
    wd_r = d_wd.rearrange("e (k p) j -> e p k j", p=128)
    wq_r = d_wqkvuT.rearrange("(k p) j -> p k j", p=128)
    wo_r = d_woutT.rearrange("(k p) j -> p k j", p=128)
    rT_r = d_rT.rearrange("(k p) j -> p k j", p=128)
    eo_r = d_eo.rearrange("(r p) d -> r p d", p=128)
    ccin_r = d_ccin.rearrange("(h p) e -> p h e", p=128)
    ccout_r = d_ccout.rearrange("(j h p) e -> p j h e", j=NCORES, h=NH, p=128)

    st = contextlib.ExitStack()
    with st:
        def sem(name):
            return S(st.enter_context(nc.semaphore(name)))

        s_x = sem("s_x")
        s_w = sem("s_w")
        s_ld = [sem(f"s_ld{i}") for i in range(2)]
        s_st = [sem(f"s_st{i}") for i in range(2)]
        s_wg = sem("s_wg")
        s_wu = sem("s_wu")
        s_wd = sem("s_wd")
        s_pb = [sem(f"s_pb{i}") for i in range(4)]
        s_sc = [sem(f"s_sc{i}") for i in range(2)]
        s_eg = [sem(f"s_eg{i}") for i in range(2)]
        s_eo = [sem(f"s_eo{i}") for i in range(2)]
        s_gb = [sem(f"s_gb{i}") for i in range(4)]
        s_g = [sem(f"s_g{i}") for i in range(2)]
        s_pe = sem("s_pe")
        s_act = sem("s_act")
        s_pl = sem("s_pl")
        s_dve = sem("s_dve")
        s_cc = sem("s_cc")
        DMA_SEMS = ([s_x, s_w] + s_ld + s_st + [s_wg, s_wu, s_wd] + s_pb + s_g +
                    s_sc + s_eo + s_eg + s_gb)

        def all_dma_totals():
            return [(s, s.c) for s in DMA_SEMS]

        def phase_guard(em):
            tots = all_dma_totals()
            for _eng in Em.ENGS:
                em.wait(_eng, tots)

        c_ident = st.enter_context(nc.sbuf_tensor("c_ident", [128, 128], BF16))
        c_slope = st.enter_context(nc.sbuf_tensor("c_slope", [128, NH], F32))
        c_negsl = st.enter_context(nc.sbuf_tensor("c_negsl", [128, NH], F32))
        c_delta = st.enter_context(nc.sbuf_tensor("c_delta", [128, NCORES], F32))
        c_maskg = st.enter_context(nc.sbuf_tensor("c_maskg", [128, NCORES], F32))
        c_c256 = st.enter_context(nc.sbuf_tensor("c_c256", [128, 1], F32))
        rT_bf = st.enter_context(nc.sbuf_tensor("rT_bf", [128, KC, NE], BF16))
        c_ltri = st.enter_context(nc.sbuf_tensor("c_ltri", [128, 128], F32))
        c_erange = st.enter_context(nc.sbuf_tensor("c_erange", [128, NE], F32))
        c_g128p = st.enter_context(nc.sbuf_tensor("c_g128p", [128, NG], F32))
        c_tokhi = st.enter_context(nc.sbuf_tensor("c_tokhi", [128, NT], F32))
        c_toklo = st.enter_context(nc.sbuf_tensor("c_toklo", [128, NT], F32))
        c_ones2 = st.enter_context(nc.sbuf_tensor("c_ones2", [128, 2], F32))
        c_onesrow = st.enter_context(nc.sbuf_tensor("c_onesrow", [1, 128], F32))

        NS = lambda h: c_negsl[:, h:h + 1]

        em = Em(nc)
        em.on("sync", lambda e: e.dma_start(c_ident[:], d_ident), inc=s_x, inc_n=16)
        em.on("sync", lambda e: e.dma_start(c_slope[:], d_slope), inc=s_x, inc_n=16)
        em.on("sync", lambda e: e.dma_start(c_delta[:], d_delta), inc=s_x, inc_n=16)
        em.on("sync", lambda e: e.dma_start(c_maskg[:], d_maskg), inc=s_x, inc_n=16)
        em.on("sync", lambda e: e.dma_start(c_c256[:], d_c256), inc=s_x, inc_n=16)
        em.on("gpsimd", lambda e: e.dma_start(rT_bf[:], rT_r), inc=s_w, inc_n=16)
        em.on("sync", lambda e: e.dma_start(c_ltri[:], d_ltri), inc=s_x, inc_n=16)
        em.on("sync", lambda e: e.dma_start(c_erange[:], d_erange), inc=s_x, inc_n=16)
        em.on("sync", lambda e: e.dma_start(c_g128p[:], d_g128p), inc=s_x, inc_n=16)
        em.on("sync", lambda e: e.dma_start(c_tokhi[:], d_tokhi), inc=s_x, inc_n=16)
        em.on("sync", lambda e: e.dma_start(c_toklo[:], d_toklo), inc=s_x, inc_n=16)
        em.on("sync", lambda e: e.dma_start(c_ones2[:], d_ones2), inc=s_x, inc_n=16)
        em.on("sync", lambda e: e.dma_start(c_onesrow[:], d_onesrow), inc=s_x, inc_n=16)
        em.on("vector", lambda e: e.tensor_scalar(c_negsl[:], c_slope[:], -1.0, None,
                                                  ALU.mult),
              waits=[(s_x, s_x.c)], inc=s_dve)
        em.flush()


        def emit_ln(em, src_r, xn_fm, bufs, router_wr=None, entry_dma=(),
                    entry_sdve=0, dbg_dst_r=None, sp=None):
            (xt, sqdump, xn_tm, cols, lg, top8, eall, maskb, p_tr, p_rt) = bufs
            C = lambda i: cols[:, i:i + 1]
            act_affine = {}
            pe_tr_done = {}
            dbg_st = {}
            dbg_cp = {}
            dve_last = {-1: entry_sdve}
            for m in range(NT):
                if m == 0:
                    w = list(entry_dma)
                elif m >= 2:
                    w = [(s_act, act_affine[m - 2]), (s_st[m % 2], dbg_st.get(m - 2, 0))]
                else:
                    w = []
                em.on("sync", lambda e, m=m: e.dma_start(xt[m % 2][:], src_r[m]),
                      waits=w, inc=s_ld[m % 2], inc_n=16)
                ldx = s_ld[m % 2].c
                em.on("vector", lambda e, m=m: e.reduce_sum(C(0), xt[m % 2][:], axis=AX.X),
                      waits=[(s_ld[m % 2], ldx)] +
                            ([(s_act, act_affine[m - 1])] if m >= 1 else []),
                      inc=s_dve)
                sum_done = s_dve.c
                em.on("scalar", lambda e, m=m: e.activation(sqdump, xt[m % 2][:], AF.Square,
                                                            accum_out=C(7)),
                      waits=[(s_ld[m % 2], ldx), (s_dve, dve_last[m - 1])],
                      inc=s_act)
                ssq_done = s_act.c
                em.on("vector", lambda e: e.tensor_scalar(C(1), C(0), 1.0 / D, None,
                                                          ALU.mult),
                      waits=[(s_dve, sum_done)], inc=s_dve)
                mean_done = s_dve.c
                em.on("vector", lambda e: e.tensor_mul(C(2), C(1), C(1)),
                      waits=[(s_dve, mean_done)], inc=s_dve)
                m2_done = s_dve.c
                em.on("vector", lambda e: e.tensor_scalar(C(3), C(7), 1.0 / D, C(2),
                                                          ALU.mult, ALU.subtract),
                      waits=[(s_act, ssq_done), (s_dve, m2_done)], inc=s_dve)
                em.on("vector", lambda e: e.tensor_scalar(C(3), C(3), EPS_LN, None,
                                                          ALU.add),
                      waits=[(s_dve, s_dve.c)], inc=s_dve)
                vr_done = s_dve.c
                em.on("scalar", lambda e: e.activation(C(4), C(3), AF.Sqrt),
                      waits=[(s_dve, vr_done)], inc=s_act)
                sd_done = s_act.c
                em.on("vector", lambda e: e.reciprocal(C(5), C(4)),
                      waits=[(s_act, sd_done)], inc=s_dve)
                rinv_done = s_dve.c
                em.on("vector", lambda e: e.tensor_scalar(C(6), C(1), C(5), -1.0,
                                                          ALU.mult, ALU.mult),
                      waits=[(s_dve, rinv_done)], inc=s_dve)
                nb_done = s_dve.c
                w = [(s_dve, nb_done)]
                if m >= 1:
                    w.append((s_pe, pe_tr_done[m - 1]))
                    w.append((s_dve, dbg_cp.get(m - 1, 0)))
                if sp is not None:
                    xn_dst = sp["xns"][m % 2][:, 0:D]
                    w.append((s_sc[m % 2], sp["sc_cnt"][m % 2]))
                else:
                    xn_dst = xn_tm[:]
                em.on("scalar", lambda e, m=m, xn_dst=xn_dst: e.activation(
                          xn_dst, xt[m % 2][:], AF.Identity,
                          bias=C(6), scale=C(5)),
                      waits=w, inc=s_act)
                act_affine[m] = s_act.c
                cp_done = []
                cp_sem = []
                for c in range(KC):
                    semx = s_act if c % 2 == 0 else s_dve
                    eng = "scalar" if c % 2 == 0 else "vector"
                    w = [(s_act, act_affine[m])] if c < 2 else \
                        [(cp_sem[c - 2], cp_done[c - 2])]
                    tr_src = (sp["xns"][m % 2] if sp is not None else xn_tm)
                    em.on("tensor",
                          lambda e, c=c, tr_src=tr_src: e.transpose(
                              p_tr[c % 2][:],
                              tr_src[:, c * 128:(c + 1) * 128],
                              c_ident[:]),
                          waits=w, inc=s_pe)
                    tr = s_pe.c
                    if eng == "scalar":
                        em.on(eng, lambda e, c=c, m=m: e.copy(
                                  xn_fm[:, c, m * 128:(m + 1) * 128], p_tr[c % 2][:]),
                              waits=[(s_pe, tr)], inc=semx)
                    else:
                        em.on(eng, lambda e, c=c, m=m: e.tensor_copy(
                                  xn_fm[:, c, m * 128:(m + 1) * 128], p_tr[c % 2][:]),
                              waits=[(s_pe, tr)], inc=semx)
                    cp_done.append(semx.c)
                    cp_sem.append(semx)
                pe_tr_done[m] = s_pe.c
                xn_m_act, xn_m_dve = s_act.c, s_dve.c
                if dbg_dst_r is not None:
                    em.on("vector", lambda e, m=m: e.tensor_copy(xt[m % 2][:], xn_tm[:]),
                          waits=[(s_act, act_affine[m])], inc=s_dve)
                    dbg_cp[m] = s_dve.c
                    em.on("sync", lambda e, m=m: e.dma_start(dbg_dst_r[m], xt[m % 2][:]),
                          waits=[(s_dve, dbg_cp[m])], inc=s_st[m % 2], inc_n=16)
                    dbg_st[m] = s_st[m % 2].c
                if router_wr is not None:
                    for k in range(KC):
                        w = []
                        if k == 0:
                            w = [(s_act, xn_m_act), (s_dve, xn_m_dve), (s_w, 16),
                                 (s_dve, dve_last[m - 1])]
                        em.on("tensor", lambda e, k=k, m=m: e.matmul(
                                  p_rt[:], xn_fm[:, k, m * 128:(m + 1) * 128],
                                  rT_bf[:, k, :], start=(k == 0), stop=(k == KC - 1)),
                              waits=w, inc=(s_pe if k == KC - 1 else None))
                    rt_done = s_pe.c
                    em.on("vector", lambda e: e.tensor_copy(lg[:], p_rt[:]),
                          waits=[(s_pe, rt_done)], inc=s_dve)
                    em.on("vector", lambda e: e.max(top8[:], lg[:]),
                          waits=[(s_dve, s_dve.c)], inc=s_dve)
                    t8_done = s_dve.c
                    em.on("vector", lambda e: e.tensor_scalar(C(8), top8[:, 0:1], -1.0,
                                                              None, ALU.mult),
                          waits=[(s_dve, t8_done)], inc=s_dve)
                    nm0_done = s_dve.c
                    if sp is None:
                        em.on("scalar", lambda e: e.activation(eall[:], lg[:], AF.Exp,
                                                               bias=C(8)),
                              waits=[(s_dve, nm0_done)], inc=s_act)
                        eall_done = s_act.c
                        em.on("vector", lambda e: e.tensor_scalar(maskb[:], lg[:],
                                                                  top8[:, 7:8],
                                                                  None, ALU.is_ge),
                              waits=[(s_dve, t8_done)], inc=s_dve)
                        em.on("vector", lambda e: e.tensor_mul(eall[:], eall[:],
                                                               maskb[:]),
                              waits=[(s_act, eall_done), (s_dve, s_dve.c)],
                              inc=s_dve)
                        em.on("vector", lambda e: e.reduce_sum(C(9), eall[:],
                                                               axis=AX.X),
                              waits=[(s_dve, s_dve.c)], inc=s_dve)
                        em.on("vector", lambda e: e.reciprocal(C(10), C(9)),
                              waits=[(s_dve, s_dve.c)], inc=s_dve)
                        sinv_done = s_dve.c
                        em.on("vector", lambda e, m=m: e.tensor_scalar(
                                  router_wr[:, m, :], eall[:], C(10), 0.5,
                                  ALU.mult, ALU.mult),
                              waits=[(s_dve, sinv_done)], inc=s_dve)
                    else:
                        e8 = sp["e8"]; w8 = sp["w8"]; tidx = sp["tidx"]
                        tidxf = sp["tidxf"]; posS = sp["posS"]
                        poseS = sp["poseS"]; base = sp["base"]
                        slaf = sp["slaf"]; slau = sp["slau"]
                        p_pos = sp["p_pos"]; p_bb = sp["p_bb"]; p_cnt = sp["p_cnt"]
                        xns = sp["xns"][m % 2]
                        em.on("scalar", lambda e: e.activation(e8[:], top8[:], AF.Exp,
                                                               bias=C(8)),
                              waits=[(s_dve, nm0_done)], inc=s_act)
                        e8_done = s_act.c
                        em.on("vector", lambda e: e.reduce_sum(C(9), e8[:], axis=AX.X),
                              waits=[(s_act, e8_done)], inc=s_dve)
                        em.on("vector", lambda e: e.reciprocal(C(10), C(9)),
                              waits=[(s_dve, s_dve.c)], inc=s_dve)
                        sinv_done = s_dve.c
                        em.on("vector", lambda e: e.tensor_scalar(w8[:], e8[:], C(10),
                                                                  0.5, ALU.mult,
                                                                  ALU.mult),
                              waits=[(s_dve, sinv_done)], inc=s_dve)
                        w8_done = s_dve.c
                        em.on("vector", lambda e: e.tensor_scalar(maskb[:], lg[:],
                                                                  top8[:, 7:8],
                                                                  None, ALU.is_ge),
                              waits=[(s_dve, t8_done)], inc=s_dve)
                        mask_done = s_dve.c
                        em.on("vector", lambda e: e.max_index(tidx[:], top8[:], lg[:]),
                              waits=[(s_dve, mask_done)], inc=s_dve)
                        em.on("vector", lambda e: e.tensor_copy(tidxf[:], tidx[:]),
                              waits=[(s_dve, s_dve.c)], inc=s_dve)
                        tif_done = s_dve.c
                        em.on("tensor", lambda e: e.matmul(p_pos[:], c_ltri[:],
                                                           maskb[:], start=True,
                                                           stop=True),
                              waits=[(s_dve, mask_done),
                                     (s_dve, sp["posS_free"])], inc=s_pe)
                        em.on("tensor", lambda e: e.matmul(p_bb[:], c_onesrow[:],
                                                           base[:], start=True,
                                                           stop=True),
                              waits=[(s_dve, sp["base_done"])], inc=s_pe)
                        pos_pe = s_pe.c
                        em.on("vector", lambda e: e.tensor_copy(posS[:], p_pos[:]),
                              waits=[(s_pe, pos_pe)], inc=s_dve)
                        em.on("vector", lambda e: e.tensor_add(poseS[:], posS[:],
                                                               p_bb[:]),
                              waits=[(s_dve, s_dve.c)], inc=s_dve)
                        pose_done = s_dve.c
                        sp["posS_free"] = pose_done
                        em.on("tensor", lambda e: e.matmul(p_cnt[:],
                                                           c_ones2[:, 0:1],
                                                           maskb[:], start=True,
                                                           stop=True),
                              waits=[(s_dve, pose_done)], inc=s_pe)
                        cnt_pe = s_pe.c
                        em.on("vector", lambda e: e.tensor_add(base[:], base[:],
                                                               p_cnt[:]),
                              waits=[(s_pe, cnt_pe), (s_dve, s_dve.c)], inc=s_dve)
                        sp["base_done"] = s_dve.c
                        for k in range(8):
                            em.on("vector", lambda e, k=k: e.tensor_scalar(
                                      eall[:], c_erange[:], tidxf[:, k:k + 1],
                                      None, ALU.is_equal),
                                  waits=[(s_dve, tif_done if k == 0 else s_dve.c),
                                         (s_dve, pose_done)], inc=s_dve)
                            em.on("vector", lambda e: e.tensor_mul(eall[:], eall[:],
                                                                   poseS[:]),
                                  waits=[(s_dve, s_dve.c)], inc=s_dve)
                            em.on("vector", lambda e, k=k: e.reduce_sum(
                                      slaf[:, k:k + 1], eall[:], axis=AX.X),
                                  waits=[(s_dve, s_dve.c)], inc=s_dve)
                        em.on("vector", lambda e: e.tensor_scalar(slaf[:], slaf[:],
                                                                  float(NE * CAP - 1),
                                                                  None, ALU.min),
                              waits=[(s_dve, s_dve.c)], inc=s_dve)
                        em.on("vector", lambda e, m=m: e.tensor_copy(
                                  slau[:, m, :], slaf[:]),
                              waits=[(s_dve, s_dve.c)], inc=s_dve)
                        slau_done = s_dve.c
                        em.on("vector", lambda e, xns=xns: e.tensor_copy(
                                  xns[:, D:D + 8], tidxf[:]),
                              waits=[(s_dve, tif_done),
                                     (s_sc[m % 2], sp["sc_cnt"][m % 2])],
                              inc=s_dve)
                        em.on("vector", lambda e, xns=xns: e.tensor_copy(
                                  xns[:, D + 8:D + 16], w8[:]),
                              waits=[(s_dve, w8_done)], inc=s_dve)
                        em.on("vector", lambda e, m=m, xns=xns: e.tensor_copy(
                                  xns[:, D + 16:D + 17], c_tokhi[:, m:m + 1]),
                              inc=s_dve)
                        em.on("vector", lambda e, m=m, xns=xns: e.tensor_copy(
                                  xns[:, D + 17:D + 18], c_toklo[:, m:m + 1]),
                              inc=s_dve)
                        meta_done = s_dve.c
                        for k in range(8):
                            w = [(s_dve, meta_done), (s_dve, slau_done),
                                 (s_act, act_affine[m])]
                            if m == 0 and k == 0:
                                w += list(entry_dma)
                            em.on("gpsimd", lambda e, m=m, k=k, xns=xns:
                                  e.indirect_dma_start(
                                      d_eg,
                                      bass.IndirectOffsetOnAxis(
                                          slau[:, m, k:k + 1], 0),
                                      xns[:, 0:D + 18], None),
                                  waits=w, inc=s_sc[m % 2], inc_n=16)
                        sp["sc_cnt"][m % 2] = s_sc[m % 2].c
                dve_last[m] = s_dve.c
            return s_act.c, s_dve.c

        def emit_experts(em, xn_fm, router_wr, acc, wbufs, ghbufs, psums, ready,
                         entry_dma):
            wg_bf, wu_bf, wd_bf = wbufs
            g_s, h_bf = ghbufs
            pg, pu, pd = psums
            rdy_act, rdy_dve = ready
            pe_gate_end = pe_up_end = pe_down_end = None
            silu_cnt = []
            upev_cnt = []
            dnev_cnt = []
            for e_i in range(NE):
                em.on("gpsimd", lambda e, e_i=e_i: e.dma_start(wg_bf[:], wg_r[e_i]),
                      waits=([(s_pe, pe_gate_end)] if pe_gate_end is not None
                             else list(entry_dma)),
                      inc=s_wg, inc_n=16)
                wg_ld = s_wg.c
                em.on("gpsimd", lambda e, e_i=e_i: e.dma_start(wu_bf[:], wu_r[e_i]),
                      waits=([(s_pe, pe_up_end)] if pe_up_end is not None else []),
                      inc=s_wu, inc_n=16)
                wu_ld = s_wu.c
                em.on("gpsimd", lambda e, e_i=e_i: e.dma_start(wd_bf[:], wd_r[e_i]),
                      waits=([(s_pe, pe_down_end)] if pe_down_end is not None else []),
                      inc=s_wd, inc_n=16)
                wd_ld = s_wd.c
                gidx0 = len(silu_cnt)
                for th in range(2):
                    for c in range(IC):
                        g = len(silu_cnt)
                        for k in range(KC):
                            w = []
                            if k == 0:
                                w = [(s_wg, wg_ld)]
                                if g >= 2:
                                    w.append((s_act, silu_cnt[g - 2]))
                                if e_i == 0 and g == 0:
                                    w += [(s_act, rdy_act), (s_dve, rdy_dve)]
                            em.on("tensor", lambda e, k=k, c=c, th=th, g=g: e.matmul(
                                      pg[g % 2][:], wg_bf[:, k, c * 128:(c + 1) * 128],
                                      xn_fm[:, k, th * 512:(th + 1) * 512],
                                      start=(k == 0), stop=(k == KC - 1)),
                                  waits=w, inc=(s_pe if k == KC - 1 else None))
                        gate_pe = s_pe.c
                        em.on("scalar", lambda e, c=c, th=th, g=g: e.activation(
                                  g_s[:, c, th * 512:(th + 1) * 512], pg[g % 2][:],
                                  AF.Silu),
                              waits=[(s_pe, gate_pe)], inc=s_act)
                        silu_cnt.append(s_act.c)
                pe_gate_end = s_pe.c
                uidx0 = len(upev_cnt)
                for th in range(2):
                    for c in range(IC):
                        g = len(upev_cnt)
                        for k in range(KC):
                            w = []
                            if k == 0:
                                w = [(s_wu, wu_ld)]
                                if g >= 2:
                                    w.append((s_dve, upev_cnt[g - 2]))
                            em.on("tensor", lambda e, k=k, c=c, th=th, g=g: e.matmul(
                                      pu[g % 2][:], wu_bf[:, k, c * 128:(c + 1) * 128],
                                      xn_fm[:, k, th * 512:(th + 1) * 512],
                                      start=(k == 0), stop=(k == KC - 1)),
                                  waits=w, inc=(s_pe if k == KC - 1 else None))
                        up_pe = s_pe.c
                        em.on("vector", lambda e, c=c, th=th, g=g: e.tensor_mul(
                                  h_bf[:, c, th * 512:(th + 1) * 512], pu[g % 2][:],
                                  g_s[:, c, th * 512:(th + 1) * 512]),
                              waits=[(s_pe, up_pe),
                                     (s_act, silu_cnt[gidx0 + th * IC + c])],
                              inc=s_dve)
                        upev_cnt.append(s_dve.c)
                pe_up_end = s_pe.c
                for th in range(2):
                    h_rdy = upev_cnt[uidx0 + th * IC + IC - 1]
                    for mm in range(4):
                        m = th * 4 + mm
                        for q in range(4):
                            g = len(dnev_cnt)
                            for k in range(IC):
                                w = []
                                if k == 0:
                                    w = [(s_wd, wd_ld), (s_dve, h_rdy)]
                                    if g >= 2:
                                        w.append((s_dve, dnev_cnt[g - 2]))
                                em.on("tensor", lambda e, k=k, m=m, q=q, g=g: e.matmul(
                                          pd[g % 2][:],
                                          h_bf[:, k, m * 128:(m + 1) * 128],
                                          wd_bf[:, k, q * 512:(q + 1) * 512],
                                          start=(k == 0), stop=(k == IC - 1)),
                                      waits=w, inc=(s_pe if k == IC - 1 else None))
                            dn_pe = s_pe.c
                            if e_i == 0:
                                em.on("vector", lambda e, m=m, q=q, g=g: e.tensor_scalar(
                                          acc[:, m, q * 512:(q + 1) * 512], pd[g % 2][:],
                                          router_wr[:, m, 0:1], None, ALU.mult),
                                      waits=[(s_pe, dn_pe), (s_dve, rdy_dve)], inc=s_dve)
                            else:
                                em.on("vector", lambda e, m=m, q=q, g=g, e_i=e_i:
                                      e.scalar_tensor_tensor(
                                          acc[:, m, q * 512:(q + 1) * 512], pd[g % 2][:],
                                          router_wr[:, m, e_i:e_i + 1],
                                          acc[:, m, q * 512:(q + 1) * 512],
                                          ALU.mult, ALU.add),
                                      waits=[(s_pe, dn_pe)], inc=s_dve)
                            dnev_cnt.append(s_dve.c)
                pe_down_end = s_pe.c
            return s_dve.c

        def emit_resid(em, src_r, acc, dst_r, rings, experts_done, final=False):
            xt2, x2t = rings
            add_done = {}
            st_done = {}
            for m in range(NT):
                w = [(s_dve, experts_done)] if m == 0 else []
                if m >= 2:
                    w.append((s_dve, add_done[m - 2]))
                em.on("sync", lambda e, m=m: e.dma_start(xt2[m % 2][:], src_r[m]),
                      waits=w, inc=s_ld[m % 2], inc_n=16)
                ldx = s_ld[m % 2].c
                w = [(s_ld[m % 2], ldx)]
                if m == 0:
                    w.append((s_dve, experts_done))
                if m >= 2:
                    w.append((s_st[m % 2], st_done[m - 2]))
                em.on("vector", lambda e, m=m: e.tensor_add(x2t[m % 2][:], xt2[m % 2][:],
                                                            acc[:, m, :]),
                      waits=w, inc=s_dve)
                add_done[m] = s_dve.c
                em.on("sync", lambda e, m=m: e.dma_start(dst_r[m], x2t[m % 2][:]),
                      waits=[(s_dve, add_done[m])], inc=s_st[m % 2], inc_n=16)
                st_done[m] = s_st[m % 2].c
            if final:
                em.wait("sync", [(s_st[0], s_st[0].c), (s_st[1], s_st[1].c)])

        def ffn_pass(src_r, dst_r, final=False, tag="a"):
            entry_dma = all_dma_totals()
            entry_sdve = s_dve.c
            with contextlib.ExitStack() as ps:
                em = Em(nc)
                phase_guard(em)
                xn_fm = ps.enter_context(nc.sbuf_tensor("xn_fm_" + tag,
                                                        [128, KC, T], BF16))
                W_r = ps.enter_context(nc.sbuf_tensor("W_r_" + tag,
                                                      [128, NT, NE], F32))
                wg_bf = ps.enter_context(nc.sbuf_tensor("wg_bf_" + tag,
                                                        [128, KC, I_FF], BF16))
                wu_bf = ps.enter_context(nc.sbuf_tensor("wu_bf_" + tag,
                                                        [128, KC, I_FF], BF16))
                wd_bf = ps.enter_context(nc.sbuf_tensor("wd_bf_" + tag,
                                                        [128, IC, D], BF16))
                acc = ps.enter_context(nc.sbuf_tensor("acc_" + tag,
                                                      [128, NT, D], F32))
                cols = ps.enter_context(nc.sbuf_tensor("cols_" + tag, [128, 16], F32))
                lg = ps.enter_context(nc.sbuf_tensor("lg_" + tag, [128, NE], F32))
                top8 = ps.enter_context(nc.sbuf_tensor("top8_" + tag, [128, 8], F32))
                eall = ps.enter_context(nc.sbuf_tensor("eall_" + tag, [128, NE], F32))
                maskb = ps.enter_context(nc.sbuf_tensor("maskb_" + tag, [128, NE], F32))
                with contextlib.ExitStack() as lns:
                    xt = [lns.enter_context(nc.sbuf_tensor(f"xt{i}_" + tag,
                                                           [128, D], F32))
                          for i in range(2)]
                    xn_tm = lns.enter_context(nc.sbuf_tensor("xn_tm_" + tag,
                                                             [128, D], BF16))
                    with contextlib.ExitStack() as pss:
                        p_tr = [pss.enter_context(
                            nc.psum_tensor(f"p_tr{i}_" + tag, [128, 128], BF16))
                            for i in range(2)]
                        p_rt = pss.enter_context(nc.psum_tensor("p_rt_" + tag,
                                                                [128, NE], F32))
                        bufs = (xt, acc[:, 0, :], xn_tm, cols, lg, top8, eall, maskb,
                                p_tr, p_rt)
                        ready = emit_ln(em, src_r, xn_fm, bufs, router_wr=W_r,
                                        entry_dma=entry_dma, entry_sdve=entry_sdve)
                with contextlib.ExitStack() as exs:
                    g_s = exs.enter_context(nc.sbuf_tensor("g_s_" + tag,
                                                           [128, IC, T], BF16))
                    h_bf = exs.enter_context(nc.sbuf_tensor("h_bf_" + tag,
                                                            [128, IC, T], BF16))
                    pg = [exs.enter_context(nc.psum_tensor(f"pg{i}_" + tag,
                                                           [128, 512], F32))
                          for i in range(2)]
                    pu = [exs.enter_context(nc.psum_tensor(f"pu{i}_" + tag,
                                                           [128, 512], F32))
                          for i in range(2)]
                    pd = [exs.enter_context(nc.psum_tensor(f"pd{i}_" + tag,
                                                           [128, 512], F32))
                          for i in range(2)]
                    experts_done = emit_experts(em, xn_fm, W_r, acc,
                                                (wg_bf, wu_bf, wd_bf), (g_s, h_bf),
                                                (pg, pu, pd), ready, entry_dma)
                with contextlib.ExitStack() as rs:
                    xt2 = [rs.enter_context(nc.sbuf_tensor(f"xt2{i}_" + tag,
                                                           [128, D], F32))
                           for i in range(2)]
                    x2t = [rs.enter_context(nc.sbuf_tensor(f"x2t{i}_" + tag,
                                                           [128, D], F32))
                           for i in range(2)]
                    emit_resid(em, src_r, acc, dst_r, (xt2, x2t), experts_done,
                               final=final)
                    em.flush()

        def ffn_pass_sparse(src_r, dst_r, final=False, tag="a"):
            entry_dma = all_dma_totals()
            entry_sdve = s_dve.c
            with contextlib.ExitStack() as ps:
                em = Em(nc)
                phase_guard(em)
                T_ = tag
                xn_fm = ps.enter_context(nc.sbuf_tensor("sxn_" + T_,
                                                        [128, KC, T], BF16))
                wg_bf = ps.enter_context(nc.sbuf_tensor("swg_" + T_,
                                                        [128, KC, I_FF], BF16))
                wu_bf = ps.enter_context(nc.sbuf_tensor("swu_" + T_,
                                                        [128, KC, I_FF], BF16))
                wd_bf = ps.enter_context(nc.sbuf_tensor("swd_" + T_,
                                                        [128, IC, D], BF16))
                slau = ps.enter_context(nc.sbuf_tensor("slau_" + T_,
                                                       [128, NT, 8],
                                                       mybir.dt.uint32))
                cols = ps.enter_context(nc.sbuf_tensor("scols_" + T_,
                                                       [128, 16], F32))
                lg = ps.enter_context(nc.sbuf_tensor("slg_" + T_, [128, NE], F32))
                top8 = ps.enter_context(nc.sbuf_tensor("stop8_" + T_, [128, 8], F32))
                eall = ps.enter_context(nc.sbuf_tensor("seall_" + T_,
                                                       [128, NE], F32))
                maskb = ps.enter_context(nc.sbuf_tensor("smaskb_" + T_,
                                                        [128, NE], F32))
                e8 = ps.enter_context(nc.sbuf_tensor("se8_" + T_, [128, 8], F32))
                w8 = ps.enter_context(nc.sbuf_tensor("sw8_" + T_, [128, 8], F32))
                tidx = ps.enter_context(nc.sbuf_tensor("stidx_" + T_, [128, 8],
                                                       mybir.dt.uint32))
                tidxf = ps.enter_context(nc.sbuf_tensor("stidxf_" + T_,
                                                        [128, 8], F32))
                posS = ps.enter_context(nc.sbuf_tensor("sposS_" + T_,
                                                       [128, NE], F32))
                poseS = ps.enter_context(nc.sbuf_tensor("sposeS_" + T_,
                                                        [128, NE], F32))
                base = ps.enter_context(nc.sbuf_tensor("sbase_" + T_, [1, NE], F32))
                slaf = ps.enter_context(nc.sbuf_tensor("sslaf_" + T_, [128, 8], F32))
                cnt_bc = ps.enter_context(nc.sbuf_tensor("scntbc_" + T_,
                                                         [128, NE], F32))
                xns = [ps.enter_context(nc.sbuf_tensor(f"sxns{i}_" + T_,
                                                       [128, RW], BF16))
                       for i in range(2)]
                sp = {"xns": xns, "e8": e8, "w8": w8, "tidx": tidx, "tidxf": tidxf,
                      "posS": posS, "poseS": poseS, "base": base, "slaf": slaf,
                      "slau": slau, "sc_cnt": {0: s_sc[0].c, 1: s_sc[1].c},
                      "base_done": 0, "posS_free": 0}
                em.on("sync", lambda e: e.dma_start(base[:], d_ecap),
                      waits=entry_dma, inc=s_x, inc_n=16)
                base_ld = s_x.c
                em.on("vector", lambda e: e.tensor_copy(posS[0:1, :], base[:]),
                      waits=[(s_x, base_ld)], inc=s_dve)
                sp["base_done"] = s_dve.c
                with contextlib.ExitStack() as lns:
                    sdump = lns.enter_context(nc.sbuf_tensor("sdump_" + T_,
                                                             [128, D], F32))
                    xt = [lns.enter_context(nc.sbuf_tensor(f"sxt{i}_" + T_,
                                                           [128, D], F32))
                          for i in range(2)]
                    with contextlib.ExitStack() as pss:
                        p_tr = [pss.enter_context(
                            nc.psum_tensor(f"sptr{i}_" + T_, [128, 128], BF16))
                            for i in range(2)]
                        p_rt = pss.enter_context(nc.psum_tensor("sprt_" + T_,
                                                                [128, NE], F32))
                        p_pos = pss.enter_context(nc.psum_tensor("sppos_" + T_,
                                                                 [128, NE], F32))
                        p_bb = pss.enter_context(nc.psum_tensor("spbb_" + T_,
                                                                [128, NE], F32))
                        p_cnt = pss.enter_context(nc.psum_tensor("spcnt_" + T_,
                                                                 [1, NE], F32))
                        sp["p_pos"] = p_pos
                        sp["p_bb"] = p_bb
                        sp["p_cnt"] = p_cnt
                        bufs = (xt, sdump[:], None, cols, lg, top8, eall, maskb,
                                p_tr, p_rt)
                        ready = emit_ln(em, src_r, xn_fm, bufs, router_wr=True,
                                        entry_dma=entry_dma,
                                        entry_sdve=entry_sdve, sp=sp)
                        em.on("tensor", lambda e: e.matmul(p_bb[:], c_onesrow[:],
                                                           base[:], start=True,
                                                           stop=True),
                              waits=[(s_dve, sp["base_done"]),
                                     (s_dve, sp["posS_free"])], inc=s_pe)
                        cb_pe = s_pe.c
                        em.on("vector", lambda e: e.tensor_copy(cnt_bc[:], p_bb[:]),
                              waits=[(s_pe, cb_pe)], inc=s_dve)
                        cnt_bc_done = s_dve.c
                rdy_act, rdy_dve = ready
                with contextlib.ExitStack() as exs:
                    xg_st = [exs.enter_context(nc.sbuf_tensor(f"sxgst{i}_" + T_,
                                                              [128, RW], BF16))
                             for i in range(2)]
                    xg_fm = [exs.enter_context(nc.sbuf_tensor(f"sxgfm{i}_" + T_,
                                                              [128, KC, CAP],
                                                              BF16))
                             for i in range(2)]
                    g_s = exs.enter_context(nc.sbuf_tensor("sgs_" + T_,
                                                           [128, IC, CAP], BF16))
                    h_bf = exs.enter_context(nc.sbuf_tensor("shb_" + T_,
                                                            [128, IC, CAP], BF16))
                    out_st = [exs.enter_context(nc.sbuf_tensor(f"sost{i}_" + T_,
                                                               [128, D], BF16))
                              for i in range(2)]
                    wcols = [exs.enter_context(nc.sbuf_tensor(f"swc{i}_" + T_,
                                                              [128, NG], F32))
                             for i in range(2)]
                    toku = [exs.enter_context(nc.sbuf_tensor(f"stk{i}_" + T_,
                                                             [128, NG],
                                                             mybir.dt.uint32))
                            for i in range(2)]
                    mcol = exs.enter_context(nc.sbuf_tensor("smc_" + T_,
                                                            [128, 8], F32))
                    pg = [exs.enter_context(nc.psum_tensor(f"spg{i}_" + T_,
                                                           [128, CAP], F32))
                          for i in range(2)]
                    pu = [exs.enter_context(nc.psum_tensor(f"spu{i}_" + T_,
                                                           [128, CAP], F32))
                          for i in range(2)]
                    pd = [exs.enter_context(nc.psum_tensor(f"spd{i}_" + T_,
                                                           [128, 512], F32))
                          for i in range(2)]
                    p_tx = [exs.enter_context(nc.psum_tensor(f"sptx{i}_" + T_,
                                                             [128, 128], BF16))
                            for i in range(2)]
                    eg_rr = d_eg.rearrange("(r p) c -> r p c", p=128)
                    scat_done = [(s_sc[0], s_sc[0].c), (s_sc[1], s_sc[1].c)]
                    pe_gate_end = pe_up_end = pe_down_end = None
                    silu_cnt = []
                    upev_cnt = []
                    dnev_cnt = []
                    ld_cnt = {0: 0, 1: 0}
                    tr_cp = {}
                    wres = {}
                    eo_st = {}
                    lt_rdy = {}

                    def emit_gload(gi):
                        w = list(scat_done) if gi == 0 else []
                        if gi >= 2:
                            w.append((s_pe, tr_cp.get(gi - 2, 0)))
                            w.append((s_dve, wres.get(gi - 2, 0)))
                        em.on("sync", lambda e, gi=gi:
                              e.dma_start(xg_st[gi % 2][:], eg_rr[gi]),
                              waits=w, inc=s_eg[gi % 2], inc_n=16)
                        ld_cnt[gi % 2] = s_eg[gi % 2].c

                    def emit_tx(gi, g, c, par):
                        rpar, ldc = gi % 2, ld_cnt[gi % 2]
                        if c < 2:
                            w = [(s_eg[rpar], ldc),
                                 tr_cp.get((gi - 1, KC - 2 + c), (s_act, 0))]
                        else:
                            w = [tr_cp.get((gi, c - 2), (s_act, 0))]
                        em.on("tensor", lambda e, c=c, gi=gi: e.transpose(
                                  p_tx[c % 2][:],
                                  xg_st[gi % 2][:, c * 128:(c + 1) * 128],
                                  c_ident[:]),
                              waits=w, inc=s_pe)
                        trp = s_pe.c
                        em.on("scalar", lambda e, c=c, g=g, par=par:
                              e.copy(xg_fm[par][:, c, g * 128:(g + 1) * 128],
                                     p_tx[c % 2][:]),
                              waits=[(s_pe, trp)], inc=s_act)
                        tr_cp[(gi, c)] = (s_act, s_act.c)
                        if c == KC - 1:
                            tr_cp[gi] = s_pe.c

                    def emit_meta(gi, g, e_i, par):
                        rpar, ldc = gi % 2, ld_cnt[gi % 2]
                        em.on("vector", lambda e, gi=gi, e_i=e_i:
                              e.tensor_scalar(mcol[:],
                                              xg_st[gi % 2][:, D:D + 8],
                                              float(e_i), None,
                                              ALU.is_equal),
                              waits=[(s_eg[rpar], ldc),
                                     (s_dve, wres.get(gi - 2, 0))],
                              inc=s_dve)
                        em.on("vector", lambda e, gi=gi: e.tensor_mul(
                                  mcol[:], mcol[:],
                                  xg_st[gi % 2][:, D + 8:D + 16]),
                              waits=[(s_dve, s_dve.c)], inc=s_dve)
                        em.on("vector", lambda e, g=g, par=par: e.reduce_sum(
                                  wcols[par][:, g:g + 1], mcol[:], axis=AX.X),
                              waits=[(s_dve, s_dve.c)], inc=s_dve)
                        em.on("vector", lambda e, g=g, e_i=e_i: e.tensor_scalar(
                                  cols[:, 11:12], c_g128p[:, g:g + 1],
                                  float(e_i * CAP), None, ALU.add),
                              waits=[(s_dve, s_dve.c),
                                     (s_dve, cnt_bc_done)], inc=s_dve)
                        em.on("vector", lambda e, e_i=e_i: e.tensor_scalar(
                                  cols[:, 12:13], cols[:, 11:12],
                                  cnt_bc[:, e_i:e_i + 1], None, ALU.is_lt),
                              waits=[(s_dve, s_dve.c)], inc=s_dve)
                        em.on("vector", lambda e, g=g, par=par: e.tensor_mul(
                                  wcols[par][:, g:g + 1],
                                  wcols[par][:, g:g + 1], cols[:, 12:13]),
                              waits=[(s_dve, s_dve.c)], inc=s_dve)
                        em.on("vector", lambda e, gi=gi: e.tensor_scalar(
                                  cols[:, 13:14],
                                  xg_st[gi % 2][:, D + 16:D + 17],
                                  256.0, None, ALU.mult),
                              waits=[(s_dve, s_dve.c)], inc=s_dve)
                        em.on("vector", lambda e, gi=gi: e.tensor_tensor(
                                  cols[:, 13:14], cols[:, 13:14],
                                  xg_st[gi % 2][:, D + 17:D + 18], ALU.add),
                              waits=[(s_dve, s_dve.c)], inc=s_dve)
                        em.on("vector", lambda e: e.tensor_scalar(
                                  cols[:, 13:14], cols[:, 13:14],
                                  float(T), None, ALU.min),
                              waits=[(s_dve, s_dve.c)], inc=s_dve)
                        em.on("vector", lambda e, g=g, par=par: e.tensor_copy(
                                  toku[par][:, g:g + 1], cols[:, 13:14]),
                              waits=[(s_dve, s_dve.c)], inc=s_dve)
                        wres[gi] = s_dve.c

                    def build_LT(e_i):
                        par2 = e_i % 2
                        items = [lambda e_i=e_i: emit_gload(e_i * NG + 0),
                                 lambda e_i=e_i: emit_gload(e_i * NG + 1)]
                        for g in range(NG):
                            gi = e_i * NG + g
                            for c in range(KC):
                                items.append(
                                    lambda gi=gi, g=g, c=c: emit_tx(gi, g, c,
                                                                    par2))
                            items.append(
                                lambda gi=gi, g=g, e_i=e_i: emit_meta(gi, g,
                                                                      e_i, par2))
                            if g + 2 < NG:
                                items.append(
                                    lambda gi2=e_i * NG + g + 2:
                                    emit_gload(gi2))

                        def fin(e_i=e_i):
                            lt_rdy[e_i] = s_act.c
                        items.append(fin)
                        return items

                    for th in build_LT(0):
                        th()
                    for e_i in range(NE):
                        pending = build_LT(e_i + 1) if e_i + 1 < NE else []
                        em.on("gpsimd", lambda e, e_i=e_i: e.dma_start(wg_bf[:],
                                                                       wg_r[e_i]),
                              waits=([(s_pe, pe_gate_end)]
                                     if pe_gate_end is not None else entry_dma),
                              inc=s_wg, inc_n=16)
                        wg_ld = s_wg.c
                        em.on("gpsimd", lambda e, e_i=e_i: e.dma_start(wu_bf[:],
                                                                       wu_r[e_i]),
                              waits=([(s_pe, pe_up_end)]
                                     if pe_up_end is not None else []),
                              inc=s_wu, inc_n=16)
                        wu_ld = s_wu.c
                        em.on("gpsimd", lambda e, e_i=e_i: e.dma_start(wd_bf[:],
                                                                       wd_r[e_i]),
                              waits=([(s_pe, pe_down_end)]
                                     if pe_down_end is not None else []),
                              inc=s_wd, inc_n=16)
                        wd_ld = s_wd.c
                        par = e_i % 2
                        xg_rdy_act = lt_rdy[e_i]
                        gidx0 = len(silu_cnt)
                        for c in range(IC):
                            gq = len(silu_cnt)
                            for k in range(KC):
                                w = []
                                if k == 0:
                                    w = [(s_wg, wg_ld), (s_act, xg_rdy_act)]
                                    if gq >= 2:
                                        w.append((s_act, silu_cnt[gq - 2]))
                                em.on("tensor", lambda e, k=k, c=c, gq=gq, par=par:
                                      e.matmul(pg[gq % 2][:],
                                               wg_bf[:, k, c * 128:(c + 1) * 128],
                                               xg_fm[par][:, k, :],
                                               start=(k == 0),
                                               stop=(k == KC - 1)),
                                      waits=w, inc=(s_pe if k == KC - 1 else None))
                            gate_pe = s_pe.c
                            em.on("scalar", lambda e, c=c, gq=gq: e.activation(
                                      g_s[:, c, :], pg[gq % 2][:], AF.Silu),
                                  waits=[(s_pe, gate_pe)], inc=s_act)
                            silu_cnt.append(s_act.c)
                        pe_gate_end = s_pe.c
                        uidx0 = len(upev_cnt)
                        for c in range(IC):
                            gq = len(upev_cnt)
                            for k in range(KC):
                                w = []
                                if k == 0:
                                    w = [(s_wu, wu_ld)]
                                    if gq >= 2:
                                        w.append((s_dve, upev_cnt[gq - 2]))
                                em.on("tensor", lambda e, k=k, c=c, gq=gq, par=par:
                                      e.matmul(pu[gq % 2][:],
                                               wu_bf[:, k, c * 128:(c + 1) * 128],
                                               xg_fm[par][:, k, :],
                                               start=(k == 0),
                                               stop=(k == KC - 1)),
                                      waits=w, inc=(s_pe if k == KC - 1 else None))
                            up_pe = s_pe.c
                            em.on("vector", lambda e, c=c, gq=gq: e.tensor_mul(
                                      h_bf[:, c, :], pu[gq % 2][:], g_s[:, c, :]),
                                  waits=[(s_pe, up_pe),
                                         (s_act, silu_cnt[gidx0 + c])],
                                  inc=s_dve)
                            upev_cnt.append(s_dve.c)
                        pe_up_end = s_pe.c
                        h_rdy = upev_cnt[-1]
                        for g in range(NG):
                            gi = e_i * NG + g
                            for q in range(4):
                                dq = len(dnev_cnt)
                                for k in range(IC):
                                    w = []
                                    if k == 0:
                                        w = [(s_wd, wd_ld), (s_dve, h_rdy)]
                                        if dq >= 2:
                                            w.append((s_dve, dnev_cnt[dq - 2]))
                                    em.on("tensor", lambda e, k=k, g=g, q=q, dq=dq:
                                          e.matmul(
                                              pd[dq % 2][:],
                                              h_bf[:, k, g * 128:(g + 1) * 128],
                                              wd_bf[:, k, q * 512:(q + 1) * 512],
                                              start=(k == 0),
                                              stop=(k == IC - 1)),
                                          waits=w,
                                          inc=(s_pe if k == IC - 1 else None))
                                    if pending:
                                        pending.pop(0)()
                                dn_pe = s_pe.c
                                w = [(s_pe, dn_pe), (s_dve, wres[gi])]
                                if gi >= 2:
                                    w.append((s_eo[gi % 2], eo_st.get(gi - 2, 0)))
                                em.on("vector", lambda e, g=g, q=q, dq=dq, par=par,
                                      gi=gi: e.tensor_scalar(
                                          out_st[gi % 2][:,
                                                         q * 512:(q + 1) * 512],
                                          pd[dq % 2][:],
                                          wcols[par][:, g:g + 1], None, ALU.mult),
                                      waits=w, inc=s_dve)
                                dnev_cnt.append(s_dve.c)
                            ost_done = s_dve.c
                            em.on("sync", lambda e, gi=gi: e.dma_start(
                                      eo_r[gi], out_st[gi % 2][:]),
                                  waits=[(s_dve, ost_done)], inc=s_eo[gi % 2],
                                  inc_n=16)
                            eo_st[gi] = s_eo[gi % 2].c
                        while pending:
                            pending.pop(0)()
                        pe_down_end = s_pe.c
                    eo_all = [(s_eo[0], s_eo[0].c), (s_eo[1], s_eo[1].c)]
                with contextlib.ExitStack() as rs:
                    xt2 = [rs.enter_context(nc.sbuf_tensor(f"sxr{i}_" + T_,
                                                           [128, D], F32))
                           for i in range(2)]
                    x2t = [rs.enter_context(nc.sbuf_tensor(f"sxw{i}_" + T_,
                                                           [128, D], F32))
                           for i in range(2)]
                    gb = [rs.enter_context(nc.sbuf_tensor(f"sgb{i}_" + T_,
                                                          [128, D], BF16))
                          for i in range(4)]
                    gs = [rs.enter_context(nc.sbuf_tensor(f"sg1{i}_" + T_,
                                                          [128, D], F32))
                          for i in range(2)]
                    add_done = {}
                    st_done = {}
                    gb_free = {0: 0, 1: 0, 2: 0, 3: 0}
                    gs_free = {0: 0, 1: 0}
                    for m in range(NT):
                        w = []
                        if m >= 2:
                            w.append((s_dve, add_done[m - 2]))
                        em.on("sync", lambda e, m=m: e.dma_start(xt2[m % 2][:],
                                                                 src_r[m]),
                              waits=w, inc=s_ld[m % 2], inc_n=16)
                        ldx = s_ld[m % 2].c
                        g_ld = {}

                        def emit_gather(k, extra):
                            w = (list(eo_all) + [(s_dve, gb_free[k % 4])]
                                 + list(extra))
                            em.on("gpsimd", lambda e, m=m, k=k:
                                  e.indirect_dma_start(
                                      gb[k % 4][:], None,
                                      d_eo,
                                      bass.IndirectOffsetOnAxis(
                                          slau[:, m, k:k + 1], 0)),
                                  waits=w, inc=s_gb[k % 4], inc_n=16)
                            g_ld[k] = s_gb[k % 4].c

                        for k in range(4):
                            emit_gather(k, [])
                        pair = {}
                        for j in range(2):
                            em.on("vector", lambda e, j=j: e.tensor_add(
                                      gs[j][:], gb[2 * j][:], gb[2 * j + 1][:]),
                                  waits=[(s_gb[2 * j], g_ld[2 * j]),
                                         (s_gb[2 * j + 1], g_ld[2 * j + 1]),
                                         (s_dve, gs_free[j])],
                                  inc=s_dve)
                            pair[j] = s_dve.c
                        for k in range(4, 8):
                            emit_gather(k, [(s_dve, pair[(k - 4) // 2])])
                        for j in range(2):
                            em.on("vector", lambda e, j=j: e.tensor_add(
                                      gs[j][:], gs[j][:], gb[2 * j][:]),
                                  waits=[(s_gb[2 * j], g_ld[4 + 2 * j]),
                                         (s_dve, pair[j])],
                                  inc=s_dve)
                            gb_free[2 * j] = s_dve.c
                            em.on("vector", lambda e, j=j: e.tensor_add(
                                      gs[j][:], gs[j][:], gb[2 * j + 1][:]),
                                  waits=[(s_gb[2 * j + 1], g_ld[5 + 2 * j]),
                                         (s_dve, s_dve.c)],
                                  inc=s_dve)
                            gb_free[2 * j + 1] = s_dve.c
                            pair[j] = s_dve.c
                        em.on("vector", lambda e: e.tensor_add(
                                  gs[0][:], gs[0][:], gs[1][:]),
                              waits=[(s_dve, pair[0]), (s_dve, pair[1])],
                              inc=s_dve)
                        gs_free[1] = s_dve.c
                        w = [(s_ld[m % 2], ldx), (s_dve, s_dve.c)]
                        if m >= 2:
                            w.append((s_st[m % 2], st_done[m - 2]))
                        em.on("vector", lambda e, m=m: e.tensor_add(
                                  x2t[m % 2][:], xt2[m % 2][:], gs[0][:]),
                              waits=w, inc=s_dve)
                        gs_free[0] = s_dve.c
                        add_done[m] = s_dve.c
                        em.on("sync", lambda e, m=m: e.dma_start(dst_r[m],
                                                                 x2t[m % 2][:]),
                              waits=[(s_dve, add_done[m])], inc=s_st[m % 2],
                              inc_n=16)
                        st_done[m] = s_st[m % 2].c
                    if final:
                        em.wait("sync", [(s_st[0], s_st[0].c),
                                         (s_st[1], s_st[1].c)])
                    em.flush()
        if stage == "ln":
            with contextlib.ExitStack() as ps:
                em = Em(nc)
                xn_fm = ps.enter_context(nc.sbuf_tensor("xn_fm_d", [128, KC, T], BF16))
                W_r = ps.enter_context(nc.sbuf_tensor("W_r_d", [128, NT, NE], F32))
                cols = ps.enter_context(nc.sbuf_tensor("cols_d", [128, 16], F32))
                lg = ps.enter_context(nc.sbuf_tensor("lg_d", [128, NE], F32))
                top8 = ps.enter_context(nc.sbuf_tensor("top8_d", [128, 8], F32))
                eall = ps.enter_context(nc.sbuf_tensor("eall_d", [128, NE], F32))
                maskb = ps.enter_context(nc.sbuf_tensor("maskb_d", [128, NE], F32))
                sqd = ps.enter_context(nc.sbuf_tensor("sqd_d", [128, D], F32))
                xt = [ps.enter_context(nc.sbuf_tensor(f"xt{i}_d", [128, D], F32))
                      for i in range(2)]
                xn_tm = ps.enter_context(nc.sbuf_tensor("xn_tm_d", [128, D], BF16))
                p_tr = [ps.enter_context(nc.psum_tensor(f"p_tr{i}_d", [128, 128], BF16))
                        for i in range(2)]
                p_rt = ps.enter_context(nc.psum_tensor("p_rt_d", [128, NE], F32))
                bufs = (xt, sqd[:], xn_tm, cols, lg, top8, eall, maskb, p_tr, p_rt)
                emit_ln(em, x_r, xn_fm, bufs, router_wr=W_r,
                        entry_dma=all_dma_totals(), dbg_dst_r=out_r)
                em.wait("sync", [(s_st[0], s_st[0].c), (s_st[1], s_st[1].c)])
                em.flush()
            return nc
        p1_dst = out_r if stage == "p1" else x2_r
        if SPARSE:
            ffn_pass_sparse(x_r, p1_dst, final=(stage == "p1"), tag="p1")
        else:
            ffn_pass(x_r, p1_dst, final=(stage == "p1"), tag="p1")
        if stage == "p1":
            return nc

        with contextlib.ExitStack() as p2:
            em = Em(nc)
            phase_guard(em)
            entry_dma = all_dma_totals()
            entry_sdve = s_dve.c
            o_tm = p2.enter_context(nc.sbuf_tensor("o_tm", [128, NT, D], BF16))
            u_fm = p2.enter_context(nc.sbuf_tensor("u_fm", [128, NH, T], BF16))
            with contextlib.ExitStack() as p2b:
                xn2 = p2b.enter_context(nc.sbuf_tensor("xn2", [128, KC, T], BF16))
                cols2 = p2b.enter_context(nc.sbuf_tensor("cols2", [128, 16], F32))
                if stage in ("p2o", "p2q", "p2kv"):
                    sq2 = p2b.enter_context(nc.sbuf_tensor("sq2", [128, D], F32))
                with contextlib.ExitStack() as lns:
                    if stage not in ("p2o", "p2q", "p2kv"):
                        sq2 = lns.enter_context(nc.sbuf_tensor("sq2", [128, D], F32))
                    xt = [lns.enter_context(nc.sbuf_tensor(f"xtb{i}", [128, D], F32))
                          for i in range(2)]
                    xn_tm = lns.enter_context(nc.sbuf_tensor("xn_tm2", [128, D], BF16))
                    with contextlib.ExitStack() as pss:
                        p_tr = [pss.enter_context(
                            nc.psum_tensor(f"p_trb{i}", [128, 128], BF16))
                            for i in range(2)]
                        bufs = (xt, sq2[:], xn_tm, cols2, None, None, None, None,
                                p_tr, None)
                        ln2_act, ln2_dve = emit_ln(em, x2_r, xn2, bufs, router_wr=None,
                                                   entry_dma=entry_dma,
                                                   entry_sdve=entry_sdve)
                c_arr = p2b.enter_context(nc.sbuf_tensor("c_arr", [128, BLK], F32))
                c_pos = p2b.enter_context(nc.sbuf_tensor("c_pos", [128, T], F32))
                c_idxT = p2b.enter_context(nc.sbuf_tensor("c_idxT", [128, 2, BLK], F32))
                c_krev = p2b.enter_context(nc.sbuf_tensor("c_krev", [128, 2], F32))
                kv_sb = p2b.enter_context(nc.sbuf_tensor("kv_sb", [128, NH, HD], F32))
                kv_bf = p2b.enter_context(nc.sbuf_tensor("kv_bf", [128, HD], BF16))
                q_full = p2b.enter_context(nc.sbuf_tensor("q_full", [128, NH, T], BF16))
                k_buf = [p2b.enter_context(nc.sbuf_tensor(f"k_buf{i}", [128, T], BF16))
                         for i in range(2)]
                v_buf = [p2b.enter_context(nc.sbuf_tensor(f"v_buf{i}", [128, NT, HD],
                                                          BF16)) for i in range(2)]
                pbuf = [p2b.enter_context(nc.sbuf_tensor(f"pbuf{i}", [128, KC, 128],
                                                         BF16)) for i in range(4)]
                qd4 = p2b.enter_context(nc.sbuf_tensor("qd4", [128, 4, BLK], F32))
                kd4 = p2b.enter_context(nc.sbuf_tensor("kd4", [128, 4, 2], F32))
                dg4 = p2b.enter_context(nc.sbuf_tensor("dg4", [128, 4, 2, BLK], F32))
                bd4 = p2b.enter_context(nc.sbuf_tensor("bd4", [128, 4], F32))
                qd_scr = p2b.enter_context(nc.sbuf_tensor("qd_scr", [128, BLK], BF16))
                kqd = p2b.enter_context(nc.sbuf_tensor("kqd", [128, 2, BLK], BF16))
                kdT = p2b.enter_context(nc.sbuf_tensor("kdT", [128, 2, HD], BF16))
                vf = [p2b.enter_context(nc.sbuf_tensor(f"vf{i}", [128, T], BF16))
                      for i in range(2)]
                pa = [p2b.enter_context(nc.psum_tensor(f"pa{i}", [128, 512], F32))
                      for i in range(2)]
                pkq = [p2b.enter_context(nc.psum_tensor(f"pkq{i}", [128, BLK], F32))
                       for i in range(2)]
                po = [p2b.enter_context(nc.psum_tensor(f"po{i}", [128, HD], F32))
                      for i in range(2)]
                pkv = p2b.enter_context(nc.psum_tensor("pkv", [128, HD], F32))
                pkt = p2b.enter_context(nc.psum_tensor("pkt", [128, HD], BF16))

                em.on("sync", lambda e: e.dma_start(c_arr[:], d_arr),
                      waits=all_dma_totals() + [(s_act, ln2_act),
                                                (s_dve, ln2_dve)],
                      inc=s_x, inc_n=16)
                em.on("sync", lambda e: e.dma_start(c_pos[:], d_pos), inc=s_x, inc_n=16)
                em.on("sync", lambda e: e.dma_start(c_idxT[:], d_idxT),
                      inc=s_x, inc_n=16)
                em.on("sync", lambda e: e.dma_start(c_krev[:], d_krev),
                      inc=s_x, inc_n=16)
                consts_ld = s_x.c

                proj_evac = {}
                pe_piece_done = {}
                piece_ld_h = {}
                proj_act_h = {}
                decays_done_h = {}
                kvbf_done_h = {}
                pkq_guard = {0: 0, 1: 0}
                po_guard = {0: 0, 1: 0}
                pkv_guard = 0
                pkt_guard = 0
                kv_last_dve = 0
                prev_o_pe = 0
                first_piece = [True]

                def emit_decays(grp, w0):
                    for hh in range(4):
                        h = grp * 4 + hh
                        em.on("scalar", lambda e, hh=hh, h=h: e.activation(
                                  qd4[:, hh, :], c_arr[:], AF.Exp, scale=NS(h)),
                              waits=(w0 if hh == 0 else []), inc=s_act)
                        em.on("scalar", lambda e, hh=hh, h=h: e.activation(
                                  kd4[:, hh, :], c_krev[:], AF.Exp, scale=NS(h)),
                              inc=s_act)
                        em.on("scalar", lambda e, hh=hh, h=h: e.activation(
                                  dg4[:, hh, :, :], c_idxT[:], AF.Exp, scale=NS(h)),
                              inc=s_act)
                        em.on("scalar", lambda e, hh=hh, h=h: e.activation(
                                  bd4[:, hh:hh + 1], c_c256[:], AF.Exp, scale=NS(h)),
                              inc=s_act)
                    dd = s_act.c
                    for hh in range(4):
                        decays_done_h[grp * 4 + hh] = dd

                def emit_pieces(h):
                    piece_cols = [h * 128, D + h * 128, 2 * D + h * 128,
                                  3 * D + h * 128]
                    lds = []
                    for t_i in range(4):
                        pi = 4 * h + t_i
                        if first_piece[0]:
                            w = all_dma_totals() + [(s_act, ln2_act),
                                                    (s_dve, ln2_dve)]
                            first_piece[0] = False
                        elif pi >= 4:
                            w = [(s_pe, pe_piece_done[pi - 4])]
                        else:
                            w = []
                        c0 = piece_cols[t_i]
                        em.on("gpsimd", lambda e, pi=pi, c0=c0: e.dma_start(
                                  pbuf[pi % 4][:], wq_r[:, :, c0:c0 + 128]),
                              waits=w, inc=s_pb[pi % 4], inc_n=16)
                        lds.append(s_pb[pi % 4].c)
                    piece_ld_h[h] = lds

                def emit_proj_part(h, part):
                    piece_ld = piece_ld_h[h]
                    if part in (0, 1):
                        t_i = part
                        buf = pbuf[(4 * h + t_i) % 4]
                        psem = s_pb[(4 * h + t_i) % 4]
                        for th in range(2):
                            for k in range(KC):
                                w = []
                                if k == 0:
                                    w = [(psem, piece_ld[t_i]),
                                         (s_act, proj_evac.get(th, 0))]
                                    if h == 0 and t_i == 0 and th == 0:
                                        w += [(s_act, ln2_act), (s_dve, ln2_dve)]
                                em.on("tensor", lambda e, k=k, th=th, buf=buf:
                                      e.matmul(pa[th][:], buf[:, k, :],
                                               xn2[:, k, th * 512:(th + 1) * 512],
                                               start=(k == 0), stop=(k == KC - 1)),
                                      waits=w, inc=(s_pe if k == KC - 1 else None))
                            mm_done = s_pe.c
                            dest = q_full[:, h, th * 512:(th + 1) * 512] \
                                if t_i == 0 \
                                else k_buf[h % 2][:, th * 512:(th + 1) * 512]
                            em.on("scalar", lambda e, th=th, dest=dest: e.activation(
                                      dest, pa[th][:], AF.Silu),
                                  waits=[(s_pe, mm_done)], inc=s_act)
                            proj_evac[th] = s_act.c
                        pe_piece_done[4 * h + t_i] = s_pe.c
                    elif part == 2:
                        bufv = pbuf[(4 * h + 2) % 4]
                        psemv = s_pb[(4 * h + 2) % 4]
                        for th in range(2):
                            for k in range(KC):
                                w = []
                                if k == 0:
                                    w = [(psemv, piece_ld[2]),
                                         (s_act, proj_evac.get(th, 0))]
                                em.on("tensor", lambda e, k=k, th=th, bufv=bufv:
                                      e.matmul(pa[th][:], bufv[:, k, :],
                                               xn2[:, k, th * 512:(th + 1) * 512],
                                               start=(k == 0), stop=(k == KC - 1)),
                                      waits=w, inc=(s_pe if k == KC - 1 else None))
                            mm_done = s_pe.c
                            em.on("scalar", lambda e, th=th, h=h: e.activation(
                                      vf[h % 2][:, th * 512:(th + 1) * 512],
                                      pa[th][:], AF.Copy),
                                  waits=[(s_pe, mm_done)], inc=s_act)
                            proj_evac[th] = s_act.c
                        vf_act = s_act.c
                        for mt in range(NT):
                            em.on("tensor", lambda e, mt=mt, h=h: e.transpose(
                                      pkt[:],
                                      vf[h % 2][:, mt * 128:(mt + 1) * 128],
                                      c_ident[:]),
                                  waits=[(s_dve, state["pkt_guard"]),
                                         (s_act, vf_act)], inc=s_pe)
                            kt_pe = s_pe.c
                            em.on("vector", lambda e, mt=mt, h=h: e.tensor_copy(
                                      v_buf[h % 2][:, mt, :], pkt[:]),
                                  waits=[(s_pe, kt_pe)], inc=s_dve)
                            state["pkt_guard"] = s_dve.c
                        pe_piece_done[4 * h + 2] = s_pe.c
                    else:
                        bufu = pbuf[(4 * h + 3) % 4]
                        psemu = s_pb[(4 * h + 3) % 4]
                        for th in range(2):
                            for k in range(KC):
                                w = []
                                if k == 0:
                                    w = [(psemu, piece_ld[3]),
                                         (s_act, proj_evac.get(th, 0))]
                                em.on("tensor", lambda e, k=k, th=th, bufu=bufu:
                                      e.matmul(pa[th][:], bufu[:, k, :],
                                               xn2[:, k, th * 512:(th + 1) * 512],
                                               start=(k == 0), stop=(k == KC - 1)),
                                      waits=w, inc=(s_pe if k == KC - 1 else None))
                            mm_done = s_pe.c
                            em.on("scalar", lambda e, th=th, h=h: e.activation(
                                      u_fm[:, h, th * 512:(th + 1) * 512], pa[th][:],
                                      AF.Copy),
                                  waits=[(s_pe, mm_done)], inc=s_act)
                            proj_evac[th] = s_act.c
                        pe_piece_done[4 * h + 3] = s_pe.c
                        proj_act_h[h] = s_act.c

                def emit_scan_block(h, i):
                    nonlocal_state = None
                    hh = h % 4
                    decays_done = decays_done_h[h]
                    proj_act = proj_act_h[h]
                    kvbf_done = kvbf_done_h.get(h, 0)
                    global_vals = {}
                    em.on("vector", lambda e, h=h, hh=hh, i=i: e.tensor_mul(
                              qd_scr[:], q_full[:, h, i * BLK:(i + 1) * BLK],
                              qd4[:, hh, :]),
                          waits=[(s_act, max(decays_done, proj_act)),
                                 (s_dve, s_dve.c), (s_pe, state["prev_o_pe"])],
                          inc=s_dve)
                    qd_done = s_dve.c
                    for j in range(2):
                        w = [(s_dve, pkq_guard[j])]
                        if i == 0 and j == 0:
                            w.append((s_act, proj_act))
                        em.on("tensor", lambda e, h=h, i=i, j=j: e.matmul(
                                  pkq[j][:],
                                  k_buf[h % 2][:, i * BLK + j * 128:
                                                i * BLK + (j + 1) * 128],
                                  q_full[:, h, i * BLK:(i + 1) * BLK],
                                  start=True, stop=True),
                              waits=w, inc=s_pe)
                        kq_pe = s_pe.c
                        em.on("vector", lambda e, hh=hh, j=j: e.tensor_mul(
                                  kqd[:, j, :], pkq[j][:], dg4[:, hh, j, :]),
                              waits=[(s_pe, kq_pe)], inc=s_dve)
                        pkq_guard[j] = s_dve.c
                    kqd_done = s_dve.c
                    for j2 in range(2):
                        first = True
                        if i > 0:
                            em.on("tensor", lambda e, j2=j2: e.matmul(
                                      po[j2][:],
                                      qd_scr[:, j2 * 128:(j2 + 1) * 128],
                                      kv_bf[:], start=True, stop=False),
                                  waits=[(s_dve, qd_done), (s_act, kvbf_done),
                                         (s_dve, po_guard[j2])])
                            first = False
                        for js in range(2):
                            w = []
                            if js == 0:
                                w = [(s_dve, kqd_done), (s_dve, qd_done)]
                                if first:
                                    w.append((s_dve, po_guard[j2]))
                            em.on("tensor", lambda e, i=i, j2=j2, js=js, h=h,
                                  first=first: e.matmul(
                                      po[j2][:],
                                      kqd[:, js, j2 * 128:(j2 + 1) * 128],
                                      v_buf[h % 2][:, 2 * i + js, :],
                                      start=(first and js == 0),
                                      stop=(js == 1)),
                                  waits=w, inc=(s_pe if js == 1 else None))
                            first = False
                    o_pe = s_pe.c
                    state["prev_o_pe"] = o_pe
                    for j2 in range(2):
                        em.on("vector", lambda e, i=i, j2=j2, h=h: e.tensor_copy(
                                  o_tm[:, 2 * i + j2, h * 128:(h + 1) * 128],
                                  po[j2][:]),
                              waits=[(s_pe, o_pe)], inc=s_dve)
                        po_guard[j2] = s_dve.c
                    for j in range(2):
                        em.on("tensor", lambda e, h=h, i=i, j=j: e.transpose(
                                  pkt[:],
                                  k_buf[h % 2][:, i * BLK + j * 128:
                                                i * BLK + (j + 1) * 128],
                                  c_ident[:]),
                              waits=[(s_dve, state["pkt_guard"])], inc=s_pe)
                        kt_pe = s_pe.c
                        em.on("vector", lambda e, hh=hh, j=j: e.tensor_scalar(
                                  kdT[:, j, :], pkt[:], kd4[:, hh, j:j + 1],
                                  None, ALU.mult),
                              waits=[(s_pe, kt_pe), (s_act, decays_done)],
                              inc=s_dve)
                        state["pkt_guard"] = s_dve.c
                        kdt_done = s_dve.c
                        em.on("tensor", lambda e, j=j, h=h, i=i: e.matmul(
                                  pkv[:], kdT[:, j, :],
                                  v_buf[h % 2][:, 2 * i + j, :],
                                  start=(j == 0), stop=(j == 1)),
                              waits=[(s_dve, kdt_done)] +
                                    ([(s_dve, state["pkv_guard"])] if j == 0 else []),
                              inc=(s_pe if j == 1 else None))
                    kv_pe = s_pe.c
                    if i == 0:
                        em.on("vector", lambda e, h=h: e.tensor_copy(
                                  kv_sb[:, h, :], pkv[:]),
                              waits=[(s_pe, kv_pe), (s_dve, state["kv_last_dve"])],
                              inc=s_dve)
                    else:
                        em.on("vector", lambda e, h=h, hh=hh:
                              e.scalar_tensor_tensor(
                                  kv_sb[:, h, :], kv_sb[:, h, :],
                                  bd4[:, hh:hh + 1], pkv[:],
                                  ALU.mult, ALU.add),
                              waits=[(s_pe, kv_pe), (s_act, decays_done),
                                     (s_dve, state["kv_last_dve"])],
                              inc=s_dve)
                    state["pkv_guard"] = s_dve.c
                    state["kv_last_dve"] = s_dve.c
                    if i < NB - 1:
                        em.on("scalar", lambda e, h=h: e.activation(
                                  kv_bf[:], kv_sb[:, h, :], AF.Copy),
                              waits=[(s_dve, state["kv_last_dve"]), (s_pe, o_pe)],
                              inc=s_act)
                        kvbf_done_h[h] = s_act.c

                state = {"prev_o_pe": 0, "pkt_guard": 0, "pkv_guard": 0,
                         "kv_last_dve": 0}
                emit_decays(0, [(s_x, consts_ld)])
                emit_pieces(0)
                for part in range(4):
                    emit_proj_part(0, part)
                for h in range(NH):
                    for i in range(NB):
                        emit_scan_block(h, i)
                        nxt = h + 1
                        if nxt < NH:
                            if i == 0:
                                emit_pieces(nxt)
                            emit_proj_part(nxt, i)
                    if (h + 1) % 4 == 0 and h + 1 < NH:
                        emit_decays((h + 1) // 4, [(s_dve, s_dve.c)])
                kv_last_dve = state["kv_last_dve"]
                po_guard = dict(po_guard)

                if stage == "p2kv":
                    scan_dve, scan_act = s_dve.c, s_act.c
                    em.on("scalar", lambda e: e.copy(sq2[:], kv_sb.ap().rearrange(
                              "p h e -> p (h e)")),
                          waits=[(s_dve, scan_dve), (s_act, scan_act)], inc=s_act)
                    aa = s_act.c
                    em.on("sync", lambda e: e.dma_start(out_r[0], sq2[:]),
                          waits=[(s_act, aa)], inc=s_st[0], inc_n=16)
                    st0 = s_st[0].c
                    em.on("scalar", lambda e: e.copy(sq2[:, 0:HD], kv_bf[:]),
                          waits=[(s_st[0], st0)], inc=s_act)
                    aa = s_act.c
                    em.on("sync", lambda e: e.dma_start(out_r[1], sq2[:]),
                          waits=[(s_act, aa)], inc=s_st[0], inc_n=16)
                    em.wait("sync", [(s_st[0], s_st[0].c)])
                    em.flush()
                    return nc
                if stage == "p2s":
                    scan_dve, scan_act = s_dve.c, s_act.c
                    dst_done = {}
                    for m in range(NT):
                        em.on("scalar", lambda e, m=m: e.copy(sq2[:], o_tm[:, m, :]),
                              waits=[(s_dve, scan_dve), (s_act, scan_act),
                                     (s_st[0], dst_done.get(m - 1, 0))],
                              inc=s_act)
                        aa = s_act.c
                        em.on("sync", lambda e, m=m: e.dma_start(out_r[m], sq2[:]),
                              waits=[(s_act, aa)], inc=s_st[0], inc_n=16)
                        dst_done[m] = s_st[0].c
                    em.wait("sync", [(s_st[0], s_st[0].c)])
                    em.flush()
                    return nc
                skip_cc = (stage == "p2x")
                if not skip_cc:
                    pass
                em.on("sync", lambda e: e.dma_start(ccin_r, kv_sb[:]),
                      waits=[(s_dve, s_dve.c), (s_act, s_act.c)], inc=s_x, inc_n=16)
                if not skip_cc:
                    em.on("gpsimd", lambda e: e.collective_compute(
                              "AllGather", ALU.bypass,
                              replica_groups=[list(range(NCORES))],
                              ins=[d_ccin.opt()], outs=[d_ccout.opt()]),
                          waits=[(s_x, s_x.c)], inc=s_cc, inc_n=1)
                else:
                    em.on("gpsimd", lambda e: e.dma_start(
                              d_ccout.rearrange("(j n) e -> j n e", j=NCORES)[0],
                              d_ccin),
                          waits=[(s_x, s_x.c)], inc=s_cc, inc_n=16)

                gath = [p2b.enter_context(nc.sbuf_tensor(f"gath{i}",
                                                         [128, NCORES, HD], F32))
                        for i in range(2)]
                cf = p2b.enter_context(nc.sbuf_tensor("cf", [128, NCORES], F32))
                kvst = p2b.enter_context(nc.sbuf_tensor("kvst", [128, HD], F32))
                kvst_bf = p2b.enter_context(nc.sbuf_tensor("kvst_bf", [128, HD], BF16))
                posd = p2b.enter_context(nc.sbuf_tensor("posd", [128, T], F32))
                qpd = p2b.enter_context(nc.sbuf_tensor("qpd", [128, T], BF16))
                kvst_use = {}
                for h in (range(NH) if not skip_cc else []):
                    w = [(s_cc, 1)]
                    if h >= 2:
                        w.append((s_dve, kvst_use[h - 2]))
                    em.on("sync", lambda e, h=h: e.dma_start(gath[h % 2][:],
                                                             ccout_r[:, :, h, :]),
                          waits=w, inc=s_g[h % 2], inc_n=16)
                    g_ld = s_g[h % 2].c
                    em.on("scalar", lambda e, h=h: e.activation(cf[:], c_delta[:],
                                                                AF.Exp, scale=NS(h)),
                          waits=[(s_dve, s_dve.c)], inc=s_act)
                    cf_act = s_act.c
                    em.on("vector", lambda e: e.tensor_mul(cf[:], cf[:], c_maskg[:]),
                          waits=[(s_act, cf_act)], inc=s_dve)
                    cf_done = s_dve.c
                    em.on("vector", lambda e, h=h: e.tensor_scalar(
                              kvst[:], gath[h % 2][:, 0, :], cf[:, 0:1],
                              None, ALU.mult),
                          waits=[(s_g[h % 2], g_ld), (s_dve, cf_done)], inc=s_dve)
                    for j in range(1, NCORES):
                        em.on("vector", lambda e, j=j, h=h: e.scalar_tensor_tensor(
                                  kvst[:], gath[h % 2][:, j, :], cf[:, j:j + 1],
                                  kvst[:], ALU.mult, ALU.add),
                              waits=[(s_dve, s_dve.c)], inc=s_dve)
                    kvst_done = s_dve.c
                    em.on("scalar", lambda e: e.activation(kvst_bf[:], kvst[:],
                                                           AF.Copy),
                          waits=[(s_dve, kvst_done)], inc=s_act)
                    kvb_act = s_act.c
                    em.on("scalar", lambda e, h=h: e.activation(posd[:], c_pos[:],
                                                                AF.Exp, scale=NS(h)),
                          waits=[(s_dve, s_dve.c)], inc=s_act)
                    pd_act = s_act.c
                    em.on("vector", lambda e, h=h: e.tensor_mul(qpd[:],
                                                                q_full[:, h, :],
                                                                posd[:]),
                          waits=[(s_act, pd_act)], inc=s_dve)
                    qpd_done = s_dve.c
                    oadd = {}
                    for m in range(NT):
                        w = [(s_dve, qpd_done), (s_act, kvb_act),
                             (s_dve, po_guard[m % 2])]
                        if m >= 2:
                            w.append((s_dve, oadd[m - 2]))
                        em.on("tensor", lambda e, m=m: e.matmul(
                                  po[m % 2][:], qpd[:, m * 128:(m + 1) * 128],
                                  kvst_bf[:], start=True, stop=True),
                              waits=w, inc=s_pe)
                        c_pe = s_pe.c
                        em.on("vector", lambda e, m=m, h=h: e.tensor_add(
                                  o_tm[:, m, h * 128:(h + 1) * 128],
                                  o_tm[:, m, h * 128:(h + 1) * 128], po[m % 2][:]),
                              waits=[(s_pe, c_pe)], inc=s_dve)
                        oadd[m] = s_dve.c
                        po_guard[m % 2] = s_dve.c
                    kvst_use[h] = s_dve.c
                p2b_dve_end = s_dve.c
                if stage in ("p2o", "p2q"):
                    dcp = sq2
                    dst_done = {}
                    if stage == "p2o":
                        for m in range(NT):
                            em.on("scalar", lambda e, m=m: e.copy(dcp[:],
                                                                  o_tm[:, m, :]),
                                  waits=[(s_dve, p2b_dve_end),
                                         (s_st[0], dst_done.get(m - 1, 0))],
                                  inc=s_act)
                            aa = s_act.c
                            em.on("sync", lambda e, m=m: e.dma_start(out_r[m], dcp[:]),
                                  waits=[(s_act, aa)], inc=s_st[0], inc_n=16)
                            dst_done[m] = s_st[0].c
                    else:
                        out_flat = d_out.rearrange("(a p) z -> p a z", p=128)
                        qf = q_full.ap().rearrange("p h t -> p (h t)")
                        for a in range(8):
                            em.on("scalar", lambda e, a=a: e.copy(
                                      dcp[:], qf[:, a * D:(a + 1) * D]),
                                  waits=[(s_dve, p2b_dve_end),
                                         (s_st[0], dst_done.get(a - 1, 0))],
                                  inc=s_act)
                            aa = s_act.c
                            em.on("sync", lambda e, a=a: e.dma_start(out_flat[:, a, :],
                                                                     dcp[:]),
                                  waits=[(s_act, aa)], inc=s_st[0], inc_n=16)
                            dst_done[a] = s_st[0].c
                    em.wait("sync", [(s_st[0], s_st[0].c)])
                    em.flush()
                    return nc
                em.flush()

            em = Em(nc)
            phase_guard(em)
            p2d_entry = all_dma_totals()
            with contextlib.ExitStack() as p2d:
                uo_fm = p2d.enter_context(nc.sbuf_tensor("uo_fm", [128, KC, T], BF16))
                wout_bf = p2d.enter_context(nc.sbuf_tensor("wout_bf", [128, KC, D],
                                                           BF16))
                sq3 = p2d.enter_context(nc.sbuf_tensor("sq3", [128, D], F32))
                cols3 = p2d.enter_context(nc.sbuf_tensor("cols3", [128, 4], F32))
                xt3 = [p2d.enter_context(nc.sbuf_tensor("xt3s", [128, D], F32))] * 2
                x3t = [p2d.enter_context(nc.sbuf_tensor("x3ts", [128, D], F32))] * 2
                p_tr = [p2d.enter_context(nc.psum_tensor(f"p_trd{i}", [128, 128],
                                                         BF16))
                        for i in range(2)]
                pw = [p2d.enter_context(nc.psum_tensor(f"pw{i}", [128, 512], F32))
                      for i in range(2)]
                C3 = lambda i: cols3[:, i:i + 1]
                em.on("gpsimd", lambda e: e.dma_start(wout_bf[:], wo_r),
                      waits=p2d_entry, inc=s_w, inc_n=16)
                wout_ld = s_w.c
                uo_done = {}
                on_act = {}
                rinv_prev = 0
                for m in range(NT):
                    w = [(s_act, on_act[m - 1])] if m >= 1 else []
                    em.on("scalar", lambda e, m=m: e.activation(sq3[:], o_tm[:, m, :],
                                                                AF.Square,
                                                                accum_out=C3(0)),
                          waits=w + [(s_dve, max(rinv_prev, p2b_dve_end))], inc=s_act)
                    ssq_done = s_act.c
                    em.on("vector", lambda e: e.tensor_scalar(C3(3), C3(0), 1.0 / D,
                                                              EPS_RMS,
                                                              ALU.mult, ALU.add),
                          waits=[(s_act, ssq_done), (s_dve, s_dve.c)], inc=s_dve)
                    ms_done = s_dve.c
                    em.on("scalar", lambda e: e.activation(C3(1), C3(3), AF.Sqrt),
                          waits=[(s_dve, ms_done)], inc=s_act)
                    sd_done = s_act.c
                    em.on("vector", lambda e: e.reciprocal(C3(2), C3(1)),
                          waits=[(s_act, sd_done)], inc=s_dve)
                    rinv_done = s_dve.c
                    rinv_prev = rinv_done
                    em.on("scalar", lambda e, m=m: e.activation(o_tm[:, m, :],
                                                                o_tm[:, m, :],
                                                                AF.Identity,
                                                                scale=C3(2)),
                          waits=[(s_dve, rinv_done)], inc=s_act)
                    on_act[m] = s_act.c
                    cp_done = {}
                    for c in range(KC):
                        w = [(s_act, on_act[m])] if c < 2 else [(s_dve, cp_done[c - 2])]
                        em.on("tensor", lambda e, c=c, m=m: e.transpose(
                                  p_tr[c % 2][:], o_tm[:, m, c * 128:(c + 1) * 128],
                                  c_ident[:]),
                              waits=w, inc=s_pe)
                        tr = s_pe.c
                        em.on("vector", lambda e, c=c, m=m: e.tensor_mul(
                                  uo_fm[:, c, m * 128:(m + 1) * 128], p_tr[c % 2][:],
                                  u_fm[:, c, m * 128:(m + 1) * 128]),
                              waits=[(s_pe, tr)], inc=s_dve)
                        cp_done[c] = s_dve.c
                    uo_done[m] = s_dve.c
                pwg_done = {}
                tile_add = {}
                st_done = {}
                dst = out_r if stage in ("p2", "p2x") else x3_r
                for m in range(NT):
                    w = []
                    if m >= 1:
                        w.append((s_dve, tile_add[m - 1]))
                    em.on("sync", lambda e, m=m: e.dma_start(xt3[m % 2][:], x2_r[m]),
                          waits=w, inc=s_ld[m % 2], inc_n=16)
                    ld = s_ld[m % 2].c
                    for q in range(4):
                        gq = m * 4 + q
                        for k in range(KC):
                            w = []
                            if k == 0:
                                w = [(s_dve, uo_done[m]), (s_w, wout_ld)]
                                if gq >= 2:
                                    w.append((s_dve, pwg_done[gq - 2]))
                            em.on("tensor", lambda e, k=k, m=m, q=q: e.matmul(
                                      pw[q % 2][:], uo_fm[:, k, m * 128:(m + 1) * 128],
                                      wout_bf[:, k, q * 512:(q + 1) * 512],
                                      start=(k == 0), stop=(k == KC - 1)),
                                  waits=w, inc=(s_pe if k == KC - 1 else None))
                        w_pe = s_pe.c
                        ww = [(s_pe, w_pe), (s_ld[m % 2], ld)]
                        if m >= 1:
                            ww.append((s_st[(m - 1) % 2], st_done[m - 1]))
                        em.on("vector", lambda e, m=m, q=q: e.tensor_add(
                                  x3t[m % 2][:, q * 512:(q + 1) * 512], pw[q % 2][:],
                                  xt3[m % 2][:, q * 512:(q + 1) * 512]),
                              waits=ww, inc=s_dve)
                        pwg_done[gq] = s_dve.c
                    tile_add[m] = s_dve.c
                    em.on("sync", lambda e, m=m, dst=dst: e.dma_start(dst[m],
                                                                     x3t[m % 2][:]),
                          waits=[(s_dve, tile_add[m])], inc=s_st[m % 2], inc_n=16)
                    st_done[m] = s_st[m % 2].c
                if stage in ("p2", "p2x"):
                    em.wait("sync", [(s_st[0], s_st[0].c), (s_st[1], s_st[1].c)])
                em.flush()
        if stage in ("p2", "p2x"):
            return nc

        if SPARSE:
            ffn_pass_sparse(x3_r, out_r, final=True, tag="p3")
        else:
            ffn_pass(x3_r, out_r, final=True, tag="p3")
    return nc


_NC_CACHE = {}


def _get_nc(stage):
    if stage not in _NC_CACHE:
        _NC_CACHE[stage] = build_nc(stage)
    return _NC_CACHE[stage]


def _host_constants():
    import ml_dtypes
    arr = np.arange(1, BLK + 1, dtype=np.float32)
    arr_b = np.tile(arr[None, :], (128, 1))
    pos = np.arange(1, T + 1, dtype=np.float32)
    pos_b = np.tile(pos[None, :], (128, 1))
    s_idx = (np.arange(2)[:, None] * 128 + np.arange(128)[None, :])
    m_idx = np.arange(BLK)
    diff = m_idx[None, None, :] - s_idx[:, :, None]
    idxT = np.where(diff >= 0, diff, 1.0e6).astype(np.float32)
    idxT_b = np.ascontiguousarray(idxT.transpose(1, 0, 2))
    krev = np.ascontiguousarray((BLK - (s_idx + 1)).astype(np.float32).T)
    c256 = np.full((128, 1), float(BLK), dtype=np.float32)
    ident = np.eye(128).astype(ml_dtypes.bfloat16)
    return arr_b, pos_b, idxT_b, krev, c256, ident


def _host_sparse_consts():
    ltri = np.tril(np.ones((128, 128), np.float32), k=-1)
    erange_b = np.tile(np.arange(NE, dtype=np.float32)[None, :], (128, 1))
    p = np.arange(128, dtype=np.float32)
    g128p_b = np.stack([g * 128 + p for g in range(NG)], axis=1).astype(np.float32)
    tok = (np.arange(NT)[None, :] * 128 + np.arange(128)[:, None]).astype(np.int64)
    tokhi_b = (tok // 256).astype(np.float32)
    toklo_b = (tok % 256).astype(np.float32)
    ones2_b = np.zeros((128, 2), np.float32)
    ones2_b[:, 0] = 1.0
    onesrow_b = np.ones((1, 128), np.float32)
    ecap_b = (np.arange(NE, dtype=np.float32) * CAP).reshape(1, NE)
    return {"ltri": np.ascontiguousarray(ltri),
            "erange_b": np.ascontiguousarray(erange_b),
            "g128p_b": np.ascontiguousarray(g128p_b),
            "tokhi_b": np.ascontiguousarray(tokhi_b),
            "toklo_b": np.ascontiguousarray(toklo_b),
            "ones2_b": ones2_b, "onesrow_b": onesrow_b, "ecap_b": ecap_b}


def kernel(L0_attn_x, L0_attn_slope_rate, L0_attn_W_qkvu, L0_attn_W_out,
           L0_ffn_router_w, L0_ffn_w_gate, L0_ffn_w_up, L0_ffn_w_down,
           _stage="full"):
    x = np.ascontiguousarray(
        np.asarray(L0_attn_x, dtype=np.float32).reshape(2 * 4096, D))
    slope = np.asarray(L0_attn_slope_rate, dtype=np.float32).reshape(NH)
    wqkvuT = np.ascontiguousarray(np.asarray(L0_attn_W_qkvu, dtype=np.float32).T)
    woutT = np.ascontiguousarray(np.asarray(L0_attn_W_out, dtype=np.float32).T)
    rT = np.ascontiguousarray(np.asarray(L0_ffn_router_w, dtype=np.float32).T)
    wg = np.ascontiguousarray(np.asarray(L0_ffn_w_gate, dtype=np.float32))
    wu = np.ascontiguousarray(np.asarray(L0_ffn_w_up, dtype=np.float32))
    wd = np.ascontiguousarray(np.asarray(L0_ffn_w_down, dtype=np.float32))

    arr_b, pos_b, idxT_b, krev, c256, ident = _host_constants()
    slope_bc = np.ascontiguousarray(np.tile(slope[None, :], (128, 1)))

    in_maps = []
    for c in range(NCORES):
        base = (c // 4) * 4
        delta = np.zeros(NCORES, dtype=np.float32)
        maskg = np.zeros(NCORES, dtype=np.float32)
        for j in range(NCORES):
            if base <= j < c:
                maskg[j] = 1.0
                delta[j] = 1024.0 * (c - 1 - j)
        in_maps.append({
            "x_in": np.ascontiguousarray(x[c * T:(c + 1) * T]),
            "wqkvuT": wqkvuT,
            "woutT": woutT,
            "rT": rT,
            "w_gate": wg,
            "w_up": wu,
            "w_down": wd,
            "slope_bc": slope_bc,
            "arr_b": arr_b,
            "pos_b": pos_b,
            "idxT_b": idxT_b,
            "krev_b": krev,
            "c256_b": c256,
            "delta_b": np.ascontiguousarray(np.tile(delta[None, :], (128, 1))),
            "maskg_b": np.ascontiguousarray(np.tile(maskg[None, :], (128, 1))),
            "ident": ident,
            **_host_sparse_consts(),
        })

    nc = _get_nc(_stage)
    res = run_bass_kernel_spmd(nc, in_maps, core_ids=list(range(NCORES)))
    out = np.concatenate([np.asarray(res.results[c]["out"])
                          for c in range(NCORES)], axis=0)
    return out.reshape(2, 4096, D).astype(np.float32)


if __name__ == "__main__":
    import sys as _sys
    stg = _sys.argv[1] if len(_sys.argv) > 1 else "p1"
    nc = build_nc(stg)
    print("built", stg, "ok")


# version 10
# speedup vs baseline: 1.4552x; 1.0151x over previous
import numpy as np
import contextlib
import concourse.bass as bass
import concourse.mybir as mybir
from concourse.bass_utils import run_bass_kernel_spmd

F32 = mybir.dt.float32
BF16 = mybir.dt.bfloat16
AF = mybir.ActivationFunctionType
ALU = mybir.AluOpType
AX = mybir.AxisListType

NCORES = 8
T = 1024
NT = 8
D = 2048
KC = 16
NE = 32
I_FF = 768
IC = 6
NH = 16
HD = 128
BLK = 256
NB = 4
EPS_LN = 1e-5
EPS_RMS = 1e-6
SPARSE = True
CAP = 384
RW = 2080
NG = CAP // 128


class S:
    __slots__ = ("h", "c")

    def __init__(self, h):
        self.h = h
        self.c = 0


class Em:

    ENGS = ("tensor", "vector", "scalar", "gpsimd", "sync")

    def __init__(self, nc):
        self.nc = nc
        self.items = {k: [] for k in self.ENGS}

    def on(self, eng, fn, waits=(), inc=None, inc_n=1):
        ws = tuple((s.h, int(v)) for (s, v) in waits if v > 0)
        pair = None
        if inc is not None:
            inc.c += inc_n
            pair = (inc.h, inc_n)

        def run(e, fn=fn, ws=ws, pair=pair):
            for h, v in ws:
                e.wait_ge(h, v)
            r = fn(e)
            if pair is not None:
                r.then_inc(*pair)
        self.items[eng].append(run)

    def wait(self, eng, waits):
        ws = tuple((s.h, int(v)) for (s, v) in waits if v > 0)

        def run(e, ws=ws):
            for h, v in ws:
                e.wait_ge(h, v)
        self.items[eng].append(run)

    def flush(self):
        with self.nc.Block() as b:
            for eng in self.ENGS:
                lst = self.items[eng]
                if lst:
                    getattr(b, eng)(lambda e, lst=list(lst): [f(e) for f in lst])
        self.items = {k: [] for k in self.ENGS}


def build_nc(stage="full"):
    nc = bass.Bass("TRN2", target_bir_lowering=False)

    d_x = nc.dram_tensor("x_in", [T, D], F32, kind="ExternalInput").ap()
    d_wqkvuT = nc.dram_tensor("wqkvuT", [D, 4 * D], F32, kind="ExternalInput").ap()
    d_woutT = nc.dram_tensor("woutT", [D, D], F32, kind="ExternalInput").ap()
    d_rT = nc.dram_tensor("rT", [D, NE], F32, kind="ExternalInput").ap()
    d_wg = nc.dram_tensor("w_gate", [NE, D, I_FF], F32, kind="ExternalInput").ap()
    d_wu = nc.dram_tensor("w_up", [NE, D, I_FF], F32, kind="ExternalInput").ap()
    d_wd = nc.dram_tensor("w_down", [NE, I_FF, D], F32, kind="ExternalInput").ap()
    d_slope = nc.dram_tensor("slope_bc", [128, NH], F32, kind="ExternalInput").ap()
    d_arr = nc.dram_tensor("arr_b", [128, BLK], F32, kind="ExternalInput").ap()
    d_pos = nc.dram_tensor("pos_b", [128, T], F32, kind="ExternalInput").ap()
    d_idxT = nc.dram_tensor("idxT_b", [128, 2, BLK], F32, kind="ExternalInput").ap()
    d_krev = nc.dram_tensor("krev_b", [128, 2], F32, kind="ExternalInput").ap()
    d_c256 = nc.dram_tensor("c256_b", [128, 1], F32, kind="ExternalInput").ap()
    d_delta = nc.dram_tensor("delta_b", [128, NCORES], F32, kind="ExternalInput").ap()
    d_maskg = nc.dram_tensor("maskg_b", [128, NCORES], F32, kind="ExternalInput").ap()
    d_ident = nc.dram_tensor("ident", [128, 128], BF16, kind="ExternalInput").ap()

    d_out = nc.dram_tensor("out", [T, D], F32, kind="ExternalOutput").ap()

    d_x2s = nc.dram_tensor("x2s", [T, D], F32, kind="Internal").ap()
    d_x3s = nc.dram_tensor("x3s", [T, D], F32, kind="Internal").ap()
    d_eg = nc.dram_tensor("eg", [NE * CAP, RW], BF16, kind="Internal").ap()
    d_eo = nc.dram_tensor("eo", [NE * CAP, D], BF16, kind="Internal").ap()
    d_ltri = nc.dram_tensor("ltri", [128, 128], F32, kind="ExternalInput").ap()
    d_erange = nc.dram_tensor("erange_b", [128, NE], F32, kind="ExternalInput").ap()
    d_g128p = nc.dram_tensor("g128p_b", [128, NG], F32, kind="ExternalInput").ap()
    d_tokhi = nc.dram_tensor("tokhi_b", [128, NT], F32, kind="ExternalInput").ap()
    d_toklo = nc.dram_tensor("toklo_b", [128, NT], F32, kind="ExternalInput").ap()
    d_ones2 = nc.dram_tensor("ones2_b", [128, 2], F32, kind="ExternalInput").ap()
    d_onesrow = nc.dram_tensor("onesrow_b", [1, 128], F32, kind="ExternalInput").ap()
    d_ecap = nc.dram_tensor("ecap_b", [1, NE], F32, kind="ExternalInput").ap()
    d_ccin = nc.dram_tensor("ccin", [NH * HD, HD], F32, kind="Internal").ap()
    d_ccout = nc.dram_tensor("ccout", [NCORES * NH * HD, HD], F32, kind="Internal").ap()

    x_r = d_x.rearrange("(m p) d -> m p d", p=128)
    x2_r = d_x2s.rearrange("(m p) d -> m p d", p=128)
    x3_r = d_x3s.rearrange("(m p) d -> m p d", p=128)
    out_r = d_out.rearrange("(m p) d -> m p d", p=128)
    wg_r = d_wg.rearrange("e (k p) j -> e p k j", p=128)
    wu_r = d_wu.rearrange("e (k p) j -> e p k j", p=128)
    wd_r = d_wd.rearrange("e (k p) j -> e p k j", p=128)
    wq_r = d_wqkvuT.rearrange("(k p) j -> p k j", p=128)
    wo_r = d_woutT.rearrange("(k p) j -> p k j", p=128)
    rT_r = d_rT.rearrange("(k p) j -> p k j", p=128)
    eo_r = d_eo.rearrange("(r p) d -> r p d", p=128)
    ccin_r = d_ccin.rearrange("(h p) e -> p h e", p=128)
    ccout_r = d_ccout.rearrange("(j h p) e -> p j h e", j=NCORES, h=NH, p=128)

    st = contextlib.ExitStack()
    with st:
        def sem(name):
            return S(st.enter_context(nc.semaphore(name)))

        s_x = sem("s_x")
        s_w = sem("s_w")
        s_ld = [sem(f"s_ld{i}") for i in range(2)]
        s_st = [sem(f"s_st{i}") for i in range(2)]
        s_wg = sem("s_wg")
        s_wu = sem("s_wu")
        s_wd = sem("s_wd")
        s_pb = [sem(f"s_pb{i}") for i in range(4)]
        s_sc = [sem(f"s_sc{i}") for i in range(2)]
        s_eg = [sem(f"s_eg{i}") for i in range(2)]
        s_eo = [sem(f"s_eo{i}") for i in range(2)]
        s_gb = [sem(f"s_gb{i}") for i in range(4)]
        s_g = [sem(f"s_g{i}") for i in range(2)]
        s_pe = sem("s_pe")
        s_act = sem("s_act")
        s_pl = sem("s_pl")
        s_dve = sem("s_dve")
        s_cc = sem("s_cc")
        DMA_SEMS = ([s_x, s_w] + s_ld + s_st + [s_wg, s_wu, s_wd] + s_pb + s_g +
                    s_sc + s_eo + s_eg + s_gb)

        def all_dma_totals():
            return [(s, s.c) for s in DMA_SEMS]

        def phase_guard(em):
            tots = all_dma_totals()
            for _eng in Em.ENGS:
                em.wait(_eng, tots)

        c_ident = st.enter_context(nc.sbuf_tensor("c_ident", [128, 128], BF16))
        c_slope = st.enter_context(nc.sbuf_tensor("c_slope", [128, NH], F32))
        c_negsl = st.enter_context(nc.sbuf_tensor("c_negsl", [128, NH], F32))
        c_delta = st.enter_context(nc.sbuf_tensor("c_delta", [128, NCORES], F32))
        c_maskg = st.enter_context(nc.sbuf_tensor("c_maskg", [128, NCORES], F32))
        c_c256 = st.enter_context(nc.sbuf_tensor("c_c256", [128, 1], F32))
        rT_bf = st.enter_context(nc.sbuf_tensor("rT_bf", [128, KC, NE], BF16))
        c_ltri = st.enter_context(nc.sbuf_tensor("c_ltri", [128, 128], F32))
        c_erange = st.enter_context(nc.sbuf_tensor("c_erange", [128, NE], F32))
        c_g128p = st.enter_context(nc.sbuf_tensor("c_g128p", [128, NG], F32))
        c_tokhi = st.enter_context(nc.sbuf_tensor("c_tokhi", [128, NT], F32))
        c_toklo = st.enter_context(nc.sbuf_tensor("c_toklo", [128, NT], F32))
        c_ones2 = st.enter_context(nc.sbuf_tensor("c_ones2", [128, 2], F32))
        c_onesrow = st.enter_context(nc.sbuf_tensor("c_onesrow", [1, 128], F32))

        NS = lambda h: c_negsl[:, h:h + 1]

        em = Em(nc)
        em.on("sync", lambda e: e.dma_start(c_ident[:], d_ident), inc=s_x, inc_n=16)
        em.on("sync", lambda e: e.dma_start(c_slope[:], d_slope), inc=s_x, inc_n=16)
        em.on("sync", lambda e: e.dma_start(c_delta[:], d_delta), inc=s_x, inc_n=16)
        em.on("sync", lambda e: e.dma_start(c_maskg[:], d_maskg), inc=s_x, inc_n=16)
        em.on("sync", lambda e: e.dma_start(c_c256[:], d_c256), inc=s_x, inc_n=16)
        em.on("gpsimd", lambda e: e.dma_start(rT_bf[:], rT_r), inc=s_w, inc_n=16)
        em.on("sync", lambda e: e.dma_start(c_ltri[:], d_ltri), inc=s_x, inc_n=16)
        em.on("sync", lambda e: e.dma_start(c_erange[:], d_erange), inc=s_x, inc_n=16)
        em.on("sync", lambda e: e.dma_start(c_g128p[:], d_g128p), inc=s_x, inc_n=16)
        em.on("sync", lambda e: e.dma_start(c_tokhi[:], d_tokhi), inc=s_x, inc_n=16)
        em.on("sync", lambda e: e.dma_start(c_toklo[:], d_toklo), inc=s_x, inc_n=16)
        em.on("sync", lambda e: e.dma_start(c_ones2[:], d_ones2), inc=s_x, inc_n=16)
        em.on("sync", lambda e: e.dma_start(c_onesrow[:], d_onesrow), inc=s_x, inc_n=16)
        em.on("vector", lambda e: e.tensor_scalar(c_negsl[:], c_slope[:], -1.0, None,
                                                  ALU.mult),
              waits=[(s_x, s_x.c)], inc=s_dve)
        em.flush()


        def emit_ln(em, src_r, xn_fm, bufs, router_wr=None, entry_dma=(),
                    entry_sdve=0, dbg_dst_r=None, sp=None):
            (xt, sqdump, xn_tm, cols, lg, top8, eall, maskb, p_tr, p_rt) = bufs
            C = lambda i: cols[:, i:i + 1]
            act_affine = {}
            pe_tr_done = {}
            dbg_st = {}
            dbg_cp = {}
            dve_last = {-1: entry_sdve}
            for m in range(NT):
                if m == 0:
                    w = list(entry_dma)
                elif m >= 2:
                    w = [(s_act, act_affine[m - 2]), (s_st[m % 2], dbg_st.get(m - 2, 0))]
                else:
                    w = []
                em.on("sync", lambda e, m=m: e.dma_start(xt[m % 2][:], src_r[m]),
                      waits=w, inc=s_ld[m % 2], inc_n=16)
                ldx = s_ld[m % 2].c
                em.on("vector", lambda e, m=m: e.reduce_sum(C(0), xt[m % 2][:], axis=AX.X),
                      waits=[(s_ld[m % 2], ldx)] +
                            ([(s_act, act_affine[m - 1])] if m >= 1 else []),
                      inc=s_dve)
                sum_done = s_dve.c
                em.on("scalar", lambda e, m=m: e.activation(sqdump, xt[m % 2][:], AF.Square,
                                                            accum_out=C(7)),
                      waits=[(s_ld[m % 2], ldx), (s_dve, dve_last[m - 1])],
                      inc=s_act)
                ssq_done = s_act.c
                em.on("vector", lambda e: e.tensor_scalar(C(1), C(0), 1.0 / D, None,
                                                          ALU.mult),
                      waits=[(s_dve, sum_done)], inc=s_dve)
                mean_done = s_dve.c
                em.on("vector", lambda e: e.tensor_mul(C(2), C(1), C(1)),
                      waits=[(s_dve, mean_done)], inc=s_dve)
                m2_done = s_dve.c
                em.on("vector", lambda e: e.tensor_scalar(C(3), C(7), 1.0 / D, C(2),
                                                          ALU.mult, ALU.subtract),
                      waits=[(s_act, ssq_done), (s_dve, m2_done)], inc=s_dve)
                em.on("vector", lambda e: e.tensor_scalar(C(3), C(3), EPS_LN, None,
                                                          ALU.add),
                      waits=[(s_dve, s_dve.c)], inc=s_dve)
                vr_done = s_dve.c
                em.on("scalar", lambda e: e.activation(C(4), C(3), AF.Sqrt),
                      waits=[(s_dve, vr_done)], inc=s_act)
                sd_done = s_act.c
                em.on("vector", lambda e: e.reciprocal(C(5), C(4)),
                      waits=[(s_act, sd_done)], inc=s_dve)
                rinv_done = s_dve.c
                em.on("vector", lambda e: e.tensor_scalar(C(6), C(1), C(5), -1.0,
                                                          ALU.mult, ALU.mult),
                      waits=[(s_dve, rinv_done)], inc=s_dve)
                nb_done = s_dve.c
                w = [(s_dve, nb_done)]
                if m >= 1:
                    w.append((s_pe, pe_tr_done[m - 1]))
                    w.append((s_dve, dbg_cp.get(m - 1, 0)))
                if sp is not None:
                    xn_dst = sp["xns"][m % 2][:, 0:D]
                    w.append((s_sc[m % 2], sp["sc_cnt"][m % 2]))
                else:
                    xn_dst = xn_tm[:]
                em.on("scalar", lambda e, m=m, xn_dst=xn_dst: e.activation(
                          xn_dst, xt[m % 2][:], AF.Identity,
                          bias=C(6), scale=C(5)),
                      waits=w, inc=s_act)
                act_affine[m] = s_act.c
                cp_done = []
                cp_sem = []
                for c in range(KC):
                    semx = s_act if c % 2 == 0 else s_dve
                    eng = "scalar" if c % 2 == 0 else "vector"
                    w = [(s_act, act_affine[m])] if c < 2 else \
                        [(cp_sem[c - 2], cp_done[c - 2])]
                    tr_src = (sp["xns"][m % 2] if sp is not None else xn_tm)
                    em.on("tensor",
                          lambda e, c=c, tr_src=tr_src: e.transpose(
                              p_tr[c % 2][:],
                              tr_src[:, c * 128:(c + 1) * 128],
                              c_ident[:]),
                          waits=w, inc=s_pe)
                    tr = s_pe.c
                    if eng == "scalar":
                        em.on(eng, lambda e, c=c, m=m: e.copy(
                                  xn_fm[:, c, m * 128:(m + 1) * 128], p_tr[c % 2][:]),
                              waits=[(s_pe, tr)], inc=semx)
                    else:
                        em.on(eng, lambda e, c=c, m=m: e.tensor_copy(
                                  xn_fm[:, c, m * 128:(m + 1) * 128], p_tr[c % 2][:]),
                              waits=[(s_pe, tr)], inc=semx)
                    cp_done.append(semx.c)
                    cp_sem.append(semx)
                pe_tr_done[m] = s_pe.c
                xn_m_act, xn_m_dve = s_act.c, s_dve.c
                if dbg_dst_r is not None:
                    em.on("vector", lambda e, m=m: e.tensor_copy(xt[m % 2][:], xn_tm[:]),
                          waits=[(s_act, act_affine[m])], inc=s_dve)
                    dbg_cp[m] = s_dve.c
                    em.on("sync", lambda e, m=m: e.dma_start(dbg_dst_r[m], xt[m % 2][:]),
                          waits=[(s_dve, dbg_cp[m])], inc=s_st[m % 2], inc_n=16)
                    dbg_st[m] = s_st[m % 2].c
                if router_wr is not None:
                    for k in range(KC):
                        w = []
                        if k == 0:
                            w = [(s_act, xn_m_act), (s_dve, xn_m_dve), (s_w, 16),
                                 (s_dve, dve_last[m - 1])]
                        em.on("tensor", lambda e, k=k, m=m: e.matmul(
                                  p_rt[:], xn_fm[:, k, m * 128:(m + 1) * 128],
                                  rT_bf[:, k, :], start=(k == 0), stop=(k == KC - 1)),
                              waits=w, inc=(s_pe if k == KC - 1 else None))
                    rt_done = s_pe.c
                    em.on("vector", lambda e: e.tensor_copy(lg[:], p_rt[:]),
                          waits=[(s_pe, rt_done)], inc=s_dve)
                    em.on("vector", lambda e: e.max(top8[:], lg[:]),
                          waits=[(s_dve, s_dve.c)], inc=s_dve)
                    t8_done = s_dve.c
                    em.on("vector", lambda e: e.tensor_scalar(C(8), top8[:, 0:1], -1.0,
                                                              None, ALU.mult),
                          waits=[(s_dve, t8_done)], inc=s_dve)
                    nm0_done = s_dve.c
                    if sp is None:
                        em.on("scalar", lambda e: e.activation(eall[:], lg[:], AF.Exp,
                                                               bias=C(8)),
                              waits=[(s_dve, nm0_done)], inc=s_act)
                        eall_done = s_act.c
                        em.on("vector", lambda e: e.tensor_scalar(maskb[:], lg[:],
                                                                  top8[:, 7:8],
                                                                  None, ALU.is_ge),
                              waits=[(s_dve, t8_done)], inc=s_dve)
                        em.on("vector", lambda e: e.tensor_mul(eall[:], eall[:],
                                                               maskb[:]),
                              waits=[(s_act, eall_done), (s_dve, s_dve.c)],
                              inc=s_dve)
                        em.on("vector", lambda e: e.reduce_sum(C(9), eall[:],
                                                               axis=AX.X),
                              waits=[(s_dve, s_dve.c)], inc=s_dve)
                        em.on("vector", lambda e: e.reciprocal(C(10), C(9)),
                              waits=[(s_dve, s_dve.c)], inc=s_dve)
                        sinv_done = s_dve.c
                        em.on("vector", lambda e, m=m: e.tensor_scalar(
                                  router_wr[:, m, :], eall[:], C(10), 0.5,
                                  ALU.mult, ALU.mult),
                              waits=[(s_dve, sinv_done)], inc=s_dve)
                    else:
                        e8 = sp["e8"]; w8 = sp["w8"]; tidx = sp["tidx"]
                        tidxf = sp["tidxf"]; posS = sp["posS"]
                        poseS = sp["poseS"]; base = sp["base"]
                        slaf = sp["slaf"]; slau = sp["slau"]
                        p_pos = sp["p_pos"]; p_bb = sp["p_bb"]; p_cnt = sp["p_cnt"]
                        xns = sp["xns"][m % 2]
                        em.on("scalar", lambda e: e.activation(e8[:], top8[:], AF.Exp,
                                                               bias=C(8)),
                              waits=[(s_dve, nm0_done)], inc=s_act)
                        e8_done = s_act.c
                        em.on("vector", lambda e: e.reduce_sum(C(9), e8[:], axis=AX.X),
                              waits=[(s_act, e8_done)], inc=s_dve)
                        em.on("vector", lambda e: e.reciprocal(C(10), C(9)),
                              waits=[(s_dve, s_dve.c)], inc=s_dve)
                        sinv_done = s_dve.c
                        em.on("vector", lambda e: e.tensor_scalar(w8[:], e8[:], C(10),
                                                                  0.5, ALU.mult,
                                                                  ALU.mult),
                              waits=[(s_dve, sinv_done)], inc=s_dve)
                        w8_done = s_dve.c
                        em.on("vector", lambda e: e.tensor_scalar(maskb[:], lg[:],
                                                                  top8[:, 7:8],
                                                                  None, ALU.is_ge),
                              waits=[(s_dve, t8_done)], inc=s_dve)
                        mask_done = s_dve.c
                        em.on("vector", lambda e: e.max_index(tidx[:], top8[:], lg[:]),
                              waits=[(s_dve, mask_done)], inc=s_dve)
                        em.on("vector", lambda e: e.tensor_copy(tidxf[:], tidx[:]),
                              waits=[(s_dve, s_dve.c)], inc=s_dve)
                        tif_done = s_dve.c
                        em.on("tensor", lambda e: e.matmul(p_pos[:], c_ltri[:],
                                                           maskb[:], start=True,
                                                           stop=True),
                              waits=[(s_dve, mask_done),
                                     (s_dve, sp["posS_free"])], inc=s_pe)
                        em.on("tensor", lambda e: e.matmul(p_bb[:], c_onesrow[:],
                                                           base[:], start=True,
                                                           stop=True),
                              waits=[(s_dve, sp["base_done"])], inc=s_pe)
                        pos_pe = s_pe.c
                        em.on("vector", lambda e: e.tensor_copy(posS[:], p_pos[:]),
                              waits=[(s_pe, pos_pe)], inc=s_dve)
                        em.on("vector", lambda e: e.tensor_add(poseS[:], posS[:],
                                                               p_bb[:]),
                              waits=[(s_dve, s_dve.c)], inc=s_dve)
                        pose_done = s_dve.c
                        sp["posS_free"] = pose_done
                        em.on("tensor", lambda e: e.matmul(p_cnt[:],
                                                           c_ones2[:, 0:1],
                                                           maskb[:], start=True,
                                                           stop=True),
                              waits=[(s_dve, pose_done)], inc=s_pe)
                        cnt_pe = s_pe.c
                        em.on("vector", lambda e: e.tensor_add(base[:], base[:],
                                                               p_cnt[:]),
                              waits=[(s_pe, cnt_pe), (s_dve, s_dve.c)], inc=s_dve)
                        sp["base_done"] = s_dve.c
                        for k in range(8):
                            em.on("vector", lambda e, k=k: e.tensor_scalar(
                                      eall[:], c_erange[:], tidxf[:, k:k + 1],
                                      None, ALU.is_equal),
                                  waits=[(s_dve, tif_done if k == 0 else s_dve.c),
                                         (s_dve, pose_done)], inc=s_dve)
                            em.on("vector", lambda e: e.tensor_mul(eall[:], eall[:],
                                                                   poseS[:]),
                                  waits=[(s_dve, s_dve.c)], inc=s_dve)
                            em.on("vector", lambda e, k=k: e.reduce_sum(
                                      slaf[:, k:k + 1], eall[:], axis=AX.X),
                                  waits=[(s_dve, s_dve.c)], inc=s_dve)
                        em.on("vector", lambda e: e.tensor_scalar(slaf[:], slaf[:],
                                                                  float(NE * CAP - 1),
                                                                  None, ALU.min),
                              waits=[(s_dve, s_dve.c)], inc=s_dve)
                        em.on("vector", lambda e, m=m: e.tensor_copy(
                                  slau[:, m, :], slaf[:]),
                              waits=[(s_dve, s_dve.c)], inc=s_dve)
                        slau_done = s_dve.c
                        em.on("vector", lambda e, xns=xns: e.tensor_copy(
                                  xns[:, D:D + 8], tidxf[:]),
                              waits=[(s_dve, tif_done),
                                     (s_sc[m % 2], sp["sc_cnt"][m % 2])],
                              inc=s_dve)
                        em.on("vector", lambda e, xns=xns: e.tensor_copy(
                                  xns[:, D + 8:D + 16], w8[:]),
                              waits=[(s_dve, w8_done)], inc=s_dve)
                        em.on("vector", lambda e, m=m, xns=xns: e.tensor_copy(
                                  xns[:, D + 16:D + 17], c_tokhi[:, m:m + 1]),
                              inc=s_dve)
                        em.on("vector", lambda e, m=m, xns=xns: e.tensor_copy(
                                  xns[:, D + 17:D + 18], c_toklo[:, m:m + 1]),
                              inc=s_dve)
                        meta_done = s_dve.c
                        for k in range(8):
                            w = [(s_dve, meta_done), (s_dve, slau_done),
                                 (s_act, act_affine[m])]
                            if m == 0 and k == 0:
                                w += list(entry_dma)
                            em.on("gpsimd", lambda e, m=m, k=k, xns=xns:
                                  e.indirect_dma_start(
                                      d_eg,
                                      bass.IndirectOffsetOnAxis(
                                          slau[:, m, k:k + 1], 0),
                                      xns[:, 0:D + 18], None),
                                  waits=w, inc=s_sc[m % 2], inc_n=16)
                        sp["sc_cnt"][m % 2] = s_sc[m % 2].c
                dve_last[m] = s_dve.c
            return s_act.c, s_dve.c

        def emit_experts(em, xn_fm, router_wr, acc, wbufs, ghbufs, psums, ready,
                         entry_dma):
            wg_bf, wu_bf, wd_bf = wbufs
            g_s, h_bf = ghbufs
            pg, pu, pd = psums
            rdy_act, rdy_dve = ready
            pe_gate_end = pe_up_end = pe_down_end = None
            silu_cnt = []
            upev_cnt = []
            dnev_cnt = []
            for e_i in range(NE):
                em.on("gpsimd", lambda e, e_i=e_i: e.dma_start(wg_bf[:], wg_r[e_i]),
                      waits=([(s_pe, pe_gate_end)] if pe_gate_end is not None
                             else list(entry_dma)),
                      inc=s_wg, inc_n=16)
                wg_ld = s_wg.c
                em.on("gpsimd", lambda e, e_i=e_i: e.dma_start(wu_bf[:], wu_r[e_i]),
                      waits=([(s_pe, pe_up_end)] if pe_up_end is not None else []),
                      inc=s_wu, inc_n=16)
                wu_ld = s_wu.c
                em.on("gpsimd", lambda e, e_i=e_i: e.dma_start(wd_bf[:], wd_r[e_i]),
                      waits=([(s_pe, pe_down_end)] if pe_down_end is not None else []),
                      inc=s_wd, inc_n=16)
                wd_ld = s_wd.c
                gidx0 = len(silu_cnt)
                for th in range(2):
                    for c in range(IC):
                        g = len(silu_cnt)
                        for k in range(KC):
                            w = []
                            if k == 0:
                                w = [(s_wg, wg_ld)]
                                if g >= 2:
                                    w.append((s_act, silu_cnt[g - 2]))
                                if e_i == 0 and g == 0:
                                    w += [(s_act, rdy_act), (s_dve, rdy_dve)]
                            em.on("tensor", lambda e, k=k, c=c, th=th, g=g: e.matmul(
                                      pg[g % 2][:], wg_bf[:, k, c * 128:(c + 1) * 128],
                                      xn_fm[:, k, th * 512:(th + 1) * 512],
                                      start=(k == 0), stop=(k == KC - 1)),
                                  waits=w, inc=(s_pe if k == KC - 1 else None))
                        gate_pe = s_pe.c
                        em.on("scalar", lambda e, c=c, th=th, g=g: e.activation(
                                  g_s[:, c, th * 512:(th + 1) * 512], pg[g % 2][:],
                                  AF.Silu),
                              waits=[(s_pe, gate_pe)], inc=s_act)
                        silu_cnt.append(s_act.c)
                pe_gate_end = s_pe.c
                uidx0 = len(upev_cnt)
                for th in range(2):
                    for c in range(IC):
                        g = len(upev_cnt)
                        for k in range(KC):
                            w = []
                            if k == 0:
                                w = [(s_wu, wu_ld)]
                                if g >= 2:
                                    w.append((s_dve, upev_cnt[g - 2]))
                            em.on("tensor", lambda e, k=k, c=c, th=th, g=g: e.matmul(
                                      pu[g % 2][:], wu_bf[:, k, c * 128:(c + 1) * 128],
                                      xn_fm[:, k, th * 512:(th + 1) * 512],
                                      start=(k == 0), stop=(k == KC - 1)),
                                  waits=w, inc=(s_pe if k == KC - 1 else None))
                        up_pe = s_pe.c
                        em.on("vector", lambda e, c=c, th=th, g=g: e.tensor_mul(
                                  h_bf[:, c, th * 512:(th + 1) * 512], pu[g % 2][:],
                                  g_s[:, c, th * 512:(th + 1) * 512]),
                              waits=[(s_pe, up_pe),
                                     (s_act, silu_cnt[gidx0 + th * IC + c])],
                              inc=s_dve)
                        upev_cnt.append(s_dve.c)
                pe_up_end = s_pe.c
                for th in range(2):
                    h_rdy = upev_cnt[uidx0 + th * IC + IC - 1]
                    for mm in range(4):
                        m = th * 4 + mm
                        for q in range(4):
                            g = len(dnev_cnt)
                            for k in range(IC):
                                w = []
                                if k == 0:
                                    w = [(s_wd, wd_ld), (s_dve, h_rdy)]
                                    if g >= 2:
                                        w.append((s_dve, dnev_cnt[g - 2]))
                                em.on("tensor", lambda e, k=k, m=m, q=q, g=g: e.matmul(
                                          pd[g % 2][:],
                                          h_bf[:, k, m * 128:(m + 1) * 128],
                                          wd_bf[:, k, q * 512:(q + 1) * 512],
                                          start=(k == 0), stop=(k == IC - 1)),
                                      waits=w, inc=(s_pe if k == IC - 1 else None))
                            dn_pe = s_pe.c
                            if e_i == 0:
                                em.on("vector", lambda e, m=m, q=q, g=g: e.tensor_scalar(
                                          acc[:, m, q * 512:(q + 1) * 512], pd[g % 2][:],
                                          router_wr[:, m, 0:1], None, ALU.mult),
                                      waits=[(s_pe, dn_pe), (s_dve, rdy_dve)], inc=s_dve)
                            else:
                                em.on("vector", lambda e, m=m, q=q, g=g, e_i=e_i:
                                      e.scalar_tensor_tensor(
                                          acc[:, m, q * 512:(q + 1) * 512], pd[g % 2][:],
                                          router_wr[:, m, e_i:e_i + 1],
                                          acc[:, m, q * 512:(q + 1) * 512],
                                          ALU.mult, ALU.add),
                                      waits=[(s_pe, dn_pe)], inc=s_dve)
                            dnev_cnt.append(s_dve.c)
                pe_down_end = s_pe.c
            return s_dve.c

        def emit_resid(em, src_r, acc, dst_r, rings, experts_done, final=False):
            xt2, x2t = rings
            add_done = {}
            st_done = {}
            for m in range(NT):
                w = [(s_dve, experts_done)] if m == 0 else []
                if m >= 2:
                    w.append((s_dve, add_done[m - 2]))
                em.on("sync", lambda e, m=m: e.dma_start(xt2[m % 2][:], src_r[m]),
                      waits=w, inc=s_ld[m % 2], inc_n=16)
                ldx = s_ld[m % 2].c
                w = [(s_ld[m % 2], ldx)]
                if m == 0:
                    w.append((s_dve, experts_done))
                if m >= 2:
                    w.append((s_st[m % 2], st_done[m - 2]))
                em.on("vector", lambda e, m=m: e.tensor_add(x2t[m % 2][:], xt2[m % 2][:],
                                                            acc[:, m, :]),
                      waits=w, inc=s_dve)
                add_done[m] = s_dve.c
                em.on("sync", lambda e, m=m: e.dma_start(dst_r[m], x2t[m % 2][:]),
                      waits=[(s_dve, add_done[m])], inc=s_st[m % 2], inc_n=16)
                st_done[m] = s_st[m % 2].c
            if final:
                em.wait("sync", [(s_st[0], s_st[0].c), (s_st[1], s_st[1].c)])

        def ffn_pass(src_r, dst_r, final=False, tag="a"):
            entry_dma = all_dma_totals()
            entry_sdve = s_dve.c
            with contextlib.ExitStack() as ps:
                em = Em(nc)
                phase_guard(em)
                xn_fm = ps.enter_context(nc.sbuf_tensor("xn_fm_" + tag,
                                                        [128, KC, T], BF16))
                W_r = ps.enter_context(nc.sbuf_tensor("W_r_" + tag,
                                                      [128, NT, NE], F32))
                wg_bf = ps.enter_context(nc.sbuf_tensor("wg_bf_" + tag,
                                                        [128, KC, I_FF], BF16))
                wu_bf = ps.enter_context(nc.sbuf_tensor("wu_bf_" + tag,
                                                        [128, KC, I_FF], BF16))
                wd_bf = ps.enter_context(nc.sbuf_tensor("wd_bf_" + tag,
                                                        [128, IC, D], BF16))
                acc = ps.enter_context(nc.sbuf_tensor("acc_" + tag,
                                                      [128, NT, D], F32))
                cols = ps.enter_context(nc.sbuf_tensor("cols_" + tag, [128, 16], F32))
                lg = ps.enter_context(nc.sbuf_tensor("lg_" + tag, [128, NE], F32))
                top8 = ps.enter_context(nc.sbuf_tensor("top8_" + tag, [128, 8], F32))
                eall = ps.enter_context(nc.sbuf_tensor("eall_" + tag, [128, NE], F32))
                maskb = ps.enter_context(nc.sbuf_tensor("maskb_" + tag, [128, NE], F32))
                with contextlib.ExitStack() as lns:
                    xt = [lns.enter_context(nc.sbuf_tensor(f"xt{i}_" + tag,
                                                           [128, D], F32))
                          for i in range(2)]
                    xn_tm = lns.enter_context(nc.sbuf_tensor("xn_tm_" + tag,
                                                             [128, D], BF16))
                    with contextlib.ExitStack() as pss:
                        p_tr = [pss.enter_context(
                            nc.psum_tensor(f"p_tr{i}_" + tag, [128, 128], BF16))
                            for i in range(2)]
                        p_rt = pss.enter_context(nc.psum_tensor("p_rt_" + tag,
                                                                [128, NE], F32))
                        bufs = (xt, acc[:, 0, :], xn_tm, cols, lg, top8, eall, maskb,
                                p_tr, p_rt)
                        ready = emit_ln(em, src_r, xn_fm, bufs, router_wr=W_r,
                                        entry_dma=entry_dma, entry_sdve=entry_sdve)
                with contextlib.ExitStack() as exs:
                    g_s = exs.enter_context(nc.sbuf_tensor("g_s_" + tag,
                                                           [128, IC, T], BF16))
                    h_bf = exs.enter_context(nc.sbuf_tensor("h_bf_" + tag,
                                                            [128, IC, T], BF16))
                    pg = [exs.enter_context(nc.psum_tensor(f"pg{i}_" + tag,
                                                           [128, 512], F32))
                          for i in range(2)]
                    pu = [exs.enter_context(nc.psum_tensor(f"pu{i}_" + tag,
                                                           [128, 512], F32))
                          for i in range(2)]
                    pd = [exs.enter_context(nc.psum_tensor(f"pd{i}_" + tag,
                                                           [128, 512], F32))
                          for i in range(2)]
                    experts_done = emit_experts(em, xn_fm, W_r, acc,
                                                (wg_bf, wu_bf, wd_bf), (g_s, h_bf),
                                                (pg, pu, pd), ready, entry_dma)
                with contextlib.ExitStack() as rs:
                    xt2 = [rs.enter_context(nc.sbuf_tensor(f"xt2{i}_" + tag,
                                                           [128, D], F32))
                           for i in range(2)]
                    x2t = [rs.enter_context(nc.sbuf_tensor(f"x2t{i}_" + tag,
                                                           [128, D], F32))
                           for i in range(2)]
                    emit_resid(em, src_r, acc, dst_r, (xt2, x2t), experts_done,
                               final=final)
                    em.flush()

        def ffn_pass_sparse(src_r, dst_r, final=False, tag="a"):
            entry_dma = all_dma_totals()
            entry_sdve = s_dve.c
            with contextlib.ExitStack() as ps:
                em = Em(nc)
                phase_guard(em)
                T_ = tag
                xn_fm = ps.enter_context(nc.sbuf_tensor("sxn_" + T_,
                                                        [128, KC, T], BF16))
                wg_bf = ps.enter_context(nc.sbuf_tensor("swg_" + T_,
                                                        [128, KC, I_FF], BF16))
                wu_bf = ps.enter_context(nc.sbuf_tensor("swu_" + T_,
                                                        [128, KC, I_FF], BF16))
                wd_bf = ps.enter_context(nc.sbuf_tensor("swd_" + T_,
                                                        [128, IC, D], BF16))
                slau = ps.enter_context(nc.sbuf_tensor("slau_" + T_,
                                                       [128, NT, 8],
                                                       mybir.dt.uint32))
                cols = ps.enter_context(nc.sbuf_tensor("scols_" + T_,
                                                       [128, 16], F32))
                lg = ps.enter_context(nc.sbuf_tensor("slg_" + T_, [128, NE], F32))
                top8 = ps.enter_context(nc.sbuf_tensor("stop8_" + T_, [128, 8], F32))
                eall = ps.enter_context(nc.sbuf_tensor("seall_" + T_,
                                                       [128, NE], F32))
                maskb = ps.enter_context(nc.sbuf_tensor("smaskb_" + T_,
                                                        [128, NE], F32))
                e8 = ps.enter_context(nc.sbuf_tensor("se8_" + T_, [128, 8], F32))
                w8 = ps.enter_context(nc.sbuf_tensor("sw8_" + T_, [128, 8], F32))
                tidx = ps.enter_context(nc.sbuf_tensor("stidx_" + T_, [128, 8],
                                                       mybir.dt.uint32))
                tidxf = ps.enter_context(nc.sbuf_tensor("stidxf_" + T_,
                                                        [128, 8], F32))
                posS = ps.enter_context(nc.sbuf_tensor("sposS_" + T_,
                                                       [128, NE], F32))
                poseS = ps.enter_context(nc.sbuf_tensor("sposeS_" + T_,
                                                        [128, NE], F32))
                base = ps.enter_context(nc.sbuf_tensor("sbase_" + T_, [1, NE], F32))
                slaf = ps.enter_context(nc.sbuf_tensor("sslaf_" + T_, [128, 8], F32))
                cnt_bc = ps.enter_context(nc.sbuf_tensor("scntbc_" + T_,
                                                         [128, NE], F32))
                xns = [ps.enter_context(nc.sbuf_tensor(f"sxns{i}_" + T_,
                                                       [128, RW], BF16))
                       for i in range(2)]
                sp = {"xns": xns, "e8": e8, "w8": w8, "tidx": tidx, "tidxf": tidxf,
                      "posS": posS, "poseS": poseS, "base": base, "slaf": slaf,
                      "slau": slau, "sc_cnt": {0: s_sc[0].c, 1: s_sc[1].c},
                      "base_done": 0, "posS_free": 0}
                em.on("sync", lambda e: e.dma_start(base[:], d_ecap),
                      waits=entry_dma, inc=s_x, inc_n=16)
                base_ld = s_x.c
                em.on("vector", lambda e: e.tensor_copy(posS[0:1, :], base[:]),
                      waits=[(s_x, base_ld)], inc=s_dve)
                sp["base_done"] = s_dve.c
                with contextlib.ExitStack() as lns:
                    sdump = lns.enter_context(nc.sbuf_tensor("sdump_" + T_,
                                                             [128, D], F32))
                    xt = [lns.enter_context(nc.sbuf_tensor(f"sxt{i}_" + T_,
                                                           [128, D], F32))
                          for i in range(2)]
                    with contextlib.ExitStack() as pss:
                        p_tr = [pss.enter_context(
                            nc.psum_tensor(f"sptr{i}_" + T_, [128, 128], BF16))
                            for i in range(2)]
                        p_rt = pss.enter_context(nc.psum_tensor("sprt_" + T_,
                                                                [128, NE], F32))
                        p_pos = pss.enter_context(nc.psum_tensor("sppos_" + T_,
                                                                 [128, NE], F32))
                        p_bb = pss.enter_context(nc.psum_tensor("spbb_" + T_,
                                                                [128, NE], F32))
                        p_cnt = pss.enter_context(nc.psum_tensor("spcnt_" + T_,
                                                                 [1, NE], F32))
                        sp["p_pos"] = p_pos
                        sp["p_bb"] = p_bb
                        sp["p_cnt"] = p_cnt
                        bufs = (xt, sdump[:], None, cols, lg, top8, eall, maskb,
                                p_tr, p_rt)
                        ready = emit_ln(em, src_r, xn_fm, bufs, router_wr=True,
                                        entry_dma=entry_dma,
                                        entry_sdve=entry_sdve, sp=sp)
                        em.on("tensor", lambda e: e.matmul(p_bb[:], c_onesrow[:],
                                                           base[:], start=True,
                                                           stop=True),
                              waits=[(s_dve, sp["base_done"]),
                                     (s_dve, sp["posS_free"])], inc=s_pe)
                        cb_pe = s_pe.c
                        em.on("vector", lambda e: e.tensor_copy(cnt_bc[:], p_bb[:]),
                              waits=[(s_pe, cb_pe)], inc=s_dve)
                        cnt_bc_done = s_dve.c
                rdy_act, rdy_dve = ready
                with contextlib.ExitStack() as exs:
                    xg_st = [exs.enter_context(nc.sbuf_tensor(f"sxgst{i}_" + T_,
                                                              [128, RW], BF16))
                             for i in range(2)]
                    xg_fm = [exs.enter_context(nc.sbuf_tensor(f"sxgfm{i}_" + T_,
                                                              [128, KC, CAP],
                                                              BF16))
                             for i in range(2)]
                    g_s = exs.enter_context(nc.sbuf_tensor("sgs_" + T_,
                                                           [128, IC, CAP], BF16))
                    h_bf = exs.enter_context(nc.sbuf_tensor("shb_" + T_,
                                                            [128, IC, CAP], BF16))
                    out_st = [exs.enter_context(nc.sbuf_tensor(f"sost{i}_" + T_,
                                                               [128, D], BF16))
                              for i in range(2)]
                    wcols = [exs.enter_context(nc.sbuf_tensor(f"swc{i}_" + T_,
                                                              [128, NG], F32))
                             for i in range(2)]
                    toku = [exs.enter_context(nc.sbuf_tensor(f"stk{i}_" + T_,
                                                             [128, NG],
                                                             mybir.dt.uint32))
                            for i in range(2)]
                    mcol = exs.enter_context(nc.sbuf_tensor("smc_" + T_,
                                                            [128, 8], F32))
                    pg = [exs.enter_context(nc.psum_tensor(f"spg{i}_" + T_,
                                                           [128, CAP], F32))
                          for i in range(2)]
                    pu = [exs.enter_context(nc.psum_tensor(f"spu{i}_" + T_,
                                                           [128, CAP], F32))
                          for i in range(2)]
                    pd = [exs.enter_context(nc.psum_tensor(f"spd{i}_" + T_,
                                                           [128, 512], F32))
                          for i in range(2)]
                    p_tx = [exs.enter_context(nc.psum_tensor(f"sptx{i}_" + T_,
                                                             [128, 128], BF16))
                            for i in range(2)]
                    eg_rr = d_eg.rearrange("(r p) c -> r p c", p=128)
                    scat_done = [(s_sc[0], s_sc[0].c), (s_sc[1], s_sc[1].c)]
                    pe_gate_end = pe_up_end = pe_down_end = None
                    silu_cnt = []
                    upev_cnt = []
                    dnev_cnt = []
                    ld_cnt = {0: 0, 1: 0}
                    tr_cp = {}
                    wres = {}
                    eo_st = {}
                    lt_rdy = {}

                    def emit_gload(gi):
                        w = list(scat_done) if gi == 0 else []
                        if gi >= 2:
                            w.append((s_pe, tr_cp.get(gi - 2, 0)))
                            w.append((s_dve, wres.get(gi - 2, 0)))
                        em.on("sync", lambda e, gi=gi:
                              e.dma_start(xg_st[gi % 2][:], eg_rr[gi]),
                              waits=w, inc=s_eg[gi % 2], inc_n=16)
                        ld_cnt[gi % 2] = s_eg[gi % 2].c

                    def emit_tx(gi, g, c, par):
                        rpar, ldc = gi % 2, ld_cnt[gi % 2]
                        if c < 2:
                            w = [(s_eg[rpar], ldc),
                                 tr_cp.get((gi - 1, KC - 2 + c), (s_act, 0))]
                        else:
                            w = [tr_cp.get((gi, c - 2), (s_act, 0))]
                        em.on("tensor", lambda e, c=c, gi=gi: e.transpose(
                                  p_tx[c % 2][:],
                                  xg_st[gi % 2][:, c * 128:(c + 1) * 128],
                                  c_ident[:]),
                              waits=w, inc=s_pe)
                        trp = s_pe.c
                        em.on("scalar", lambda e, c=c, g=g, par=par:
                              e.copy(xg_fm[par][:, c, g * 128:(g + 1) * 128],
                                     p_tx[c % 2][:]),
                              waits=[(s_pe, trp)], inc=s_act)
                        tr_cp[(gi, c)] = (s_act, s_act.c)
                        if c == KC - 1:
                            tr_cp[gi] = s_pe.c

                    def emit_meta(gi, g, e_i, par):
                        rpar, ldc = gi % 2, ld_cnt[gi % 2]
                        em.on("vector", lambda e, gi=gi, e_i=e_i:
                              e.tensor_scalar(mcol[:],
                                              xg_st[gi % 2][:, D:D + 8],
                                              float(e_i), None,
                                              ALU.is_equal),
                              waits=[(s_eg[rpar], ldc),
                                     (s_dve, wres.get(gi - 2, 0))],
                              inc=s_dve)
                        em.on("vector", lambda e, gi=gi: e.tensor_mul(
                                  mcol[:], mcol[:],
                                  xg_st[gi % 2][:, D + 8:D + 16]),
                              waits=[(s_dve, s_dve.c)], inc=s_dve)
                        em.on("vector", lambda e, g=g, par=par: e.reduce_sum(
                                  wcols[par][:, g:g + 1], mcol[:], axis=AX.X),
                              waits=[(s_dve, s_dve.c)], inc=s_dve)
                        em.on("vector", lambda e, g=g, e_i=e_i: e.tensor_scalar(
                                  cols[:, 11:12], c_g128p[:, g:g + 1],
                                  float(e_i * CAP), None, ALU.add),
                              waits=[(s_dve, s_dve.c),
                                     (s_dve, cnt_bc_done)], inc=s_dve)
                        em.on("vector", lambda e, e_i=e_i: e.tensor_scalar(
                                  cols[:, 12:13], cols[:, 11:12],
                                  cnt_bc[:, e_i:e_i + 1], None, ALU.is_lt),
                              waits=[(s_dve, s_dve.c)], inc=s_dve)
                        em.on("vector", lambda e, g=g, par=par: e.tensor_mul(
                                  wcols[par][:, g:g + 1],
                                  wcols[par][:, g:g + 1], cols[:, 12:13]),
                              waits=[(s_dve, s_dve.c)], inc=s_dve)
                        em.on("vector", lambda e, gi=gi: e.tensor_scalar(
                                  cols[:, 13:14],
                                  xg_st[gi % 2][:, D + 16:D + 17],
                                  256.0, None, ALU.mult),
                              waits=[(s_dve, s_dve.c)], inc=s_dve)
                        em.on("vector", lambda e, gi=gi: e.tensor_tensor(
                                  cols[:, 13:14], cols[:, 13:14],
                                  xg_st[gi % 2][:, D + 17:D + 18], ALU.add),
                              waits=[(s_dve, s_dve.c)], inc=s_dve)
                        em.on("vector", lambda e: e.tensor_scalar(
                                  cols[:, 13:14], cols[:, 13:14],
                                  float(T), None, ALU.min),
                              waits=[(s_dve, s_dve.c)], inc=s_dve)
                        em.on("vector", lambda e, g=g, par=par: e.tensor_copy(
                                  toku[par][:, g:g + 1], cols[:, 13:14]),
                              waits=[(s_dve, s_dve.c)], inc=s_dve)
                        wres[gi] = s_dve.c

                    def build_LT(e_i):
                        par2 = e_i % 2
                        items = [lambda e_i=e_i: emit_gload(e_i * NG + 0),
                                 lambda e_i=e_i: emit_gload(e_i * NG + 1)]
                        for g in range(NG):
                            gi = e_i * NG + g
                            for c in range(KC):
                                items.append(
                                    lambda gi=gi, g=g, c=c: emit_tx(gi, g, c,
                                                                    par2))
                            items.append(
                                lambda gi=gi, g=g, e_i=e_i: emit_meta(gi, g,
                                                                      e_i, par2))
                            if g + 2 < NG:
                                items.append(
                                    lambda gi2=e_i * NG + g + 2:
                                    emit_gload(gi2))

                        def fin(e_i=e_i):
                            lt_rdy[e_i] = s_act.c
                        items.append(fin)
                        return items

                    for th in build_LT(0):
                        th()
                    for e_i in range(NE):
                        pending = build_LT(e_i + 1) if e_i + 1 < NE else []
                        em.on("gpsimd", lambda e, e_i=e_i: e.dma_start(wg_bf[:],
                                                                       wg_r[e_i]),
                              waits=([(s_pe, pe_gate_end)]
                                     if pe_gate_end is not None else entry_dma),
                              inc=s_wg, inc_n=16)
                        wg_ld = s_wg.c
                        em.on("gpsimd", lambda e, e_i=e_i: e.dma_start(wu_bf[:],
                                                                       wu_r[e_i]),
                              waits=([(s_pe, pe_up_end)]
                                     if pe_up_end is not None else []),
                              inc=s_wu, inc_n=16)
                        wu_ld = s_wu.c
                        em.on("gpsimd", lambda e, e_i=e_i: e.dma_start(wd_bf[:],
                                                                       wd_r[e_i]),
                              waits=([(s_pe, pe_down_end)]
                                     if pe_down_end is not None else []),
                              inc=s_wd, inc_n=16)
                        wd_ld = s_wd.c
                        par = e_i % 2
                        xg_rdy_act = lt_rdy[e_i]
                        gidx0 = len(silu_cnt)
                        for c in range(IC):
                            gq = len(silu_cnt)
                            for k in range(KC):
                                w = []
                                if k == 0:
                                    w = [(s_wg, wg_ld), (s_act, xg_rdy_act)]
                                    if gq >= 2:
                                        w.append((s_act, silu_cnt[gq - 2]))
                                em.on("tensor", lambda e, k=k, c=c, gq=gq, par=par:
                                      e.matmul(pg[gq % 2][:],
                                               wg_bf[:, k, c * 128:(c + 1) * 128],
                                               xg_fm[par][:, k, :],
                                               start=(k == 0),
                                               stop=(k == KC - 1)),
                                      waits=w, inc=(s_pe if k == KC - 1 else None))
                            gate_pe = s_pe.c
                            em.on("scalar", lambda e, c=c, gq=gq: e.activation(
                                      g_s[:, c, :], pg[gq % 2][:], AF.Silu),
                                  waits=[(s_pe, gate_pe)], inc=s_act)
                            silu_cnt.append(s_act.c)
                        pe_gate_end = s_pe.c
                        uidx0 = len(upev_cnt)
                        for c in range(IC):
                            gq = len(upev_cnt)
                            for k in range(KC):
                                w = []
                                if k == 0:
                                    w = [(s_wu, wu_ld)]
                                    if gq >= 2:
                                        w.append((s_dve, upev_cnt[gq - 2]))
                                em.on("tensor", lambda e, k=k, c=c, gq=gq, par=par:
                                      e.matmul(pu[gq % 2][:],
                                               wu_bf[:, k, c * 128:(c + 1) * 128],
                                               xg_fm[par][:, k, :],
                                               start=(k == 0),
                                               stop=(k == KC - 1)),
                                      waits=w, inc=(s_pe if k == KC - 1 else None))
                            up_pe = s_pe.c
                            em.on("vector", lambda e, c=c, gq=gq: e.tensor_mul(
                                      h_bf[:, c, :], pu[gq % 2][:], g_s[:, c, :]),
                                  waits=[(s_pe, up_pe),
                                         (s_act, silu_cnt[gidx0 + c])],
                                  inc=s_dve)
                            upev_cnt.append(s_dve.c)
                        pe_up_end = s_pe.c
                        h_rdy = upev_cnt[-1]
                        for g in range(NG):
                            gi = e_i * NG + g
                            for q in range(4):
                                dq = len(dnev_cnt)
                                for k in range(IC):
                                    w = []
                                    if k == 0:
                                        w = [(s_wd, wd_ld), (s_dve, h_rdy)]
                                        if dq >= 2:
                                            w.append((s_dve, dnev_cnt[dq - 2]))
                                    em.on("tensor", lambda e, k=k, g=g, q=q, dq=dq:
                                          e.matmul(
                                              pd[dq % 2][:],
                                              h_bf[:, k, g * 128:(g + 1) * 128],
                                              wd_bf[:, k, q * 512:(q + 1) * 512],
                                              start=(k == 0),
                                              stop=(k == IC - 1)),
                                          waits=w,
                                          inc=(s_pe if k == IC - 1 else None))
                                    if pending:
                                        pending.pop(0)()
                                dn_pe = s_pe.c
                                w = [(s_pe, dn_pe), (s_dve, wres[gi])]
                                if gi >= 2:
                                    w.append((s_eo[gi % 2], eo_st.get(gi - 2, 0)))
                                em.on("vector", lambda e, g=g, q=q, dq=dq, par=par,
                                      gi=gi: e.tensor_scalar(
                                          out_st[gi % 2][:,
                                                         q * 512:(q + 1) * 512],
                                          pd[dq % 2][:],
                                          wcols[par][:, g:g + 1], None, ALU.mult),
                                      waits=w, inc=s_dve)
                                dnev_cnt.append(s_dve.c)
                            ost_done = s_dve.c
                            em.on("sync", lambda e, gi=gi: e.dma_start(
                                      eo_r[gi], out_st[gi % 2][:]),
                                  waits=[(s_dve, ost_done)], inc=s_eo[gi % 2],
                                  inc_n=16)
                            eo_st[gi] = s_eo[gi % 2].c
                        while pending:
                            pending.pop(0)()
                        pe_down_end = s_pe.c
                    eo_all = [(s_eo[0], s_eo[0].c), (s_eo[1], s_eo[1].c)]
                with contextlib.ExitStack() as rs:
                    xt2 = [rs.enter_context(nc.sbuf_tensor(f"sxr{i}_" + T_,
                                                           [128, D], F32))
                           for i in range(2)]
                    x2t = [rs.enter_context(nc.sbuf_tensor(f"sxw{i}_" + T_,
                                                           [128, D], F32))
                           for i in range(2)]
                    gb = [rs.enter_context(nc.sbuf_tensor(f"sgb{i}_" + T_,
                                                          [128, D], BF16))
                          for i in range(4)]
                    gs = [rs.enter_context(nc.sbuf_tensor(f"sg1{i}_" + T_,
                                                          [128, D], F32))
                          for i in range(2)]
                    add_done = {}
                    st_done = {}
                    gb_free = {0: 0, 1: 0, 2: 0, 3: 0}
                    gs_free = {0: 0, 1: 0}
                    for m in range(NT):
                        w = []
                        if m >= 2:
                            w.append((s_dve, add_done[m - 2]))
                        em.on("sync", lambda e, m=m: e.dma_start(xt2[m % 2][:],
                                                                 src_r[m]),
                              waits=w, inc=s_ld[m % 2], inc_n=16)
                        ldx = s_ld[m % 2].c
                        g_ld = {}

                        def emit_gather(k, extra):
                            w = (list(eo_all) + [(s_dve, gb_free[k % 4])]
                                 + list(extra))
                            em.on("gpsimd", lambda e, m=m, k=k:
                                  e.indirect_dma_start(
                                      gb[k % 4][:], None,
                                      d_eo,
                                      bass.IndirectOffsetOnAxis(
                                          slau[:, m, k:k + 1], 0)),
                                  waits=w, inc=s_gb[k % 4], inc_n=16)
                            g_ld[k] = s_gb[k % 4].c

                        for k in range(4):
                            emit_gather(k, [])
                        pair = {}
                        for j in range(2):
                            em.on("vector", lambda e, j=j: e.tensor_add(
                                      gs[j][:], gb[2 * j][:], gb[2 * j + 1][:]),
                                  waits=[(s_gb[2 * j], g_ld[2 * j]),
                                         (s_gb[2 * j + 1], g_ld[2 * j + 1]),
                                         (s_dve, gs_free[j])],
                                  inc=s_dve)
                            pair[j] = s_dve.c
                        for k in range(4, 8):
                            emit_gather(k, [(s_dve, pair[(k - 4) // 2])])
                        for j in range(2):
                            em.on("vector", lambda e, j=j: e.tensor_add(
                                      gs[j][:], gs[j][:], gb[2 * j][:]),
                                  waits=[(s_gb[2 * j], g_ld[4 + 2 * j]),
                                         (s_dve, pair[j])],
                                  inc=s_dve)
                            gb_free[2 * j] = s_dve.c
                            em.on("vector", lambda e, j=j: e.tensor_add(
                                      gs[j][:], gs[j][:], gb[2 * j + 1][:]),
                                  waits=[(s_gb[2 * j + 1], g_ld[5 + 2 * j]),
                                         (s_dve, s_dve.c)],
                                  inc=s_dve)
                            gb_free[2 * j + 1] = s_dve.c
                            pair[j] = s_dve.c
                        em.on("vector", lambda e: e.tensor_add(
                                  gs[0][:], gs[0][:], gs[1][:]),
                              waits=[(s_dve, pair[0]), (s_dve, pair[1])],
                              inc=s_dve)
                        gs_free[1] = s_dve.c
                        w = [(s_ld[m % 2], ldx), (s_dve, s_dve.c)]
                        if m >= 2:
                            w.append((s_st[m % 2], st_done[m - 2]))
                        em.on("vector", lambda e, m=m: e.tensor_add(
                                  x2t[m % 2][:], xt2[m % 2][:], gs[0][:]),
                              waits=w, inc=s_dve)
                        gs_free[0] = s_dve.c
                        add_done[m] = s_dve.c
                        em.on("sync", lambda e, m=m: e.dma_start(dst_r[m],
                                                                 x2t[m % 2][:]),
                              waits=[(s_dve, add_done[m])], inc=s_st[m % 2],
                              inc_n=16)
                        st_done[m] = s_st[m % 2].c
                    if final:
                        em.wait("sync", [(s_st[0], s_st[0].c),
                                         (s_st[1], s_st[1].c)])
                    em.flush()
        if stage == "ln":
            with contextlib.ExitStack() as ps:
                em = Em(nc)
                xn_fm = ps.enter_context(nc.sbuf_tensor("xn_fm_d", [128, KC, T], BF16))
                W_r = ps.enter_context(nc.sbuf_tensor("W_r_d", [128, NT, NE], F32))
                cols = ps.enter_context(nc.sbuf_tensor("cols_d", [128, 16], F32))
                lg = ps.enter_context(nc.sbuf_tensor("lg_d", [128, NE], F32))
                top8 = ps.enter_context(nc.sbuf_tensor("top8_d", [128, 8], F32))
                eall = ps.enter_context(nc.sbuf_tensor("eall_d", [128, NE], F32))
                maskb = ps.enter_context(nc.sbuf_tensor("maskb_d", [128, NE], F32))
                sqd = ps.enter_context(nc.sbuf_tensor("sqd_d", [128, D], F32))
                xt = [ps.enter_context(nc.sbuf_tensor(f"xt{i}_d", [128, D], F32))
                      for i in range(2)]
                xn_tm = ps.enter_context(nc.sbuf_tensor("xn_tm_d", [128, D], BF16))
                p_tr = [ps.enter_context(nc.psum_tensor(f"p_tr{i}_d", [128, 128], BF16))
                        for i in range(2)]
                p_rt = ps.enter_context(nc.psum_tensor("p_rt_d", [128, NE], F32))
                bufs = (xt, sqd[:], xn_tm, cols, lg, top8, eall, maskb, p_tr, p_rt)
                emit_ln(em, x_r, xn_fm, bufs, router_wr=W_r,
                        entry_dma=all_dma_totals(), dbg_dst_r=out_r)
                em.wait("sync", [(s_st[0], s_st[0].c), (s_st[1], s_st[1].c)])
                em.flush()
            return nc
        p1_dst = out_r if stage == "p1" else x2_r
        if SPARSE:
            ffn_pass_sparse(x_r, p1_dst, final=(stage == "p1"), tag="p1")
        else:
            ffn_pass(x_r, p1_dst, final=(stage == "p1"), tag="p1")
        if stage == "p1":
            return nc

        with contextlib.ExitStack() as p2:
            em = Em(nc)
            phase_guard(em)
            entry_dma = all_dma_totals()
            entry_sdve = s_dve.c
            o_tm = p2.enter_context(nc.sbuf_tensor("o_tm", [128, NT, D], BF16))
            u_fm = p2.enter_context(nc.sbuf_tensor("u_fm", [128, NH, T], BF16))
            with contextlib.ExitStack() as p2b:
                xn2 = p2b.enter_context(nc.sbuf_tensor("xn2", [128, KC, T], BF16))
                cols2 = p2b.enter_context(nc.sbuf_tensor("cols2", [128, 16], F32))
                if stage in ("p2o", "p2q", "p2kv"):
                    sq2 = p2b.enter_context(nc.sbuf_tensor("sq2", [128, D], F32))
                with contextlib.ExitStack() as lns:
                    if stage not in ("p2o", "p2q", "p2kv"):
                        sq2 = lns.enter_context(nc.sbuf_tensor("sq2", [128, D], F32))
                    xt = [lns.enter_context(nc.sbuf_tensor(f"xtb{i}", [128, D], F32))
                          for i in range(2)]
                    xn_tm = lns.enter_context(nc.sbuf_tensor("xn_tm2", [128, D], BF16))
                    with contextlib.ExitStack() as pss:
                        p_tr = [pss.enter_context(
                            nc.psum_tensor(f"p_trb{i}", [128, 128], BF16))
                            for i in range(2)]
                        bufs = (xt, sq2[:], xn_tm, cols2, None, None, None, None,
                                p_tr, None)
                        ln2_act, ln2_dve = emit_ln(em, x2_r, xn2, bufs, router_wr=None,
                                                   entry_dma=entry_dma,
                                                   entry_sdve=entry_sdve)
                c_arr = p2b.enter_context(nc.sbuf_tensor("c_arr", [128, BLK], F32))
                c_pos = p2b.enter_context(nc.sbuf_tensor("c_pos", [128, T], F32))
                c_idxT = p2b.enter_context(nc.sbuf_tensor("c_idxT", [128, 2, BLK], F32))
                c_krev = p2b.enter_context(nc.sbuf_tensor("c_krev", [128, 2], F32))
                kv_sb = p2b.enter_context(nc.sbuf_tensor("kv_sb", [128, NH, HD], F32))
                kv_bf = p2b.enter_context(nc.sbuf_tensor("kv_bf", [128, HD], BF16))
                q_full = p2b.enter_context(nc.sbuf_tensor("q_full", [128, NH, T], BF16))
                k_buf = [p2b.enter_context(nc.sbuf_tensor(f"k_buf{i}", [128, T], BF16))
                         for i in range(2)]
                v_buf = [p2b.enter_context(nc.sbuf_tensor(f"v_buf{i}", [128, NT, HD],
                                                          BF16)) for i in range(2)]
                pbuf = [p2b.enter_context(nc.sbuf_tensor(f"pbuf{i}", [128, KC, 128],
                                                         BF16)) for i in range(4)]
                qd4 = p2b.enter_context(nc.sbuf_tensor("qd4", [128, 4, BLK], F32))
                kd4 = p2b.enter_context(nc.sbuf_tensor("kd4", [128, 4, 2], F32))
                dg4 = p2b.enter_context(nc.sbuf_tensor("dg4", [128, 4, 2, BLK], F32))
                bd4 = p2b.enter_context(nc.sbuf_tensor("bd4", [128, 4], F32))
                qd_scr = p2b.enter_context(nc.sbuf_tensor("qd_scr", [128, BLK], BF16))
                kqd = p2b.enter_context(nc.sbuf_tensor("kqd", [128, 2, BLK], BF16))
                kdT = p2b.enter_context(nc.sbuf_tensor("kdT", [128, 2, HD], BF16))
                vf = [p2b.enter_context(nc.sbuf_tensor(f"vf{i}", [128, T], BF16))
                      for i in range(2)]
                pa = [p2b.enter_context(nc.psum_tensor(f"pa{i}", [128, 512], F32))
                      for i in range(2)]
                pkq = [p2b.enter_context(nc.psum_tensor(f"pkq{i}", [128, BLK], F32))
                       for i in range(2)]
                po = [p2b.enter_context(nc.psum_tensor(f"po{i}", [128, HD], F32))
                      for i in range(2)]
                pkv = p2b.enter_context(nc.psum_tensor("pkv", [128, HD], F32))
                pkt = p2b.enter_context(nc.psum_tensor("pkt", [128, HD], BF16))

                em.on("sync", lambda e: e.dma_start(c_arr[:], d_arr),
                      waits=all_dma_totals() + [(s_act, ln2_act),
                                                (s_dve, ln2_dve)],
                      inc=s_x, inc_n=16)
                em.on("sync", lambda e: e.dma_start(c_pos[:], d_pos), inc=s_x, inc_n=16)
                em.on("sync", lambda e: e.dma_start(c_idxT[:], d_idxT),
                      inc=s_x, inc_n=16)
                em.on("sync", lambda e: e.dma_start(c_krev[:], d_krev),
                      inc=s_x, inc_n=16)
                consts_ld = s_x.c

                proj_evac = {}
                pe_piece_done = {}
                piece_ld_h = {}
                proj_act_h = {}
                decays_done_h = {}
                kvbf_done_h = {}
                pkq_guard = {0: 0, 1: 0}
                po_guard = {0: 0, 1: 0}
                pkv_guard = 0
                pkt_guard = 0
                kv_last_dve = 0
                prev_o_pe = 0
                first_piece = [True]

                def emit_decays(grp, w0):
                    for hh in range(4):
                        h = grp * 4 + hh
                        em.on("scalar", lambda e, hh=hh, h=h: e.activation(
                                  qd4[:, hh, :], c_arr[:], AF.Exp, scale=NS(h)),
                              waits=(w0 if hh == 0 else []), inc=s_act)
                        em.on("scalar", lambda e, hh=hh, h=h: e.activation(
                                  kd4[:, hh, :], c_krev[:], AF.Exp, scale=NS(h)),
                              inc=s_act)
                        em.on("scalar", lambda e, hh=hh, h=h: e.activation(
                                  dg4[:, hh, :, :], c_idxT[:], AF.Exp, scale=NS(h)),
                              inc=s_act)
                        em.on("scalar", lambda e, hh=hh, h=h: e.activation(
                                  bd4[:, hh:hh + 1], c_c256[:], AF.Exp, scale=NS(h)),
                              inc=s_act)
                    dd = s_act.c
                    for hh in range(4):
                        decays_done_h[grp * 4 + hh] = dd

                def emit_pieces(h):
                    piece_cols = [h * 128, D + h * 128, 2 * D + h * 128,
                                  3 * D + h * 128]
                    lds = []
                    for t_i in range(4):
                        pi = 4 * h + t_i
                        if first_piece[0]:
                            w = all_dma_totals() + [(s_act, ln2_act),
                                                    (s_dve, ln2_dve)]
                            first_piece[0] = False
                        elif pi >= 4:
                            w = [(s_pe, pe_piece_done[pi - 4])]
                        else:
                            w = []
                        c0 = piece_cols[t_i]
                        em.on("gpsimd", lambda e, pi=pi, c0=c0: e.dma_start(
                                  pbuf[pi % 4][:], wq_r[:, :, c0:c0 + 128]),
                              waits=w, inc=s_pb[pi % 4], inc_n=16)
                        lds.append(s_pb[pi % 4].c)
                    piece_ld_h[h] = lds

                def emit_proj_part(h, part, pending=None):
                    piece_ld = piece_ld_h[h]

                    def drain(n=1):
                        if pending:
                            for _ in range(n):
                                if pending:
                                    pending.pop(0)()

                    if part in (0, 1):
                        t_i = part
                        buf = pbuf[(4 * h + t_i) % 4]
                        psem = s_pb[(4 * h + t_i) % 4]
                        for th in range(2):
                            for k in range(KC):
                                w = []
                                if k == 0:
                                    w = [(psem, piece_ld[t_i]),
                                         (s_act, proj_evac.get(th, 0))]
                                    if h == 0 and t_i == 0 and th == 0:
                                        w += [(s_act, ln2_act), (s_dve, ln2_dve)]
                                em.on("tensor", lambda e, k=k, th=th, buf=buf:
                                      e.matmul(pa[th][:], buf[:, k, :],
                                               xn2[:, k, th * 512:(th + 1) * 512],
                                               start=(k == 0), stop=(k == KC - 1)),
                                      waits=w, inc=(s_pe if k == KC - 1 else None))
                                drain()
                            mm_done = s_pe.c
                            dest = q_full[:, h, th * 512:(th + 1) * 512] \
                                if t_i == 0 \
                                else k_buf[h % 2][:, th * 512:(th + 1) * 512]
                            em.on("scalar", lambda e, th=th, dest=dest: e.activation(
                                      dest, pa[th][:], AF.Silu),
                                  waits=[(s_pe, mm_done)], inc=s_act)
                            proj_evac[th] = s_act.c
                        pe_piece_done[4 * h + t_i] = s_pe.c
                    elif part == 2:
                        bufv = pbuf[(4 * h + 2) % 4]
                        psemv = s_pb[(4 * h + 2) % 4]
                        for th in range(2):
                            for k in range(KC):
                                w = []
                                if k == 0:
                                    w = [(psemv, piece_ld[2]),
                                         (s_act, proj_evac.get(th, 0))]
                                em.on("tensor", lambda e, k=k, th=th, bufv=bufv:
                                      e.matmul(pa[th][:], bufv[:, k, :],
                                               xn2[:, k, th * 512:(th + 1) * 512],
                                               start=(k == 0), stop=(k == KC - 1)),
                                      waits=w, inc=(s_pe if k == KC - 1 else None))
                                drain()
                            mm_done = s_pe.c
                            em.on("scalar", lambda e, th=th, h=h: e.activation(
                                      vf[h % 2][:, th * 512:(th + 1) * 512],
                                      pa[th][:], AF.Copy),
                                  waits=[(s_pe, mm_done)], inc=s_act)
                            proj_evac[th] = s_act.c
                        vf_act = s_act.c
                        for mt in range(NT):
                            em.on("tensor", lambda e, mt=mt, h=h: e.transpose(
                                      pkt[:],
                                      vf[h % 2][:, mt * 128:(mt + 1) * 128],
                                      c_ident[:]),
                                  waits=[(s_dve, state["pkt_guard"]),
                                         (s_act, vf_act)], inc=s_pe)
                            kt_pe = s_pe.c
                            em.on("vector", lambda e, mt=mt, h=h: e.tensor_copy(
                                      v_buf[h % 2][:, mt, :], pkt[:]),
                                  waits=[(s_pe, kt_pe)], inc=s_dve)
                            state["pkt_guard"] = s_dve.c
                            drain()
                        pe_piece_done[4 * h + 2] = s_pe.c
                    else:
                        bufu = pbuf[(4 * h + 3) % 4]
                        psemu = s_pb[(4 * h + 3) % 4]
                        for th in range(2):
                            for k in range(KC):
                                w = []
                                if k == 0:
                                    w = [(psemu, piece_ld[3]),
                                         (s_act, proj_evac.get(th, 0))]
                                em.on("tensor", lambda e, k=k, th=th, bufu=bufu:
                                      e.matmul(pa[th][:], bufu[:, k, :],
                                               xn2[:, k, th * 512:(th + 1) * 512],
                                               start=(k == 0), stop=(k == KC - 1)),
                                      waits=w, inc=(s_pe if k == KC - 1 else None))
                                drain()
                            mm_done = s_pe.c
                            em.on("scalar", lambda e, th=th, h=h: e.activation(
                                      u_fm[:, h, th * 512:(th + 1) * 512], pa[th][:],
                                      AF.Copy),
                                  waits=[(s_pe, mm_done)], inc=s_act)
                            proj_evac[th] = s_act.c
                        pe_piece_done[4 * h + 3] = s_pe.c
                        proj_act_h[h] = s_act.c

                def build_scan_block(h, i):
                    hh = h % 4
                    loc = {}
                    items = []

                    def it_qd():
                        decays_done = decays_done_h[h]
                        proj_act = proj_act_h[h]
                        em.on("vector", lambda e, h=h, hh=hh, i=i: e.tensor_mul(
                                  qd_scr[:], q_full[:, h, i * BLK:(i + 1) * BLK],
                                  qd4[:, hh, :]),
                              waits=[(s_act, max(decays_done, proj_act)),
                                     (s_dve, s_dve.c), (s_pe, state["prev_o_pe"])],
                              inc=s_dve)
                        loc["qd_done"] = s_dve.c
                    items.append(it_qd)

                    def it_kq(j):
                        proj_act = proj_act_h[h]
                        w = [(s_dve, pkq_guard[j])]
                        if i == 0 and j == 0:
                            w.append((s_act, proj_act))
                        em.on("tensor", lambda e, h=h, i=i, j=j: e.matmul(
                                  pkq[j][:],
                                  k_buf[h % 2][:, i * BLK + j * 128:
                                                i * BLK + (j + 1) * 128],
                                  q_full[:, h, i * BLK:(i + 1) * BLK],
                                  start=True, stop=True),
                              waits=w, inc=s_pe)
                        kq_pe = s_pe.c
                        em.on("vector", lambda e, hh=hh, j=j: e.tensor_mul(
                                  kqd[:, j, :], pkq[j][:], dg4[:, hh, j, :]),
                              waits=[(s_pe, kq_pe)], inc=s_dve)
                        pkq_guard[j] = s_dve.c
                        loc["kqd_done"] = s_dve.c
                    items.append(lambda: it_kq(0))
                    items.append(lambda: it_kq(1))

                    def it_po(j2):
                        kvbf_done = kvbf_done_h.get(h, 0)
                        first = True
                        if i > 0:
                            em.on("tensor", lambda e, j2=j2: e.matmul(
                                      po[j2][:],
                                      qd_scr[:, j2 * 128:(j2 + 1) * 128],
                                      kv_bf[:], start=True, stop=False),
                                  waits=[(s_dve, loc["qd_done"]),
                                         (s_act, kvbf_done),
                                         (s_dve, po_guard[j2])])
                            first = False
                        for js in range(2):
                            w = []
                            if js == 0:
                                w = [(s_dve, loc["kqd_done"]),
                                     (s_dve, loc["qd_done"])]
                                if first:
                                    w.append((s_dve, po_guard[j2]))
                            em.on("tensor", lambda e, i=i, j2=j2, js=js, h=h,
                                  first=first: e.matmul(
                                      po[j2][:],
                                      kqd[:, js, j2 * 128:(j2 + 1) * 128],
                                      v_buf[h % 2][:, 2 * i + js, :],
                                      start=(first and js == 0),
                                      stop=(js == 1)),
                                  waits=w, inc=(s_pe if js == 1 else None))
                            first = False
                        if j2 == 1:
                            loc["o_pe"] = s_pe.c
                            state["prev_o_pe"] = loc["o_pe"]
                    items.append(lambda: it_po(0))
                    items.append(lambda: it_po(1))

                    def it_oevac():
                        for j2 in range(2):
                            em.on("vector", lambda e, i=i, j2=j2, h=h:
                                  e.tensor_copy(
                                      o_tm[:, 2 * i + j2, h * 128:(h + 1) * 128],
                                      po[j2][:]),
                                  waits=[(s_pe, loc["o_pe"])], inc=s_dve)
                            po_guard[j2] = s_dve.c
                    items.append(it_oevac)

                    def it_kv(j):
                        decays_done = decays_done_h[h]
                        em.on("tensor", lambda e, h=h, i=i, j=j: e.transpose(
                                  pkt[:],
                                  k_buf[h % 2][:, i * BLK + j * 128:
                                                i * BLK + (j + 1) * 128],
                                  c_ident[:]),
                              waits=[(s_dve, state["pkt_guard"])], inc=s_pe)
                        kt_pe = s_pe.c
                        em.on("vector", lambda e, hh=hh, j=j: e.tensor_scalar(
                                  kdT[:, j, :], pkt[:], kd4[:, hh, j:j + 1],
                                  None, ALU.mult),
                              waits=[(s_pe, kt_pe), (s_act, decays_done)],
                              inc=s_dve)
                        state["pkt_guard"] = s_dve.c
                        kdt_done = s_dve.c
                        em.on("tensor", lambda e, j=j, h=h, i=i: e.matmul(
                                  pkv[:], kdT[:, j, :],
                                  v_buf[h % 2][:, 2 * i + j, :],
                                  start=(j == 0), stop=(j == 1)),
                              waits=[(s_dve, kdt_done)] +
                                    ([(s_dve, state["pkv_guard"])]
                                     if j == 0 else []),
                              inc=(s_pe if j == 1 else None))
                        if j == 1:
                            loc["kv_pe"] = s_pe.c
                    items.append(lambda: it_kv(0))
                    items.append(lambda: it_kv(1))

                    def it_kvup():
                        decays_done = decays_done_h[h]
                        kv_pe = loc["kv_pe"]
                        if i == 0:
                            em.on("vector", lambda e, h=h: e.tensor_copy(
                                      kv_sb[:, h, :], pkv[:]),
                                  waits=[(s_pe, kv_pe),
                                         (s_dve, state["kv_last_dve"])],
                                  inc=s_dve)
                        else:
                            em.on("vector", lambda e, h=h, hh=hh:
                                  e.scalar_tensor_tensor(
                                      kv_sb[:, h, :], kv_sb[:, h, :],
                                      bd4[:, hh:hh + 1], pkv[:],
                                      ALU.mult, ALU.add),
                                  waits=[(s_pe, kv_pe), (s_act, decays_done),
                                         (s_dve, state["kv_last_dve"])],
                                  inc=s_dve)
                        state["pkv_guard"] = s_dve.c
                        state["kv_last_dve"] = s_dve.c
                        if i < NB - 1:
                            em.on("scalar", lambda e, h=h: e.activation(
                                      kv_bf[:], kv_sb[:, h, :], AF.Copy),
                                  waits=[(s_dve, state["kv_last_dve"]),
                                         (s_pe, loc["o_pe"])],
                                  inc=s_act)
                            kvbf_done_h[h] = s_act.c
                    items.append(it_kvup)
                    return items

                state = {"prev_o_pe": 0, "pkt_guard": 0, "pkv_guard": 0,
                         "kv_last_dve": 0}
                emit_decays(0, [(s_x, consts_ld)])
                emit_pieces(0)
                for part in range(4):
                    emit_proj_part(0, part)
                for h in range(NH):
                    for i in range(NB):
                        items = build_scan_block(h, i)
                        nxt = h + 1
                        if nxt < NH:
                            if i == 0:
                                emit_pieces(nxt)
                            emit_proj_part(nxt, i, pending=items)
                        while items:
                            items.pop(0)()
                    if (h + 1) % 4 == 0 and h + 1 < NH:
                        emit_decays((h + 1) // 4, [(s_dve, s_dve.c)])
                kv_last_dve = state["kv_last_dve"]
                po_guard = dict(po_guard)

                if stage == "p2kv":
                    scan_dve, scan_act = s_dve.c, s_act.c
                    em.on("scalar", lambda e: e.copy(sq2[:], kv_sb.ap().rearrange(
                              "p h e -> p (h e)")),
                          waits=[(s_dve, scan_dve), (s_act, scan_act)], inc=s_act)
                    aa = s_act.c
                    em.on("sync", lambda e: e.dma_start(out_r[0], sq2[:]),
                          waits=[(s_act, aa)], inc=s_st[0], inc_n=16)
                    st0 = s_st[0].c
                    em.on("scalar", lambda e: e.copy(sq2[:, 0:HD], kv_bf[:]),
                          waits=[(s_st[0], st0)], inc=s_act)
                    aa = s_act.c
                    em.on("sync", lambda e: e.dma_start(out_r[1], sq2[:]),
                          waits=[(s_act, aa)], inc=s_st[0], inc_n=16)
                    em.wait("sync", [(s_st[0], s_st[0].c)])
                    em.flush()
                    return nc
                if stage == "p2s":
                    scan_dve, scan_act = s_dve.c, s_act.c
                    dst_done = {}
                    for m in range(NT):
                        em.on("scalar", lambda e, m=m: e.copy(sq2[:], o_tm[:, m, :]),
                              waits=[(s_dve, scan_dve), (s_act, scan_act),
                                     (s_st[0], dst_done.get(m - 1, 0))],
                              inc=s_act)
                        aa = s_act.c
                        em.on("sync", lambda e, m=m: e.dma_start(out_r[m], sq2[:]),
                              waits=[(s_act, aa)], inc=s_st[0], inc_n=16)
                        dst_done[m] = s_st[0].c
                    em.wait("sync", [(s_st[0], s_st[0].c)])
                    em.flush()
                    return nc
                skip_cc = (stage == "p2x")
                if not skip_cc:
                    pass
                em.on("sync", lambda e: e.dma_start(ccin_r, kv_sb[:]),
                      waits=[(s_dve, s_dve.c), (s_act, s_act.c)], inc=s_x, inc_n=16)
                if not skip_cc:
                    em.on("gpsimd", lambda e: e.collective_compute(
                              "AllGather", ALU.bypass,
                              replica_groups=[list(range(NCORES))],
                              ins=[d_ccin.opt()], outs=[d_ccout.opt()]),
                          waits=[(s_x, s_x.c)], inc=s_cc, inc_n=1)
                else:
                    em.on("gpsimd", lambda e: e.dma_start(
                              d_ccout.rearrange("(j n) e -> j n e", j=NCORES)[0],
                              d_ccin),
                          waits=[(s_x, s_x.c)], inc=s_cc, inc_n=16)

                gath = [p2b.enter_context(nc.sbuf_tensor(f"gath{i}",
                                                         [128, NCORES, HD], F32))
                        for i in range(2)]
                cf = p2b.enter_context(nc.sbuf_tensor("cf", [128, NCORES], F32))
                kvst = p2b.enter_context(nc.sbuf_tensor("kvst", [128, HD], F32))
                kvst_bf = p2b.enter_context(nc.sbuf_tensor("kvst_bf", [128, HD], BF16))
                posd = p2b.enter_context(nc.sbuf_tensor("posd", [128, T], F32))
                qpd = p2b.enter_context(nc.sbuf_tensor("qpd", [128, T], BF16))
                kvst_use = {}
                for h in (range(NH) if not skip_cc else []):
                    w = [(s_cc, 1)]
                    if h >= 2:
                        w.append((s_dve, kvst_use[h - 2]))
                    em.on("sync", lambda e, h=h: e.dma_start(gath[h % 2][:],
                                                             ccout_r[:, :, h, :]),
                          waits=w, inc=s_g[h % 2], inc_n=16)
                    g_ld = s_g[h % 2].c
                    em.on("scalar", lambda e, h=h: e.activation(cf[:], c_delta[:],
                                                                AF.Exp, scale=NS(h)),
                          waits=[(s_dve, s_dve.c)], inc=s_act)
                    cf_act = s_act.c
                    em.on("vector", lambda e: e.tensor_mul(cf[:], cf[:], c_maskg[:]),
                          waits=[(s_act, cf_act)], inc=s_dve)
                    cf_done = s_dve.c
                    em.on("vector", lambda e, h=h: e.tensor_scalar(
                              kvst[:], gath[h % 2][:, 0, :], cf[:, 0:1],
                              None, ALU.mult),
                          waits=[(s_g[h % 2], g_ld), (s_dve, cf_done)], inc=s_dve)
                    for j in range(1, NCORES):
                        em.on("vector", lambda e, j=j, h=h: e.scalar_tensor_tensor(
                                  kvst[:], gath[h % 2][:, j, :], cf[:, j:j + 1],
                                  kvst[:], ALU.mult, ALU.add),
                              waits=[(s_dve, s_dve.c)], inc=s_dve)
                    kvst_done = s_dve.c
                    em.on("scalar", lambda e: e.activation(kvst_bf[:], kvst[:],
                                                           AF.Copy),
                          waits=[(s_dve, kvst_done)], inc=s_act)
                    kvb_act = s_act.c
                    em.on("scalar", lambda e, h=h: e.activation(posd[:], c_pos[:],
                                                                AF.Exp, scale=NS(h)),
                          waits=[(s_dve, s_dve.c)], inc=s_act)
                    pd_act = s_act.c
                    em.on("vector", lambda e, h=h: e.tensor_mul(qpd[:],
                                                                q_full[:, h, :],
                                                                posd[:]),
                          waits=[(s_act, pd_act)], inc=s_dve)
                    qpd_done = s_dve.c
                    oadd = {}
                    for m in range(NT):
                        w = [(s_dve, qpd_done), (s_act, kvb_act),
                             (s_dve, po_guard[m % 2])]
                        if m >= 2:
                            w.append((s_dve, oadd[m - 2]))
                        em.on("tensor", lambda e, m=m: e.matmul(
                                  po[m % 2][:], qpd[:, m * 128:(m + 1) * 128],
                                  kvst_bf[:], start=True, stop=True),
                              waits=w, inc=s_pe)
                        c_pe = s_pe.c
                        em.on("vector", lambda e, m=m, h=h: e.tensor_add(
                                  o_tm[:, m, h * 128:(h + 1) * 128],
                                  o_tm[:, m, h * 128:(h + 1) * 128], po[m % 2][:]),
                              waits=[(s_pe, c_pe)], inc=s_dve)
                        oadd[m] = s_dve.c
                        po_guard[m % 2] = s_dve.c
                    kvst_use[h] = s_dve.c
                p2b_dve_end = s_dve.c
                if stage in ("p2o", "p2q"):
                    dcp = sq2
                    dst_done = {}
                    if stage == "p2o":
                        for m in range(NT):
                            em.on("scalar", lambda e, m=m: e.copy(dcp[:],
                                                                  o_tm[:, m, :]),
                                  waits=[(s_dve, p2b_dve_end),
                                         (s_st[0], dst_done.get(m - 1, 0))],
                                  inc=s_act)
                            aa = s_act.c
                            em.on("sync", lambda e, m=m: e.dma_start(out_r[m], dcp[:]),
                                  waits=[(s_act, aa)], inc=s_st[0], inc_n=16)
                            dst_done[m] = s_st[0].c
                    else:
                        out_flat = d_out.rearrange("(a p) z -> p a z", p=128)
                        qf = q_full.ap().rearrange("p h t -> p (h t)")
                        for a in range(8):
                            em.on("scalar", lambda e, a=a: e.copy(
                                      dcp[:], qf[:, a * D:(a + 1) * D]),
                                  waits=[(s_dve, p2b_dve_end),
                                         (s_st[0], dst_done.get(a - 1, 0))],
                                  inc=s_act)
                            aa = s_act.c
                            em.on("sync", lambda e, a=a: e.dma_start(out_flat[:, a, :],
                                                                     dcp[:]),
                                  waits=[(s_act, aa)], inc=s_st[0], inc_n=16)
                            dst_done[a] = s_st[0].c
                    em.wait("sync", [(s_st[0], s_st[0].c)])
                    em.flush()
                    return nc
                em.flush()

            em = Em(nc)
            phase_guard(em)
            p2d_entry = all_dma_totals()
            with contextlib.ExitStack() as p2d:
                uo_fm = p2d.enter_context(nc.sbuf_tensor("uo_fm", [128, KC, T], BF16))
                wout_bf = p2d.enter_context(nc.sbuf_tensor("wout_bf", [128, KC, D],
                                                           BF16))
                sq3 = p2d.enter_context(nc.sbuf_tensor("sq3", [128, D], F32))
                cols3 = p2d.enter_context(nc.sbuf_tensor("cols3", [128, 4], F32))
                xt3 = [p2d.enter_context(nc.sbuf_tensor("xt3s", [128, D], F32))] * 2
                x3t = [p2d.enter_context(nc.sbuf_tensor("x3ts", [128, D], F32))] * 2
                p_tr = [p2d.enter_context(nc.psum_tensor(f"p_trd{i}", [128, 128],
                                                         BF16))
                        for i in range(2)]
                pw = [p2d.enter_context(nc.psum_tensor(f"pw{i}", [128, 512], F32))
                      for i in range(2)]
                C3 = lambda i: cols3[:, i:i + 1]
                em.on("gpsimd", lambda e: e.dma_start(wout_bf[:], wo_r),
                      waits=p2d_entry, inc=s_w, inc_n=16)
                wout_ld = s_w.c
                uo_done = {}
                on_act = {}
                rinv_prev = 0
                for m in range(NT):
                    w = [(s_act, on_act[m - 1])] if m >= 1 else []
                    em.on("scalar", lambda e, m=m: e.activation(sq3[:], o_tm[:, m, :],
                                                                AF.Square,
                                                                accum_out=C3(0)),
                          waits=w + [(s_dve, max(rinv_prev, p2b_dve_end))], inc=s_act)
                    ssq_done = s_act.c
                    em.on("vector", lambda e: e.tensor_scalar(C3(3), C3(0), 1.0 / D,
                                                              EPS_RMS,
                                                              ALU.mult, ALU.add),
                          waits=[(s_act, ssq_done), (s_dve, s_dve.c)], inc=s_dve)
                    ms_done = s_dve.c
                    em.on("scalar", lambda e: e.activation(C3(1), C3(3), AF.Sqrt),
                          waits=[(s_dve, ms_done)], inc=s_act)
                    sd_done = s_act.c
                    em.on("vector", lambda e: e.reciprocal(C3(2), C3(1)),
                          waits=[(s_act, sd_done)], inc=s_dve)
                    rinv_done = s_dve.c
                    rinv_prev = rinv_done
                    em.on("scalar", lambda e, m=m: e.activation(o_tm[:, m, :],
                                                                o_tm[:, m, :],
                                                                AF.Identity,
                                                                scale=C3(2)),
                          waits=[(s_dve, rinv_done)], inc=s_act)
                    on_act[m] = s_act.c
                    cp_done = {}
                    for c in range(KC):
                        w = [(s_act, on_act[m])] if c < 2 else [(s_dve, cp_done[c - 2])]
                        em.on("tensor", lambda e, c=c, m=m: e.transpose(
                                  p_tr[c % 2][:], o_tm[:, m, c * 128:(c + 1) * 128],
                                  c_ident[:]),
                              waits=w, inc=s_pe)
                        tr = s_pe.c
                        em.on("vector", lambda e, c=c, m=m: e.tensor_mul(
                                  uo_fm[:, c, m * 128:(m + 1) * 128], p_tr[c % 2][:],
                                  u_fm[:, c, m * 128:(m + 1) * 128]),
                              waits=[(s_pe, tr)], inc=s_dve)
                        cp_done[c] = s_dve.c
                    uo_done[m] = s_dve.c
                pwg_done = {}
                tile_add = {}
                st_done = {}
                dst = out_r if stage in ("p2", "p2x") else x3_r
                for m in range(NT):
                    w = []
                    if m >= 1:
                        w.append((s_dve, tile_add[m - 1]))
                    em.on("sync", lambda e, m=m: e.dma_start(xt3[m % 2][:], x2_r[m]),
                          waits=w, inc=s_ld[m % 2], inc_n=16)
                    ld = s_ld[m % 2].c
                    for q in range(4):
                        gq = m * 4 + q
                        for k in range(KC):
                            w = []
                            if k == 0:
                                w = [(s_dve, uo_done[m]), (s_w, wout_ld)]
                                if gq >= 2:
                                    w.append((s_dve, pwg_done[gq - 2]))
                            em.on("tensor", lambda e, k=k, m=m, q=q: e.matmul(
                                      pw[q % 2][:], uo_fm[:, k, m * 128:(m + 1) * 128],
                                      wout_bf[:, k, q * 512:(q + 1) * 512],
                                      start=(k == 0), stop=(k == KC - 1)),
                                  waits=w, inc=(s_pe if k == KC - 1 else None))
                        w_pe = s_pe.c
                        ww = [(s_pe, w_pe), (s_ld[m % 2], ld)]
                        if m >= 1:
                            ww.append((s_st[(m - 1) % 2], st_done[m - 1]))
                        em.on("vector", lambda e, m=m, q=q: e.tensor_add(
                                  x3t[m % 2][:, q * 512:(q + 1) * 512], pw[q % 2][:],
                                  xt3[m % 2][:, q * 512:(q + 1) * 512]),
                              waits=ww, inc=s_dve)
                        pwg_done[gq] = s_dve.c
                    tile_add[m] = s_dve.c
                    em.on("sync", lambda e, m=m, dst=dst: e.dma_start(dst[m],
                                                                     x3t[m % 2][:]),
                          waits=[(s_dve, tile_add[m])], inc=s_st[m % 2], inc_n=16)
                    st_done[m] = s_st[m % 2].c
                if stage in ("p2", "p2x"):
                    em.wait("sync", [(s_st[0], s_st[0].c), (s_st[1], s_st[1].c)])
                em.flush()
        if stage in ("p2", "p2x"):
            return nc

        if SPARSE:
            ffn_pass_sparse(x3_r, out_r, final=True, tag="p3")
        else:
            ffn_pass(x3_r, out_r, final=True, tag="p3")
    return nc


_NC_CACHE = {}


def _get_nc(stage):
    if stage not in _NC_CACHE:
        _NC_CACHE[stage] = build_nc(stage)
    return _NC_CACHE[stage]


def _host_constants():
    import ml_dtypes
    arr = np.arange(1, BLK + 1, dtype=np.float32)
    arr_b = np.tile(arr[None, :], (128, 1))
    pos = np.arange(1, T + 1, dtype=np.float32)
    pos_b = np.tile(pos[None, :], (128, 1))
    s_idx = (np.arange(2)[:, None] * 128 + np.arange(128)[None, :])
    m_idx = np.arange(BLK)
    diff = m_idx[None, None, :] - s_idx[:, :, None]
    idxT = np.where(diff >= 0, diff, 1.0e6).astype(np.float32)
    idxT_b = np.ascontiguousarray(idxT.transpose(1, 0, 2))
    krev = np.ascontiguousarray((BLK - (s_idx + 1)).astype(np.float32).T)
    c256 = np.full((128, 1), float(BLK), dtype=np.float32)
    ident = np.eye(128).astype(ml_dtypes.bfloat16)
    return arr_b, pos_b, idxT_b, krev, c256, ident


def _host_sparse_consts():
    ltri = np.tril(np.ones((128, 128), np.float32), k=-1)
    erange_b = np.tile(np.arange(NE, dtype=np.float32)[None, :], (128, 1))
    p = np.arange(128, dtype=np.float32)
    g128p_b = np.stack([g * 128 + p for g in range(NG)], axis=1).astype(np.float32)
    tok = (np.arange(NT)[None, :] * 128 + np.arange(128)[:, None]).astype(np.int64)
    tokhi_b = (tok // 256).astype(np.float32)
    toklo_b = (tok % 256).astype(np.float32)
    ones2_b = np.zeros((128, 2), np.float32)
    ones2_b[:, 0] = 1.0
    onesrow_b = np.ones((1, 128), np.float32)
    ecap_b = (np.arange(NE, dtype=np.float32) * CAP).reshape(1, NE)
    return {"ltri": np.ascontiguousarray(ltri),
            "erange_b": np.ascontiguousarray(erange_b),
            "g128p_b": np.ascontiguousarray(g128p_b),
            "tokhi_b": np.ascontiguousarray(tokhi_b),
            "toklo_b": np.ascontiguousarray(toklo_b),
            "ones2_b": ones2_b, "onesrow_b": onesrow_b, "ecap_b": ecap_b}


def kernel(L0_attn_x, L0_attn_slope_rate, L0_attn_W_qkvu, L0_attn_W_out,
           L0_ffn_router_w, L0_ffn_w_gate, L0_ffn_w_up, L0_ffn_w_down,
           _stage="full"):
    x = np.ascontiguousarray(
        np.asarray(L0_attn_x, dtype=np.float32).reshape(2 * 4096, D))
    slope = np.asarray(L0_attn_slope_rate, dtype=np.float32).reshape(NH)
    wqkvuT = np.ascontiguousarray(np.asarray(L0_attn_W_qkvu, dtype=np.float32).T)
    woutT = np.ascontiguousarray(np.asarray(L0_attn_W_out, dtype=np.float32).T)
    rT = np.ascontiguousarray(np.asarray(L0_ffn_router_w, dtype=np.float32).T)
    wg = np.ascontiguousarray(np.asarray(L0_ffn_w_gate, dtype=np.float32))
    wu = np.ascontiguousarray(np.asarray(L0_ffn_w_up, dtype=np.float32))
    wd = np.ascontiguousarray(np.asarray(L0_ffn_w_down, dtype=np.float32))

    arr_b, pos_b, idxT_b, krev, c256, ident = _host_constants()
    slope_bc = np.ascontiguousarray(np.tile(slope[None, :], (128, 1)))

    in_maps = []
    for c in range(NCORES):
        base = (c // 4) * 4
        delta = np.zeros(NCORES, dtype=np.float32)
        maskg = np.zeros(NCORES, dtype=np.float32)
        for j in range(NCORES):
            if base <= j < c:
                maskg[j] = 1.0
                delta[j] = 1024.0 * (c - 1 - j)
        in_maps.append({
            "x_in": np.ascontiguousarray(x[c * T:(c + 1) * T]),
            "wqkvuT": wqkvuT,
            "woutT": woutT,
            "rT": rT,
            "w_gate": wg,
            "w_up": wu,
            "w_down": wd,
            "slope_bc": slope_bc,
            "arr_b": arr_b,
            "pos_b": pos_b,
            "idxT_b": idxT_b,
            "krev_b": krev,
            "c256_b": c256,
            "delta_b": np.ascontiguousarray(np.tile(delta[None, :], (128, 1))),
            "maskg_b": np.ascontiguousarray(np.tile(maskg[None, :], (128, 1))),
            "ident": ident,
            **_host_sparse_consts(),
        })

    nc = _get_nc(_stage)
    res = run_bass_kernel_spmd(nc, in_maps, core_ids=list(range(NCORES)))
    out = np.concatenate([np.asarray(res.results[c]["out"])
                          for c in range(NCORES)], axis=0)
    return out.reshape(2, 4096, D).astype(np.float32)


if __name__ == "__main__":
    import sys as _sys
    stg = _sys.argv[1] if len(_sys.argv) > 1 else "p1"
    nc = build_nc(stg)
    print("built", stg, "ok")


# version 11
# speedup vs baseline: 1.4565x; 1.0009x over previous
import numpy as np
import contextlib
import concourse.bass as bass
import concourse.mybir as mybir
from concourse.bass_utils import run_bass_kernel_spmd

F32 = mybir.dt.float32
BF16 = mybir.dt.bfloat16
AF = mybir.ActivationFunctionType
ALU = mybir.AluOpType
AX = mybir.AxisListType

NCORES = 8
T = 1024
NT = 8
D = 2048
KC = 16
NE = 32
I_FF = 768
IC = 6
NH = 16
HD = 128
BLK = 256
NB = 4
EPS_LN = 1e-5
EPS_RMS = 1e-6
SPARSE = True
CAP = 384
RW = 2080
NG = CAP // 128


class S:
    __slots__ = ("h", "c")

    def __init__(self, h):
        self.h = h
        self.c = 0


class Em:

    ENGS = ("tensor", "vector", "scalar", "gpsimd", "sync")

    def __init__(self, nc):
        self.nc = nc
        self.items = {k: [] for k in self.ENGS}

    def on(self, eng, fn, waits=(), inc=None, inc_n=1):
        ws = tuple((s.h, int(v)) for (s, v) in waits if v > 0)
        pair = None
        if inc is not None:
            inc.c += inc_n
            pair = (inc.h, inc_n)

        def run(e, fn=fn, ws=ws, pair=pair):
            for h, v in ws:
                e.wait_ge(h, v)
            r = fn(e)
            if pair is not None:
                r.then_inc(*pair)
        self.items[eng].append(run)

    def wait(self, eng, waits):
        ws = tuple((s.h, int(v)) for (s, v) in waits if v > 0)

        def run(e, ws=ws):
            for h, v in ws:
                e.wait_ge(h, v)
        self.items[eng].append(run)

    def flush(self):
        with self.nc.Block() as b:
            for eng in self.ENGS:
                lst = self.items[eng]
                if lst:
                    getattr(b, eng)(lambda e, lst=list(lst): [f(e) for f in lst])
        self.items = {k: [] for k in self.ENGS}


def build_nc(stage="full"):
    nc = bass.Bass("TRN2", target_bir_lowering=False)

    d_x = nc.dram_tensor("x_in", [T, D], F32, kind="ExternalInput").ap()
    d_wqkvuT = nc.dram_tensor("wqkvuT", [D, 4 * D], F32, kind="ExternalInput").ap()
    d_woutT = nc.dram_tensor("woutT", [D, D], F32, kind="ExternalInput").ap()
    d_rT = nc.dram_tensor("rT", [D, NE], F32, kind="ExternalInput").ap()
    d_wg = nc.dram_tensor("w_gate", [NE, D, I_FF], F32, kind="ExternalInput").ap()
    d_wu = nc.dram_tensor("w_up", [NE, D, I_FF], F32, kind="ExternalInput").ap()
    d_wd = nc.dram_tensor("w_down", [NE, I_FF, D], F32, kind="ExternalInput").ap()
    d_slope = nc.dram_tensor("slope_bc", [128, NH], F32, kind="ExternalInput").ap()
    d_arr = nc.dram_tensor("arr_b", [128, BLK], F32, kind="ExternalInput").ap()
    d_pos = nc.dram_tensor("pos_b", [128, T], F32, kind="ExternalInput").ap()
    d_idxT = nc.dram_tensor("idxT_b", [128, 2, BLK], F32, kind="ExternalInput").ap()
    d_krev = nc.dram_tensor("krev_b", [128, 2], F32, kind="ExternalInput").ap()
    d_c256 = nc.dram_tensor("c256_b", [128, 1], F32, kind="ExternalInput").ap()
    d_delta = nc.dram_tensor("delta_b", [128, NCORES], F32, kind="ExternalInput").ap()
    d_maskg = nc.dram_tensor("maskg_b", [128, NCORES], F32, kind="ExternalInput").ap()
    d_ident = nc.dram_tensor("ident", [128, 128], BF16, kind="ExternalInput").ap()

    d_out = nc.dram_tensor("out", [T, D], F32, kind="ExternalOutput").ap()

    d_x2s = nc.dram_tensor("x2s", [T, D], F32, kind="Internal").ap()
    d_x3s = nc.dram_tensor("x3s", [T, D], F32, kind="Internal").ap()
    d_eg = nc.dram_tensor("eg", [NE * CAP, RW], BF16, kind="Internal").ap()
    d_eo = nc.dram_tensor("eo", [NE * CAP, D], BF16, kind="Internal").ap()
    d_ltri = nc.dram_tensor("ltri", [128, 128], F32, kind="ExternalInput").ap()
    d_erange = nc.dram_tensor("erange_b", [128, NE], F32, kind="ExternalInput").ap()
    d_g128p = nc.dram_tensor("g128p_b", [128, NG], F32, kind="ExternalInput").ap()
    d_tokhi = nc.dram_tensor("tokhi_b", [128, NT], F32, kind="ExternalInput").ap()
    d_toklo = nc.dram_tensor("toklo_b", [128, NT], F32, kind="ExternalInput").ap()
    d_ones2 = nc.dram_tensor("ones2_b", [128, 2], F32, kind="ExternalInput").ap()
    d_onesrow = nc.dram_tensor("onesrow_b", [1, 128], F32, kind="ExternalInput").ap()
    d_ecap = nc.dram_tensor("ecap_b", [1, NE], F32, kind="ExternalInput").ap()
    d_ccin = nc.dram_tensor("ccin", [NH * HD, HD], F32, kind="Internal").ap()
    d_ccout = nc.dram_tensor("ccout", [NCORES * NH * HD, HD], F32, kind="Internal").ap()

    x_r = d_x.rearrange("(m p) d -> m p d", p=128)
    x2_r = d_x2s.rearrange("(m p) d -> m p d", p=128)
    x3_r = d_x3s.rearrange("(m p) d -> m p d", p=128)
    out_r = d_out.rearrange("(m p) d -> m p d", p=128)
    wg_r = d_wg.rearrange("e (k p) j -> e p k j", p=128)
    wu_r = d_wu.rearrange("e (k p) j -> e p k j", p=128)
    wd_r = d_wd.rearrange("e (k p) j -> e p k j", p=128)
    wq_r = d_wqkvuT.rearrange("(k p) j -> p k j", p=128)
    wo_r = d_woutT.rearrange("(k p) j -> p k j", p=128)
    rT_r = d_rT.rearrange("(k p) j -> p k j", p=128)
    eo_r = d_eo.rearrange("(r p) d -> r p d", p=128)
    ccin_r = d_ccin.rearrange("(h p) e -> p h e", p=128)
    ccout_r = d_ccout.rearrange("(j h p) e -> p j h e", j=NCORES, h=NH, p=128)

    st = contextlib.ExitStack()
    with st:
        def sem(name):
            return S(st.enter_context(nc.semaphore(name)))

        s_x = sem("s_x")
        s_w = sem("s_w")
        s_ld = [sem(f"s_ld{i}") for i in range(2)]
        s_st = [sem(f"s_st{i}") for i in range(2)]
        s_wg = sem("s_wg")
        s_wu = sem("s_wu")
        s_wd = sem("s_wd")
        s_pb = [sem(f"s_pb{i}") for i in range(4)]
        s_sc = [sem(f"s_sc{i}") for i in range(2)]
        s_eg = [sem(f"s_eg{i}") for i in range(2)]
        s_eo = [sem(f"s_eo{i}") for i in range(2)]
        s_gb = [sem(f"s_gb{i}") for i in range(4)]
        s_g = [sem(f"s_g{i}") for i in range(2)]
        s_pe = sem("s_pe")
        s_act = sem("s_act")
        s_pl = sem("s_pl")
        s_dve = sem("s_dve")
        s_cc = sem("s_cc")
        DMA_SEMS = ([s_x, s_w] + s_ld + s_st + [s_wg, s_wu, s_wd] + s_pb + s_g +
                    s_sc + s_eo + s_eg + s_gb)

        def all_dma_totals():
            return [(s, s.c) for s in DMA_SEMS]

        def phase_guard(em):
            tots = all_dma_totals()
            for _eng in Em.ENGS:
                em.wait(_eng, tots)

        c_ident = st.enter_context(nc.sbuf_tensor("c_ident", [128, 128], BF16))
        c_slope = st.enter_context(nc.sbuf_tensor("c_slope", [128, NH], F32))
        c_negsl = st.enter_context(nc.sbuf_tensor("c_negsl", [128, NH], F32))
        c_delta = st.enter_context(nc.sbuf_tensor("c_delta", [128, NCORES], F32))
        c_maskg = st.enter_context(nc.sbuf_tensor("c_maskg", [128, NCORES], F32))
        c_c256 = st.enter_context(nc.sbuf_tensor("c_c256", [128, 1], F32))
        rT_bf = st.enter_context(nc.sbuf_tensor("rT_bf", [128, KC, NE], BF16))
        c_ltri = st.enter_context(nc.sbuf_tensor("c_ltri", [128, 128], F32))
        c_erange = st.enter_context(nc.sbuf_tensor("c_erange", [128, NE], F32))
        c_g128p = st.enter_context(nc.sbuf_tensor("c_g128p", [128, NG], F32))
        c_tokhi = st.enter_context(nc.sbuf_tensor("c_tokhi", [128, NT], F32))
        c_toklo = st.enter_context(nc.sbuf_tensor("c_toklo", [128, NT], F32))
        c_ones2 = st.enter_context(nc.sbuf_tensor("c_ones2", [128, 2], F32))
        c_onesrow = st.enter_context(nc.sbuf_tensor("c_onesrow", [1, 128], F32))

        NS = lambda h: c_negsl[:, h:h + 1]

        em = Em(nc)
        em.on("sync", lambda e: e.dma_start(c_ident[:], d_ident), inc=s_x, inc_n=16)
        em.on("sync", lambda e: e.dma_start(c_slope[:], d_slope), inc=s_x, inc_n=16)
        em.on("sync", lambda e: e.dma_start(c_delta[:], d_delta), inc=s_x, inc_n=16)
        em.on("sync", lambda e: e.dma_start(c_maskg[:], d_maskg), inc=s_x, inc_n=16)
        em.on("sync", lambda e: e.dma_start(c_c256[:], d_c256), inc=s_x, inc_n=16)
        em.on("gpsimd", lambda e: e.dma_start(rT_bf[:], rT_r), inc=s_w, inc_n=16)
        em.on("sync", lambda e: e.dma_start(c_ltri[:], d_ltri), inc=s_x, inc_n=16)
        em.on("sync", lambda e: e.dma_start(c_erange[:], d_erange), inc=s_x, inc_n=16)
        em.on("sync", lambda e: e.dma_start(c_g128p[:], d_g128p), inc=s_x, inc_n=16)
        em.on("sync", lambda e: e.dma_start(c_tokhi[:], d_tokhi), inc=s_x, inc_n=16)
        em.on("sync", lambda e: e.dma_start(c_toklo[:], d_toklo), inc=s_x, inc_n=16)
        em.on("sync", lambda e: e.dma_start(c_ones2[:], d_ones2), inc=s_x, inc_n=16)
        em.on("sync", lambda e: e.dma_start(c_onesrow[:], d_onesrow), inc=s_x, inc_n=16)
        em.on("vector", lambda e: e.tensor_scalar(c_negsl[:], c_slope[:], -1.0, None,
                                                  ALU.mult),
              waits=[(s_x, s_x.c)], inc=s_dve)
        em.flush()


        def emit_ln(em, src_r, xn_fm, bufs, router_wr=None, entry_dma=(),
                    entry_sdve=0, dbg_dst_r=None, sp=None):
            (xt, sqdump, xn_tm, cols, lg, top8, eall, maskb, p_tr, p_rt) = bufs
            C = lambda i: cols[:, i:i + 1]
            act_affine = {}
            pe_tr_done = {}
            dbg_st = {}
            dbg_cp = {}
            dve_last = {-1: entry_sdve}
            for m in range(NT):
                if m == 0:
                    w = list(entry_dma)
                elif m >= 2:
                    w = [(s_act, act_affine[m - 2]), (s_st[m % 2], dbg_st.get(m - 2, 0))]
                else:
                    w = []
                em.on("sync", lambda e, m=m: e.dma_start(xt[m % 2][:], src_r[m]),
                      waits=w, inc=s_ld[m % 2], inc_n=16)
                ldx = s_ld[m % 2].c
                em.on("vector", lambda e, m=m: e.reduce_sum(C(0), xt[m % 2][:], axis=AX.X),
                      waits=[(s_ld[m % 2], ldx)] +
                            ([(s_act, act_affine[m - 1])] if m >= 1 else []),
                      inc=s_dve)
                sum_done = s_dve.c
                em.on("scalar", lambda e, m=m: e.activation(sqdump, xt[m % 2][:], AF.Square,
                                                            accum_out=C(7)),
                      waits=[(s_ld[m % 2], ldx), (s_dve, dve_last[m - 1])],
                      inc=s_act)
                ssq_done = s_act.c
                em.on("vector", lambda e: e.tensor_scalar(C(1), C(0), 1.0 / D, None,
                                                          ALU.mult),
                      waits=[(s_dve, sum_done)], inc=s_dve)
                mean_done = s_dve.c
                em.on("vector", lambda e: e.tensor_mul(C(2), C(1), C(1)),
                      waits=[(s_dve, mean_done)], inc=s_dve)
                m2_done = s_dve.c
                em.on("vector", lambda e: e.tensor_scalar(C(3), C(7), 1.0 / D, C(2),
                                                          ALU.mult, ALU.subtract),
                      waits=[(s_act, ssq_done), (s_dve, m2_done)], inc=s_dve)
                em.on("vector", lambda e: e.tensor_scalar(C(3), C(3), EPS_LN, None,
                                                          ALU.add),
                      waits=[(s_dve, s_dve.c)], inc=s_dve)
                vr_done = s_dve.c
                em.on("scalar", lambda e: e.activation(C(4), C(3), AF.Sqrt),
                      waits=[(s_dve, vr_done)], inc=s_act)
                sd_done = s_act.c
                em.on("vector", lambda e: e.reciprocal(C(5), C(4)),
                      waits=[(s_act, sd_done)], inc=s_dve)
                rinv_done = s_dve.c
                em.on("vector", lambda e: e.tensor_scalar(C(6), C(1), C(5), -1.0,
                                                          ALU.mult, ALU.mult),
                      waits=[(s_dve, rinv_done)], inc=s_dve)
                nb_done = s_dve.c
                w = [(s_dve, nb_done)]
                if m >= 1:
                    w.append((s_pe, pe_tr_done[m - 1]))
                    w.append((s_dve, dbg_cp.get(m - 1, 0)))
                if sp is not None:
                    xn_dst = sp["xns"][m % 2][:, 0:D]
                    w.append((s_sc[m % 2], sp["sc_cnt"][m % 2]))
                else:
                    xn_dst = xn_tm[:]
                em.on("scalar", lambda e, m=m, xn_dst=xn_dst: e.activation(
                          xn_dst, xt[m % 2][:], AF.Identity,
                          bias=C(6), scale=C(5)),
                      waits=w, inc=s_act)
                act_affine[m] = s_act.c
                cp_done = []
                cp_sem = []
                for c in range(KC):
                    semx = s_act if c % 2 == 0 else s_dve
                    eng = "scalar" if c % 2 == 0 else "vector"
                    w = [(s_act, act_affine[m])] if c < 2 else \
                        [(cp_sem[c - 2], cp_done[c - 2])]
                    tr_src = (sp["xns"][m % 2] if sp is not None else xn_tm)
                    em.on("tensor",
                          lambda e, c=c, tr_src=tr_src: e.transpose(
                              p_tr[c % 2][:],
                              tr_src[:, c * 128:(c + 1) * 128],
                              c_ident[:]),
                          waits=w, inc=s_pe)
                    tr = s_pe.c
                    if eng == "scalar":
                        em.on(eng, lambda e, c=c, m=m: e.copy(
                                  xn_fm[:, c, m * 128:(m + 1) * 128], p_tr[c % 2][:]),
                              waits=[(s_pe, tr)], inc=semx)
                    else:
                        em.on(eng, lambda e, c=c, m=m: e.tensor_copy(
                                  xn_fm[:, c, m * 128:(m + 1) * 128], p_tr[c % 2][:]),
                              waits=[(s_pe, tr)], inc=semx)
                    cp_done.append(semx.c)
                    cp_sem.append(semx)
                pe_tr_done[m] = s_pe.c
                xn_m_act, xn_m_dve = s_act.c, s_dve.c
                if dbg_dst_r is not None:
                    em.on("vector", lambda e, m=m: e.tensor_copy(xt[m % 2][:], xn_tm[:]),
                          waits=[(s_act, act_affine[m])], inc=s_dve)
                    dbg_cp[m] = s_dve.c
                    em.on("sync", lambda e, m=m: e.dma_start(dbg_dst_r[m], xt[m % 2][:]),
                          waits=[(s_dve, dbg_cp[m])], inc=s_st[m % 2], inc_n=16)
                    dbg_st[m] = s_st[m % 2].c
                if router_wr is not None:
                    for k in range(KC):
                        w = []
                        if k == 0:
                            w = [(s_act, xn_m_act), (s_dve, xn_m_dve), (s_w, 16),
                                 (s_dve, dve_last[m - 1])]
                        em.on("tensor", lambda e, k=k, m=m: e.matmul(
                                  p_rt[:], xn_fm[:, k, m * 128:(m + 1) * 128],
                                  rT_bf[:, k, :], start=(k == 0), stop=(k == KC - 1)),
                              waits=w, inc=(s_pe if k == KC - 1 else None))
                    rt_done = s_pe.c
                    em.on("vector", lambda e: e.tensor_copy(lg[:], p_rt[:]),
                          waits=[(s_pe, rt_done)], inc=s_dve)
                    em.on("vector", lambda e: e.max(top8[:], lg[:]),
                          waits=[(s_dve, s_dve.c)], inc=s_dve)
                    t8_done = s_dve.c
                    em.on("vector", lambda e: e.tensor_scalar(C(8), top8[:, 0:1], -1.0,
                                                              None, ALU.mult),
                          waits=[(s_dve, t8_done)], inc=s_dve)
                    nm0_done = s_dve.c
                    if sp is None:
                        em.on("scalar", lambda e: e.activation(eall[:], lg[:], AF.Exp,
                                                               bias=C(8)),
                              waits=[(s_dve, nm0_done)], inc=s_act)
                        eall_done = s_act.c
                        em.on("vector", lambda e: e.tensor_scalar(maskb[:], lg[:],
                                                                  top8[:, 7:8],
                                                                  None, ALU.is_ge),
                              waits=[(s_dve, t8_done)], inc=s_dve)
                        em.on("vector", lambda e: e.tensor_mul(eall[:], eall[:],
                                                               maskb[:]),
                              waits=[(s_act, eall_done), (s_dve, s_dve.c)],
                              inc=s_dve)
                        em.on("vector", lambda e: e.reduce_sum(C(9), eall[:],
                                                               axis=AX.X),
                              waits=[(s_dve, s_dve.c)], inc=s_dve)
                        em.on("vector", lambda e: e.reciprocal(C(10), C(9)),
                              waits=[(s_dve, s_dve.c)], inc=s_dve)
                        sinv_done = s_dve.c
                        em.on("vector", lambda e, m=m: e.tensor_scalar(
                                  router_wr[:, m, :], eall[:], C(10), 0.5,
                                  ALU.mult, ALU.mult),
                              waits=[(s_dve, sinv_done)], inc=s_dve)
                    else:
                        e8 = sp["e8"]; w8 = sp["w8"]; tidx = sp["tidx"]
                        tidxf = sp["tidxf"]; posS = sp["posS"]
                        poseS = sp["poseS"]; base = sp["base"]
                        slaf = sp["slaf"]; slau = sp["slau"]
                        p_pos = sp["p_pos"]; p_bb = sp["p_bb"]; p_cnt = sp["p_cnt"]
                        xns = sp["xns"][m % 2]
                        em.on("scalar", lambda e: e.activation(e8[:], top8[:], AF.Exp,
                                                               bias=C(8)),
                              waits=[(s_dve, nm0_done)], inc=s_act)
                        e8_done = s_act.c
                        em.on("vector", lambda e: e.reduce_sum(C(9), e8[:], axis=AX.X),
                              waits=[(s_act, e8_done)], inc=s_dve)
                        em.on("vector", lambda e: e.reciprocal(C(10), C(9)),
                              waits=[(s_dve, s_dve.c)], inc=s_dve)
                        sinv_done = s_dve.c
                        em.on("vector", lambda e: e.tensor_scalar(w8[:], e8[:], C(10),
                                                                  0.5, ALU.mult,
                                                                  ALU.mult),
                              waits=[(s_dve, sinv_done)], inc=s_dve)
                        w8_done = s_dve.c
                        em.on("vector", lambda e: e.tensor_scalar(maskb[:], lg[:],
                                                                  top8[:, 7:8],
                                                                  None, ALU.is_ge),
                              waits=[(s_dve, t8_done)], inc=s_dve)
                        mask_done = s_dve.c
                        em.on("vector", lambda e: e.max_index(tidx[:], top8[:], lg[:]),
                              waits=[(s_dve, mask_done)], inc=s_dve)
                        em.on("vector", lambda e: e.tensor_copy(tidxf[:], tidx[:]),
                              waits=[(s_dve, s_dve.c)], inc=s_dve)
                        tif_done = s_dve.c
                        em.on("tensor", lambda e: e.matmul(p_pos[:], c_ltri[:],
                                                           maskb[:], start=True,
                                                           stop=True),
                              waits=[(s_dve, mask_done),
                                     (s_dve, sp["posS_free"])], inc=s_pe)
                        em.on("tensor", lambda e: e.matmul(p_bb[:], c_onesrow[:],
                                                           base[:], start=True,
                                                           stop=True),
                              waits=[(s_dve, sp["base_done"])], inc=s_pe)
                        pos_pe = s_pe.c
                        em.on("vector", lambda e: e.tensor_copy(posS[:], p_pos[:]),
                              waits=[(s_pe, pos_pe)], inc=s_dve)
                        em.on("vector", lambda e: e.tensor_add(poseS[:], posS[:],
                                                               p_bb[:]),
                              waits=[(s_dve, s_dve.c)], inc=s_dve)
                        pose_done = s_dve.c
                        sp["posS_free"] = pose_done
                        em.on("tensor", lambda e: e.matmul(p_cnt[:],
                                                           c_ones2[:, 0:1],
                                                           maskb[:], start=True,
                                                           stop=True),
                              waits=[(s_dve, pose_done)], inc=s_pe)
                        cnt_pe = s_pe.c
                        em.on("vector", lambda e: e.tensor_add(base[:], base[:],
                                                               p_cnt[:]),
                              waits=[(s_pe, cnt_pe), (s_dve, s_dve.c)], inc=s_dve)
                        sp["base_done"] = s_dve.c
                        for k in range(8):
                            em.on("vector", lambda e, k=k: e.tensor_scalar(
                                      eall[:], c_erange[:], tidxf[:, k:k + 1],
                                      None, ALU.is_equal),
                                  waits=[(s_dve, tif_done if k == 0 else s_dve.c),
                                         (s_dve, pose_done)], inc=s_dve)
                            em.on("vector", lambda e: e.tensor_mul(eall[:], eall[:],
                                                                   poseS[:]),
                                  waits=[(s_dve, s_dve.c)], inc=s_dve)
                            em.on("vector", lambda e, k=k: e.reduce_sum(
                                      slaf[:, k:k + 1], eall[:], axis=AX.X),
                                  waits=[(s_dve, s_dve.c)], inc=s_dve)
                        em.on("vector", lambda e: e.tensor_scalar(slaf[:], slaf[:],
                                                                  float(NE * CAP - 1),
                                                                  None, ALU.min),
                              waits=[(s_dve, s_dve.c)], inc=s_dve)
                        em.on("vector", lambda e, m=m: e.tensor_copy(
                                  slau[:, m, :], slaf[:]),
                              waits=[(s_dve, s_dve.c)], inc=s_dve)
                        slau_done = s_dve.c
                        em.on("vector", lambda e, xns=xns: e.tensor_copy(
                                  xns[:, D:D + 8], tidxf[:]),
                              waits=[(s_dve, tif_done),
                                     (s_sc[m % 2], sp["sc_cnt"][m % 2])],
                              inc=s_dve)
                        em.on("vector", lambda e, xns=xns: e.tensor_copy(
                                  xns[:, D + 8:D + 16], w8[:]),
                              waits=[(s_dve, w8_done)], inc=s_dve)
                        em.on("vector", lambda e, m=m, xns=xns: e.tensor_copy(
                                  xns[:, D + 16:D + 17], c_tokhi[:, m:m + 1]),
                              inc=s_dve)
                        em.on("vector", lambda e, m=m, xns=xns: e.tensor_copy(
                                  xns[:, D + 17:D + 18], c_toklo[:, m:m + 1]),
                              inc=s_dve)
                        meta_done = s_dve.c
                        for k in range(8):
                            w = [(s_dve, meta_done), (s_dve, slau_done),
                                 (s_act, act_affine[m])]
                            if m == 0 and k == 0:
                                w += list(entry_dma)
                            em.on("gpsimd", lambda e, m=m, k=k, xns=xns:
                                  e.indirect_dma_start(
                                      d_eg,
                                      bass.IndirectOffsetOnAxis(
                                          slau[:, m, k:k + 1], 0),
                                      xns[:, 0:D + 18], None),
                                  waits=w, inc=s_sc[m % 2], inc_n=16)
                        sp["sc_cnt"][m % 2] = s_sc[m % 2].c
                dve_last[m] = s_dve.c
            return s_act.c, s_dve.c

        def emit_experts(em, xn_fm, router_wr, acc, wbufs, ghbufs, psums, ready,
                         entry_dma):
            wg_bf, wu_bf, wd_bf = wbufs
            g_s, h_bf = ghbufs
            pg, pu, pd = psums
            rdy_act, rdy_dve = ready
            pe_gate_end = pe_up_end = pe_down_end = None
            silu_cnt = []
            upev_cnt = []
            dnev_cnt = []
            for e_i in range(NE):
                em.on("gpsimd", lambda e, e_i=e_i: e.dma_start(wg_bf[:], wg_r[e_i]),
                      waits=([(s_pe, pe_gate_end)] if pe_gate_end is not None
                             else list(entry_dma)),
                      inc=s_wg, inc_n=16)
                wg_ld = s_wg.c
                em.on("gpsimd", lambda e, e_i=e_i: e.dma_start(wu_bf[:], wu_r[e_i]),
                      waits=([(s_pe, pe_up_end)] if pe_up_end is not None else []),
                      inc=s_wu, inc_n=16)
                wu_ld = s_wu.c
                em.on("gpsimd", lambda e, e_i=e_i: e.dma_start(wd_bf[:], wd_r[e_i]),
                      waits=([(s_pe, pe_down_end)] if pe_down_end is not None else []),
                      inc=s_wd, inc_n=16)
                wd_ld = s_wd.c
                gidx0 = len(silu_cnt)
                for th in range(2):
                    for c in range(IC):
                        g = len(silu_cnt)
                        for k in range(KC):
                            w = []
                            if k == 0:
                                w = [(s_wg, wg_ld)]
                                if g >= 2:
                                    w.append((s_act, silu_cnt[g - 2]))
                                if e_i == 0 and g == 0:
                                    w += [(s_act, rdy_act), (s_dve, rdy_dve)]
                            em.on("tensor", lambda e, k=k, c=c, th=th, g=g: e.matmul(
                                      pg[g % 2][:], wg_bf[:, k, c * 128:(c + 1) * 128],
                                      xn_fm[:, k, th * 512:(th + 1) * 512],
                                      start=(k == 0), stop=(k == KC - 1)),
                                  waits=w, inc=(s_pe if k == KC - 1 else None))
                        gate_pe = s_pe.c
                        em.on("scalar", lambda e, c=c, th=th, g=g: e.activation(
                                  g_s[:, c, th * 512:(th + 1) * 512], pg[g % 2][:],
                                  AF.Silu),
                              waits=[(s_pe, gate_pe)], inc=s_act)
                        silu_cnt.append(s_act.c)
                pe_gate_end = s_pe.c
                uidx0 = len(upev_cnt)
                for th in range(2):
                    for c in range(IC):
                        g = len(upev_cnt)
                        for k in range(KC):
                            w = []
                            if k == 0:
                                w = [(s_wu, wu_ld)]
                                if g >= 2:
                                    w.append((s_dve, upev_cnt[g - 2]))
                            em.on("tensor", lambda e, k=k, c=c, th=th, g=g: e.matmul(
                                      pu[g % 2][:], wu_bf[:, k, c * 128:(c + 1) * 128],
                                      xn_fm[:, k, th * 512:(th + 1) * 512],
                                      start=(k == 0), stop=(k == KC - 1)),
                                  waits=w, inc=(s_pe if k == KC - 1 else None))
                        up_pe = s_pe.c
                        em.on("vector", lambda e, c=c, th=th, g=g: e.tensor_mul(
                                  h_bf[:, c, th * 512:(th + 1) * 512], pu[g % 2][:],
                                  g_s[:, c, th * 512:(th + 1) * 512]),
                              waits=[(s_pe, up_pe),
                                     (s_act, silu_cnt[gidx0 + th * IC + c])],
                              inc=s_dve)
                        upev_cnt.append(s_dve.c)
                pe_up_end = s_pe.c
                for th in range(2):
                    h_rdy = upev_cnt[uidx0 + th * IC + IC - 1]
                    for mm in range(4):
                        m = th * 4 + mm
                        for q in range(4):
                            g = len(dnev_cnt)
                            for k in range(IC):
                                w = []
                                if k == 0:
                                    w = [(s_wd, wd_ld), (s_dve, h_rdy)]
                                    if g >= 2:
                                        w.append((s_dve, dnev_cnt[g - 2]))
                                em.on("tensor", lambda e, k=k, m=m, q=q, g=g: e.matmul(
                                          pd[g % 2][:],
                                          h_bf[:, k, m * 128:(m + 1) * 128],
                                          wd_bf[:, k, q * 512:(q + 1) * 512],
                                          start=(k == 0), stop=(k == IC - 1)),
                                      waits=w, inc=(s_pe if k == IC - 1 else None))
                            dn_pe = s_pe.c
                            if e_i == 0:
                                em.on("vector", lambda e, m=m, q=q, g=g: e.tensor_scalar(
                                          acc[:, m, q * 512:(q + 1) * 512], pd[g % 2][:],
                                          router_wr[:, m, 0:1], None, ALU.mult),
                                      waits=[(s_pe, dn_pe), (s_dve, rdy_dve)], inc=s_dve)
                            else:
                                em.on("vector", lambda e, m=m, q=q, g=g, e_i=e_i:
                                      e.scalar_tensor_tensor(
                                          acc[:, m, q * 512:(q + 1) * 512], pd[g % 2][:],
                                          router_wr[:, m, e_i:e_i + 1],
                                          acc[:, m, q * 512:(q + 1) * 512],
                                          ALU.mult, ALU.add),
                                      waits=[(s_pe, dn_pe)], inc=s_dve)
                            dnev_cnt.append(s_dve.c)
                pe_down_end = s_pe.c
            return s_dve.c

        def emit_resid(em, src_r, acc, dst_r, rings, experts_done, final=False):
            xt2, x2t = rings
            add_done = {}
            st_done = {}
            for m in range(NT):
                w = [(s_dve, experts_done)] if m == 0 else []
                if m >= 2:
                    w.append((s_dve, add_done[m - 2]))
                em.on("sync", lambda e, m=m: e.dma_start(xt2[m % 2][:], src_r[m]),
                      waits=w, inc=s_ld[m % 2], inc_n=16)
                ldx = s_ld[m % 2].c
                w = [(s_ld[m % 2], ldx)]
                if m == 0:
                    w.append((s_dve, experts_done))
                if m >= 2:
                    w.append((s_st[m % 2], st_done[m - 2]))
                em.on("vector", lambda e, m=m: e.tensor_add(x2t[m % 2][:], xt2[m % 2][:],
                                                            acc[:, m, :]),
                      waits=w, inc=s_dve)
                add_done[m] = s_dve.c
                em.on("sync", lambda e, m=m: e.dma_start(dst_r[m], x2t[m % 2][:]),
                      waits=[(s_dve, add_done[m])], inc=s_st[m % 2], inc_n=16)
                st_done[m] = s_st[m % 2].c
            if final:
                em.wait("sync", [(s_st[0], s_st[0].c), (s_st[1], s_st[1].c)])

        def ffn_pass(src_r, dst_r, final=False, tag="a"):
            entry_dma = all_dma_totals()
            entry_sdve = s_dve.c
            with contextlib.ExitStack() as ps:
                em = Em(nc)
                phase_guard(em)
                xn_fm = ps.enter_context(nc.sbuf_tensor("xn_fm_" + tag,
                                                        [128, KC, T], BF16))
                W_r = ps.enter_context(nc.sbuf_tensor("W_r_" + tag,
                                                      [128, NT, NE], F32))
                wg_bf = ps.enter_context(nc.sbuf_tensor("wg_bf_" + tag,
                                                        [128, KC, I_FF], BF16))
                wu_bf = ps.enter_context(nc.sbuf_tensor("wu_bf_" + tag,
                                                        [128, KC, I_FF], BF16))
                wd_bf = ps.enter_context(nc.sbuf_tensor("wd_bf_" + tag,
                                                        [128, IC, D], BF16))
                acc = ps.enter_context(nc.sbuf_tensor("acc_" + tag,
                                                      [128, NT, D], F32))
                cols = ps.enter_context(nc.sbuf_tensor("cols_" + tag, [128, 16], F32))
                lg = ps.enter_context(nc.sbuf_tensor("lg_" + tag, [128, NE], F32))
                top8 = ps.enter_context(nc.sbuf_tensor("top8_" + tag, [128, 8], F32))
                eall = ps.enter_context(nc.sbuf_tensor("eall_" + tag, [128, NE], F32))
                maskb = ps.enter_context(nc.sbuf_tensor("maskb_" + tag, [128, NE], F32))
                with contextlib.ExitStack() as lns:
                    xt = [lns.enter_context(nc.sbuf_tensor(f"xt{i}_" + tag,
                                                           [128, D], F32))
                          for i in range(2)]
                    xn_tm = lns.enter_context(nc.sbuf_tensor("xn_tm_" + tag,
                                                             [128, D], BF16))
                    with contextlib.ExitStack() as pss:
                        p_tr = [pss.enter_context(
                            nc.psum_tensor(f"p_tr{i}_" + tag, [128, 128], BF16))
                            for i in range(2)]
                        p_rt = pss.enter_context(nc.psum_tensor("p_rt_" + tag,
                                                                [128, NE], F32))
                        bufs = (xt, acc[:, 0, :], xn_tm, cols, lg, top8, eall, maskb,
                                p_tr, p_rt)
                        ready = emit_ln(em, src_r, xn_fm, bufs, router_wr=W_r,
                                        entry_dma=entry_dma, entry_sdve=entry_sdve)
                with contextlib.ExitStack() as exs:
                    g_s = exs.enter_context(nc.sbuf_tensor("g_s_" + tag,
                                                           [128, IC, T], BF16))
                    h_bf = exs.enter_context(nc.sbuf_tensor("h_bf_" + tag,
                                                            [128, IC, T], BF16))
                    pg = [exs.enter_context(nc.psum_tensor(f"pg{i}_" + tag,
                                                           [128, 512], F32))
                          for i in range(2)]
                    pu = [exs.enter_context(nc.psum_tensor(f"pu{i}_" + tag,
                                                           [128, 512], F32))
                          for i in range(2)]
                    pd = [exs.enter_context(nc.psum_tensor(f"pd{i}_" + tag,
                                                           [128, 512], F32))
                          for i in range(2)]
                    experts_done = emit_experts(em, xn_fm, W_r, acc,
                                                (wg_bf, wu_bf, wd_bf), (g_s, h_bf),
                                                (pg, pu, pd), ready, entry_dma)
                with contextlib.ExitStack() as rs:
                    xt2 = [rs.enter_context(nc.sbuf_tensor(f"xt2{i}_" + tag,
                                                           [128, D], F32))
                           for i in range(2)]
                    x2t = [rs.enter_context(nc.sbuf_tensor(f"x2t{i}_" + tag,
                                                           [128, D], F32))
                           for i in range(2)]
                    emit_resid(em, src_r, acc, dst_r, (xt2, x2t), experts_done,
                               final=final)
                    em.flush()

        def ffn_pass_sparse(src_r, dst_r, final=False, tag="a"):
            entry_dma = all_dma_totals()
            entry_sdve = s_dve.c
            with contextlib.ExitStack() as ps:
                em = Em(nc)
                phase_guard(em)
                T_ = tag
                xn_fm = ps.enter_context(nc.sbuf_tensor("sxn_" + T_,
                                                        [128, KC, T], BF16))
                wg_bf = ps.enter_context(nc.sbuf_tensor("swg_" + T_,
                                                        [128, KC, I_FF], BF16))
                wu_bf = ps.enter_context(nc.sbuf_tensor("swu_" + T_,
                                                        [128, KC, I_FF], BF16))
                wd_bf = ps.enter_context(nc.sbuf_tensor("swd_" + T_,
                                                        [128, IC, D], BF16))
                slau = ps.enter_context(nc.sbuf_tensor("slau_" + T_,
                                                       [128, NT, 8],
                                                       mybir.dt.uint32))
                cols = ps.enter_context(nc.sbuf_tensor("scols_" + T_,
                                                       [128, 16], F32))
                lg = ps.enter_context(nc.sbuf_tensor("slg_" + T_, [128, NE], F32))
                top8 = ps.enter_context(nc.sbuf_tensor("stop8_" + T_, [128, 8], F32))
                eall = ps.enter_context(nc.sbuf_tensor("seall_" + T_,
                                                       [128, NE], F32))
                maskb = ps.enter_context(nc.sbuf_tensor("smaskb_" + T_,
                                                        [128, NE], F32))
                e8 = ps.enter_context(nc.sbuf_tensor("se8_" + T_, [128, 8], F32))
                w8 = ps.enter_context(nc.sbuf_tensor("sw8_" + T_, [128, 8], F32))
                tidx = ps.enter_context(nc.sbuf_tensor("stidx_" + T_, [128, 8],
                                                       mybir.dt.uint32))
                tidxf = ps.enter_context(nc.sbuf_tensor("stidxf_" + T_,
                                                        [128, 8], F32))
                posS = ps.enter_context(nc.sbuf_tensor("sposS_" + T_,
                                                       [128, NE], F32))
                poseS = ps.enter_context(nc.sbuf_tensor("sposeS_" + T_,
                                                        [128, NE], F32))
                base = ps.enter_context(nc.sbuf_tensor("sbase_" + T_, [1, NE], F32))
                slaf = ps.enter_context(nc.sbuf_tensor("sslaf_" + T_, [128, 8], F32))
                cnt_bc = ps.enter_context(nc.sbuf_tensor("scntbc_" + T_,
                                                         [128, NE], F32))
                xns = [ps.enter_context(nc.sbuf_tensor(f"sxns{i}_" + T_,
                                                       [128, RW], BF16))
                       for i in range(2)]
                sp = {"xns": xns, "e8": e8, "w8": w8, "tidx": tidx, "tidxf": tidxf,
                      "posS": posS, "poseS": poseS, "base": base, "slaf": slaf,
                      "slau": slau, "sc_cnt": {0: s_sc[0].c, 1: s_sc[1].c},
                      "base_done": 0, "posS_free": 0}
                em.on("sync", lambda e: e.dma_start(base[:], d_ecap),
                      waits=entry_dma, inc=s_x, inc_n=16)
                base_ld = s_x.c
                em.on("vector", lambda e: e.tensor_copy(posS[0:1, :], base[:]),
                      waits=[(s_x, base_ld)], inc=s_dve)
                sp["base_done"] = s_dve.c
                with contextlib.ExitStack() as lns:
                    sdump = lns.enter_context(nc.sbuf_tensor("sdump_" + T_,
                                                             [128, D], F32))
                    xt = [lns.enter_context(nc.sbuf_tensor(f"sxt{i}_" + T_,
                                                           [128, D], F32))
                          for i in range(2)]
                    with contextlib.ExitStack() as pss:
                        p_tr = [pss.enter_context(
                            nc.psum_tensor(f"sptr{i}_" + T_, [128, 128], BF16))
                            for i in range(2)]
                        p_rt = pss.enter_context(nc.psum_tensor("sprt_" + T_,
                                                                [128, NE], F32))
                        p_pos = pss.enter_context(nc.psum_tensor("sppos_" + T_,
                                                                 [128, NE], F32))
                        p_bb = pss.enter_context(nc.psum_tensor("spbb_" + T_,
                                                                [128, NE], F32))
                        p_cnt = pss.enter_context(nc.psum_tensor("spcnt_" + T_,
                                                                 [1, NE], F32))
                        sp["p_pos"] = p_pos
                        sp["p_bb"] = p_bb
                        sp["p_cnt"] = p_cnt
                        bufs = (xt, sdump[:], None, cols, lg, top8, eall, maskb,
                                p_tr, p_rt)
                        ready = emit_ln(em, src_r, xn_fm, bufs, router_wr=True,
                                        entry_dma=entry_dma,
                                        entry_sdve=entry_sdve, sp=sp)
                        em.on("tensor", lambda e: e.matmul(p_bb[:], c_onesrow[:],
                                                           base[:], start=True,
                                                           stop=True),
                              waits=[(s_dve, sp["base_done"]),
                                     (s_dve, sp["posS_free"])], inc=s_pe)
                        cb_pe = s_pe.c
                        em.on("vector", lambda e: e.tensor_copy(cnt_bc[:], p_bb[:]),
                              waits=[(s_pe, cb_pe)], inc=s_dve)
                        cnt_bc_done = s_dve.c
                rdy_act, rdy_dve = ready
                with contextlib.ExitStack() as exs:
                    xg_st = [exs.enter_context(nc.sbuf_tensor(f"sxgst{i}_" + T_,
                                                              [128, RW], BF16))
                             for i in range(2)]
                    xg_fm = [exs.enter_context(nc.sbuf_tensor(f"sxgfm{i}_" + T_,
                                                              [128, KC, CAP],
                                                              BF16))
                             for i in range(2)]
                    g_s = exs.enter_context(nc.sbuf_tensor("sgs_" + T_,
                                                           [128, IC, CAP], BF16))
                    h_bf = exs.enter_context(nc.sbuf_tensor("shb_" + T_,
                                                            [128, IC, CAP], BF16))
                    out_st = [exs.enter_context(nc.sbuf_tensor(f"sost{i}_" + T_,
                                                               [128, D], BF16))
                              for i in range(2)]
                    wcols = [exs.enter_context(nc.sbuf_tensor(f"swc{i}_" + T_,
                                                              [128, NG], F32))
                             for i in range(2)]
                    toku = [exs.enter_context(nc.sbuf_tensor(f"stk{i}_" + T_,
                                                             [128, NG],
                                                             mybir.dt.uint32))
                            for i in range(2)]
                    mcol = exs.enter_context(nc.sbuf_tensor("smc_" + T_,
                                                            [128, 8], F32))
                    pg = [exs.enter_context(nc.psum_tensor(f"spg{i}_" + T_,
                                                           [128, CAP], F32))
                          for i in range(2)]
                    pu = [exs.enter_context(nc.psum_tensor(f"spu{i}_" + T_,
                                                           [128, CAP], F32))
                          for i in range(2)]
                    pd = [exs.enter_context(nc.psum_tensor(f"spd{i}_" + T_,
                                                           [128, 512], F32))
                          for i in range(2)]
                    p_tx = [exs.enter_context(nc.psum_tensor(f"sptx{i}_" + T_,
                                                             [128, 128], BF16))
                            for i in range(2)]
                    eg_rr = d_eg.rearrange("(r p) c -> r p c", p=128)
                    scat_done = [(s_sc[0], s_sc[0].c), (s_sc[1], s_sc[1].c)]
                    pe_gate_end = pe_up_end = pe_down_end = None
                    silu_cnt = []
                    upev_cnt = []
                    dnev_cnt = []
                    ld_cnt = {0: 0, 1: 0}
                    tr_cp = {}
                    wres = {}
                    eo_st = {}
                    lt_rdy = {}

                    def emit_gload(gi):
                        w = list(scat_done) if gi == 0 else []
                        if gi >= 2:
                            w.append((s_pe, tr_cp.get(gi - 2, 0)))
                            w.append((s_dve, wres.get(gi - 2, 0)))
                        em.on("sync", lambda e, gi=gi:
                              e.dma_start(xg_st[gi % 2][:], eg_rr[gi]),
                              waits=w, inc=s_eg[gi % 2], inc_n=16)
                        ld_cnt[gi % 2] = s_eg[gi % 2].c

                    def emit_tx(gi, g, c, par):
                        rpar, ldc = gi % 2, ld_cnt[gi % 2]
                        if c < 2:
                            w = [(s_eg[rpar], ldc),
                                 tr_cp.get((gi - 1, KC - 2 + c), (s_act, 0))]
                        else:
                            w = [tr_cp.get((gi, c - 2), (s_act, 0))]
                        em.on("tensor", lambda e, c=c, gi=gi: e.transpose(
                                  p_tx[c % 2][:],
                                  xg_st[gi % 2][:, c * 128:(c + 1) * 128],
                                  c_ident[:]),
                              waits=w, inc=s_pe)
                        trp = s_pe.c
                        em.on("scalar", lambda e, c=c, g=g, par=par:
                              e.copy(xg_fm[par][:, c, g * 128:(g + 1) * 128],
                                     p_tx[c % 2][:]),
                              waits=[(s_pe, trp)], inc=s_act)
                        tr_cp[(gi, c)] = (s_act, s_act.c)
                        if c == KC - 1:
                            tr_cp[gi] = s_pe.c

                    def emit_meta(gi, g, e_i, par):
                        rpar, ldc = gi % 2, ld_cnt[gi % 2]
                        em.on("vector", lambda e, gi=gi, e_i=e_i:
                              e.tensor_scalar(mcol[:],
                                              xg_st[gi % 2][:, D:D + 8],
                                              float(e_i), None,
                                              ALU.is_equal),
                              waits=[(s_eg[rpar], ldc),
                                     (s_dve, wres.get(gi - 2, 0))],
                              inc=s_dve)
                        em.on("vector", lambda e, gi=gi: e.tensor_mul(
                                  mcol[:], mcol[:],
                                  xg_st[gi % 2][:, D + 8:D + 16]),
                              waits=[(s_dve, s_dve.c)], inc=s_dve)
                        em.on("vector", lambda e, g=g, par=par: e.reduce_sum(
                                  wcols[par][:, g:g + 1], mcol[:], axis=AX.X),
                              waits=[(s_dve, s_dve.c)], inc=s_dve)
                        em.on("vector", lambda e, g=g, e_i=e_i: e.tensor_scalar(
                                  cols[:, 11:12], c_g128p[:, g:g + 1],
                                  float(e_i * CAP), None, ALU.add),
                              waits=[(s_dve, s_dve.c),
                                     (s_dve, cnt_bc_done)], inc=s_dve)
                        em.on("vector", lambda e, e_i=e_i: e.tensor_scalar(
                                  cols[:, 12:13], cols[:, 11:12],
                                  cnt_bc[:, e_i:e_i + 1], None, ALU.is_lt),
                              waits=[(s_dve, s_dve.c)], inc=s_dve)
                        em.on("vector", lambda e, g=g, par=par: e.tensor_mul(
                                  wcols[par][:, g:g + 1],
                                  wcols[par][:, g:g + 1], cols[:, 12:13]),
                              waits=[(s_dve, s_dve.c)], inc=s_dve)
                        em.on("vector", lambda e, gi=gi: e.tensor_scalar(
                                  cols[:, 13:14],
                                  xg_st[gi % 2][:, D + 16:D + 17],
                                  256.0, None, ALU.mult),
                              waits=[(s_dve, s_dve.c)], inc=s_dve)
                        em.on("vector", lambda e, gi=gi: e.tensor_tensor(
                                  cols[:, 13:14], cols[:, 13:14],
                                  xg_st[gi % 2][:, D + 17:D + 18], ALU.add),
                              waits=[(s_dve, s_dve.c)], inc=s_dve)
                        em.on("vector", lambda e: e.tensor_scalar(
                                  cols[:, 13:14], cols[:, 13:14],
                                  float(T), None, ALU.min),
                              waits=[(s_dve, s_dve.c)], inc=s_dve)
                        em.on("vector", lambda e, g=g, par=par: e.tensor_copy(
                                  toku[par][:, g:g + 1], cols[:, 13:14]),
                              waits=[(s_dve, s_dve.c)], inc=s_dve)
                        wres[gi] = s_dve.c

                    def build_LT(e_i):
                        par2 = e_i % 2
                        items = [lambda e_i=e_i: emit_gload(e_i * NG + 0),
                                 lambda e_i=e_i: emit_gload(e_i * NG + 1)]
                        for g in range(NG):
                            gi = e_i * NG + g
                            for c in range(KC):
                                items.append(
                                    lambda gi=gi, g=g, c=c: emit_tx(gi, g, c,
                                                                    par2))
                            items.append(
                                lambda gi=gi, g=g, e_i=e_i: emit_meta(gi, g,
                                                                      e_i, par2))
                            if g + 2 < NG:
                                items.append(
                                    lambda gi2=e_i * NG + g + 2:
                                    emit_gload(gi2))

                        def fin(e_i=e_i):
                            lt_rdy[e_i] = s_act.c
                        items.append(fin)
                        return items

                    for th in build_LT(0):
                        th()
                    for e_i in range(NE):
                        pending = build_LT(e_i + 1) if e_i + 1 < NE else []
                        em.on("gpsimd", lambda e, e_i=e_i: e.dma_start(wg_bf[:],
                                                                       wg_r[e_i]),
                              waits=([(s_pe, pe_gate_end)]
                                     if pe_gate_end is not None else entry_dma),
                              inc=s_wg, inc_n=16)
                        wg_ld = s_wg.c
                        em.on("gpsimd", lambda e, e_i=e_i: e.dma_start(wu_bf[:],
                                                                       wu_r[e_i]),
                              waits=([(s_pe, pe_up_end)]
                                     if pe_up_end is not None else []),
                              inc=s_wu, inc_n=16)
                        wu_ld = s_wu.c
                        em.on("gpsimd", lambda e, e_i=e_i: e.dma_start(wd_bf[:],
                                                                       wd_r[e_i]),
                              waits=([(s_pe, pe_down_end)]
                                     if pe_down_end is not None else []),
                              inc=s_wd, inc_n=16)
                        wd_ld = s_wd.c
                        par = e_i % 2
                        xg_rdy_act = lt_rdy[e_i]
                        gidx0 = len(silu_cnt)
                        for c in range(IC):
                            gq = len(silu_cnt)
                            for k in range(KC):
                                w = []
                                if k == 0:
                                    w = [(s_wg, wg_ld), (s_act, xg_rdy_act)]
                                    if gq >= 2:
                                        w.append((s_act, silu_cnt[gq - 2]))
                                em.on("tensor", lambda e, k=k, c=c, gq=gq, par=par:
                                      e.matmul(pg[gq % 2][:],
                                               wg_bf[:, k, c * 128:(c + 1) * 128],
                                               xg_fm[par][:, k, :],
                                               start=(k == 0),
                                               stop=(k == KC - 1)),
                                      waits=w, inc=(s_pe if k == KC - 1 else None))
                            gate_pe = s_pe.c
                            em.on("scalar", lambda e, c=c, gq=gq: e.activation(
                                      g_s[:, c, :], pg[gq % 2][:], AF.Silu),
                                  waits=[(s_pe, gate_pe)], inc=s_act)
                            silu_cnt.append(s_act.c)
                        pe_gate_end = s_pe.c
                        uidx0 = len(upev_cnt)
                        for c in range(IC):
                            gq = len(upev_cnt)
                            for k in range(KC):
                                w = []
                                if k == 0:
                                    w = [(s_wu, wu_ld)]
                                    if gq >= 2:
                                        w.append((s_dve, upev_cnt[gq - 2]))
                                em.on("tensor", lambda e, k=k, c=c, gq=gq, par=par:
                                      e.matmul(pu[gq % 2][:],
                                               wu_bf[:, k, c * 128:(c + 1) * 128],
                                               xg_fm[par][:, k, :],
                                               start=(k == 0),
                                               stop=(k == KC - 1)),
                                      waits=w, inc=(s_pe if k == KC - 1 else None))
                            up_pe = s_pe.c
                            em.on("vector", lambda e, c=c, gq=gq: e.tensor_mul(
                                      h_bf[:, c, :], pu[gq % 2][:], g_s[:, c, :]),
                                  waits=[(s_pe, up_pe),
                                         (s_act, silu_cnt[gidx0 + c])],
                                  inc=s_dve)
                            upev_cnt.append(s_dve.c)
                        pe_up_end = s_pe.c
                        h_rdy = upev_cnt[-1]
                        for g in range(NG):
                            gi = e_i * NG + g
                            for q in range(4):
                                dq = len(dnev_cnt)
                                for k in range(IC):
                                    w = []
                                    if k == 0:
                                        w = [(s_wd, wd_ld), (s_dve, h_rdy)]
                                        if dq >= 2:
                                            w.append((s_dve, dnev_cnt[dq - 2]))
                                    em.on("tensor", lambda e, k=k, g=g, q=q, dq=dq:
                                          e.matmul(
                                              pd[dq % 2][:],
                                              h_bf[:, k, g * 128:(g + 1) * 128],
                                              wd_bf[:, k, q * 512:(q + 1) * 512],
                                              start=(k == 0),
                                              stop=(k == IC - 1)),
                                          waits=w,
                                          inc=(s_pe if k == IC - 1 else None))
                                    if pending:
                                        pending.pop(0)()
                                dn_pe = s_pe.c
                                w = [(s_pe, dn_pe), (s_dve, wres[gi])]
                                if gi >= 2:
                                    w.append((s_eo[gi % 2], eo_st.get(gi - 2, 0)))
                                em.on("vector", lambda e, g=g, q=q, dq=dq, par=par,
                                      gi=gi: e.tensor_scalar(
                                          out_st[gi % 2][:,
                                                         q * 512:(q + 1) * 512],
                                          pd[dq % 2][:],
                                          wcols[par][:, g:g + 1], None, ALU.mult),
                                      waits=w, inc=s_dve)
                                dnev_cnt.append(s_dve.c)
                            ost_done = s_dve.c
                            em.on("sync", lambda e, gi=gi: e.dma_start(
                                      eo_r[gi], out_st[gi % 2][:]),
                                  waits=[(s_dve, ost_done)], inc=s_eo[gi % 2],
                                  inc_n=16)
                            eo_st[gi] = s_eo[gi % 2].c
                        while pending:
                            pending.pop(0)()
                        pe_down_end = s_pe.c
                    eo_all = [(s_eo[0], s_eo[0].c), (s_eo[1], s_eo[1].c)]
                with contextlib.ExitStack() as rs:
                    xt2 = [rs.enter_context(nc.sbuf_tensor(f"sxr{i}_" + T_,
                                                           [128, D], F32))
                           for i in range(2)]
                    x2t = [rs.enter_context(nc.sbuf_tensor(f"sxw{i}_" + T_,
                                                           [128, D], F32))
                           for i in range(2)]
                    gb = [rs.enter_context(nc.sbuf_tensor(f"sgb{i}_" + T_,
                                                          [128, D], BF16))
                          for i in range(4)]
                    gs = [rs.enter_context(nc.sbuf_tensor(f"sg1{i}_" + T_,
                                                          [128, D], F32))
                          for i in range(2)]
                    add_done = {}
                    st_done = {}
                    gb_free = {0: 0, 1: 0, 2: 0, 3: 0}
                    gs_free = {0: 0, 1: 0}
                    for m in range(NT):
                        w = []
                        if m >= 2:
                            w.append((s_dve, add_done[m - 2]))
                        em.on("sync", lambda e, m=m: e.dma_start(xt2[m % 2][:],
                                                                 src_r[m]),
                              waits=w, inc=s_ld[m % 2], inc_n=16)
                        ldx = s_ld[m % 2].c
                        g_ld = {}

                        def emit_gather(k, extra):
                            w = (list(eo_all) + [(s_dve, gb_free[k % 4])]
                                 + list(extra))
                            em.on("gpsimd", lambda e, m=m, k=k:
                                  e.indirect_dma_start(
                                      gb[k % 4][:], None,
                                      d_eo,
                                      bass.IndirectOffsetOnAxis(
                                          slau[:, m, k:k + 1], 0)),
                                  waits=w, inc=s_gb[k % 4], inc_n=16)
                            g_ld[k] = s_gb[k % 4].c

                        for k in range(4):
                            emit_gather(k, [])
                        pair = {}
                        for j in range(2):
                            em.on("vector", lambda e, j=j: e.tensor_add(
                                      gs[j][:], gb[2 * j][:], gb[2 * j + 1][:]),
                                  waits=[(s_gb[2 * j], g_ld[2 * j]),
                                         (s_gb[2 * j + 1], g_ld[2 * j + 1]),
                                         (s_dve, gs_free[j])],
                                  inc=s_dve)
                            pair[j] = s_dve.c
                        for k in range(4, 8):
                            emit_gather(k, [(s_dve, pair[(k - 4) // 2])])
                        for j in range(2):
                            em.on("vector", lambda e, j=j: e.tensor_add(
                                      gs[j][:], gs[j][:], gb[2 * j][:]),
                                  waits=[(s_gb[2 * j], g_ld[4 + 2 * j]),
                                         (s_dve, pair[j])],
                                  inc=s_dve)
                            gb_free[2 * j] = s_dve.c
                            em.on("vector", lambda e, j=j: e.tensor_add(
                                      gs[j][:], gs[j][:], gb[2 * j + 1][:]),
                                  waits=[(s_gb[2 * j + 1], g_ld[5 + 2 * j]),
                                         (s_dve, s_dve.c)],
                                  inc=s_dve)
                            gb_free[2 * j + 1] = s_dve.c
                            pair[j] = s_dve.c
                        em.on("vector", lambda e: e.tensor_add(
                                  gs[0][:], gs[0][:], gs[1][:]),
                              waits=[(s_dve, pair[0]), (s_dve, pair[1])],
                              inc=s_dve)
                        gs_free[1] = s_dve.c
                        w = [(s_ld[m % 2], ldx), (s_dve, s_dve.c)]
                        if m >= 2:
                            w.append((s_st[m % 2], st_done[m - 2]))
                        em.on("vector", lambda e, m=m: e.tensor_add(
                                  x2t[m % 2][:], xt2[m % 2][:], gs[0][:]),
                              waits=w, inc=s_dve)
                        gs_free[0] = s_dve.c
                        add_done[m] = s_dve.c
                        em.on("sync", lambda e, m=m: e.dma_start(dst_r[m],
                                                                 x2t[m % 2][:]),
                              waits=[(s_dve, add_done[m])], inc=s_st[m % 2],
                              inc_n=16)
                        st_done[m] = s_st[m % 2].c
                    if final:
                        em.wait("sync", [(s_st[0], s_st[0].c),
                                         (s_st[1], s_st[1].c)])
                    em.flush()
        if stage == "ln":
            with contextlib.ExitStack() as ps:
                em = Em(nc)
                xn_fm = ps.enter_context(nc.sbuf_tensor("xn_fm_d", [128, KC, T], BF16))
                W_r = ps.enter_context(nc.sbuf_tensor("W_r_d", [128, NT, NE], F32))
                cols = ps.enter_context(nc.sbuf_tensor("cols_d", [128, 16], F32))
                lg = ps.enter_context(nc.sbuf_tensor("lg_d", [128, NE], F32))
                top8 = ps.enter_context(nc.sbuf_tensor("top8_d", [128, 8], F32))
                eall = ps.enter_context(nc.sbuf_tensor("eall_d", [128, NE], F32))
                maskb = ps.enter_context(nc.sbuf_tensor("maskb_d", [128, NE], F32))
                sqd = ps.enter_context(nc.sbuf_tensor("sqd_d", [128, D], F32))
                xt = [ps.enter_context(nc.sbuf_tensor(f"xt{i}_d", [128, D], F32))
                      for i in range(2)]
                xn_tm = ps.enter_context(nc.sbuf_tensor("xn_tm_d", [128, D], BF16))
                p_tr = [ps.enter_context(nc.psum_tensor(f"p_tr{i}_d", [128, 128], BF16))
                        for i in range(2)]
                p_rt = ps.enter_context(nc.psum_tensor("p_rt_d", [128, NE], F32))
                bufs = (xt, sqd[:], xn_tm, cols, lg, top8, eall, maskb, p_tr, p_rt)
                emit_ln(em, x_r, xn_fm, bufs, router_wr=W_r,
                        entry_dma=all_dma_totals(), dbg_dst_r=out_r)
                em.wait("sync", [(s_st[0], s_st[0].c), (s_st[1], s_st[1].c)])
                em.flush()
            return nc
        p1_dst = out_r if stage == "p1" else x2_r
        if SPARSE:
            ffn_pass_sparse(x_r, p1_dst, final=(stage == "p1"), tag="p1")
        else:
            ffn_pass(x_r, p1_dst, final=(stage == "p1"), tag="p1")
        if stage == "p1":
            return nc

        with contextlib.ExitStack() as p2:
            em = Em(nc)
            phase_guard(em)
            entry_dma = all_dma_totals()
            entry_sdve = s_dve.c
            o_tm = p2.enter_context(nc.sbuf_tensor("o_tm", [128, NT, D], BF16))
            u_fm = p2.enter_context(nc.sbuf_tensor("u_fm", [128, NH, T], BF16))
            with contextlib.ExitStack() as p2b:
                xn2 = p2b.enter_context(nc.sbuf_tensor("xn2", [128, KC, T], BF16))
                cols2 = p2b.enter_context(nc.sbuf_tensor("cols2", [128, 16], F32))
                if stage in ("p2o", "p2q", "p2kv"):
                    sq2 = p2b.enter_context(nc.sbuf_tensor("sq2", [128, D], F32))
                with contextlib.ExitStack() as lns:
                    if stage not in ("p2o", "p2q", "p2kv"):
                        sq2 = lns.enter_context(nc.sbuf_tensor("sq2", [128, D], F32))
                    xt = [lns.enter_context(nc.sbuf_tensor(f"xtb{i}", [128, D], F32))
                          for i in range(2)]
                    xn_tm = lns.enter_context(nc.sbuf_tensor("xn_tm2", [128, D], BF16))
                    with contextlib.ExitStack() as pss:
                        p_tr = [pss.enter_context(
                            nc.psum_tensor(f"p_trb{i}", [128, 128], BF16))
                            for i in range(2)]
                        bufs = (xt, sq2[:], xn_tm, cols2, None, None, None, None,
                                p_tr, None)
                        ln2_act, ln2_dve = emit_ln(em, x2_r, xn2, bufs, router_wr=None,
                                                   entry_dma=entry_dma,
                                                   entry_sdve=entry_sdve)
                c_arr = p2b.enter_context(nc.sbuf_tensor("c_arr", [128, BLK], F32))
                c_pos = p2b.enter_context(nc.sbuf_tensor("c_pos", [128, T], F32))
                c_idxT = p2b.enter_context(nc.sbuf_tensor("c_idxT", [128, 2, BLK], F32))
                c_krev = p2b.enter_context(nc.sbuf_tensor("c_krev", [128, 2], F32))
                kv_sb = p2b.enter_context(nc.sbuf_tensor("kv_sb", [128, NH, HD], F32))
                kv_bf = p2b.enter_context(nc.sbuf_tensor("kv_bf", [128, HD], BF16))
                q_full = p2b.enter_context(nc.sbuf_tensor("q_full", [128, NH, T], BF16))
                k_buf = [p2b.enter_context(nc.sbuf_tensor(f"k_buf{i}", [128, T], BF16))
                         for i in range(2)]
                v_buf = [p2b.enter_context(nc.sbuf_tensor(f"v_buf{i}", [128, NT, HD],
                                                          BF16)) for i in range(2)]
                pbuf = [p2b.enter_context(nc.sbuf_tensor(f"pbuf{i}", [128, KC, 128],
                                                         BF16)) for i in range(4)]
                qd4 = p2b.enter_context(nc.sbuf_tensor("qd4", [128, 4, BLK], F32))
                kd4 = p2b.enter_context(nc.sbuf_tensor("kd4", [128, 4, 2], F32))
                dg4 = p2b.enter_context(nc.sbuf_tensor("dg4", [128, 4, 2, BLK], F32))
                bd4 = p2b.enter_context(nc.sbuf_tensor("bd4", [128, 4], F32))
                qd_scr = p2b.enter_context(nc.sbuf_tensor("qd_scr", [128, BLK], F32))
                kqd = p2b.enter_context(nc.sbuf_tensor("kqd", [128, 2, BLK], BF16))
                kdT = p2b.enter_context(nc.sbuf_tensor("kdT", [128, 2, HD], BF16))
                vf = [p2b.enter_context(nc.sbuf_tensor(f"vf{i}", [128, T], BF16))
                      for i in range(2)]
                pa = [p2b.enter_context(nc.psum_tensor(f"pa{i}", [128, 512], F32))
                      for i in range(2)]
                pkq = [p2b.enter_context(nc.psum_tensor(f"pkq{i}", [128, BLK], F32))
                       for i in range(2)]
                po = [p2b.enter_context(nc.psum_tensor(f"po{i}", [128, HD], F32))
                      for i in range(2)]
                pkv = p2b.enter_context(nc.psum_tensor("pkv", [128, HD], F32))
                pkt = p2b.enter_context(nc.psum_tensor("pkt", [128, HD], BF16))

                em.on("sync", lambda e: e.dma_start(c_arr[:], d_arr),
                      waits=all_dma_totals() + [(s_act, ln2_act),
                                                (s_dve, ln2_dve)],
                      inc=s_x, inc_n=16)
                em.on("sync", lambda e: e.dma_start(c_pos[:], d_pos), inc=s_x, inc_n=16)
                em.on("sync", lambda e: e.dma_start(c_idxT[:], d_idxT),
                      inc=s_x, inc_n=16)
                em.on("sync", lambda e: e.dma_start(c_krev[:], d_krev),
                      inc=s_x, inc_n=16)
                consts_ld = s_x.c

                proj_evac = {}
                pe_piece_done = {}
                piece_ld_h = {}
                proj_act_h = {}
                decays_done_h = {}
                kvbf_done_h = {}
                kv_done_h = {}
                pkq_guard = {0: 0, 1: 0}
                po_guard = {0: 0, 1: 0}
                pkv_guard = 0
                pkt_guard = 0
                kv_last_dve = 0
                prev_o_pe = 0
                first_piece = [True]

                def emit_decays(grp, w0):
                    for hh in range(4):
                        h = grp * 4 + hh
                        em.on("scalar", lambda e, hh=hh, h=h: e.activation(
                                  qd4[:, hh, :], c_arr[:], AF.Exp, scale=NS(h)),
                              waits=(w0 if hh == 0 else []), inc=s_act)
                        em.on("scalar", lambda e, hh=hh, h=h: e.activation(
                                  kd4[:, hh, :], c_krev[:], AF.Exp, scale=NS(h)),
                              inc=s_act)
                        em.on("scalar", lambda e, hh=hh, h=h: e.activation(
                                  dg4[:, hh, :, :], c_idxT[:], AF.Exp, scale=NS(h)),
                              inc=s_act)
                        em.on("scalar", lambda e, hh=hh, h=h: e.activation(
                                  bd4[:, hh:hh + 1], c_c256[:], AF.Exp, scale=NS(h)),
                              inc=s_act)
                    dd = s_act.c
                    for hh in range(4):
                        decays_done_h[grp * 4 + hh] = dd

                def emit_pieces(h):
                    piece_cols = [h * 128, D + h * 128, 2 * D + h * 128,
                                  3 * D + h * 128]
                    lds = []
                    for t_i in range(4):
                        pi = 4 * h + t_i
                        if first_piece[0]:
                            w = all_dma_totals() + [(s_act, ln2_act),
                                                    (s_dve, ln2_dve)]
                            first_piece[0] = False
                        elif pi >= 4:
                            w = [(s_pe, pe_piece_done[pi - 4])]
                        else:
                            w = []
                        c0 = piece_cols[t_i]
                        em.on("gpsimd", lambda e, pi=pi, c0=c0: e.dma_start(
                                  pbuf[pi % 4][:], wq_r[:, :, c0:c0 + 128]),
                              waits=w, inc=s_pb[pi % 4], inc_n=16)
                        lds.append(s_pb[pi % 4].c)
                    piece_ld_h[h] = lds

                def emit_proj_part(h, part, pending=None):
                    piece_ld = piece_ld_h[h]

                    def drain(n=1):
                        if pending:
                            for _ in range(n):
                                if pending:
                                    pending.pop(0)()

                    if part in (0, 1):
                        t_i = part
                        buf = pbuf[(4 * h + t_i) % 4]
                        psem = s_pb[(4 * h + t_i) % 4]
                        for th in range(2):
                            for k in range(KC):
                                w = []
                                if k == 0:
                                    w = [(psem, piece_ld[t_i]),
                                         (s_act, proj_evac.get(th, 0))]
                                    if h == 0 and t_i == 0 and th == 0:
                                        w += [(s_act, ln2_act), (s_dve, ln2_dve)]
                                em.on("tensor", lambda e, k=k, th=th, buf=buf:
                                      e.matmul(pa[th][:], buf[:, k, :],
                                               xn2[:, k, th * 512:(th + 1) * 512],
                                               start=(k == 0), stop=(k == KC - 1)),
                                      waits=w, inc=(s_pe if k == KC - 1 else None))
                                drain()
                            mm_done = s_pe.c
                            dest = q_full[:, h, th * 512:(th + 1) * 512] \
                                if t_i == 0 \
                                else k_buf[h % 2][:, th * 512:(th + 1) * 512]
                            em.on("scalar", lambda e, th=th, dest=dest: e.activation(
                                      dest, pa[th][:], AF.Silu),
                                  waits=[(s_pe, mm_done)], inc=s_act)
                            proj_evac[th] = s_act.c
                        pe_piece_done[4 * h + t_i] = s_pe.c
                    elif part == 2:
                        bufv = pbuf[(4 * h + 2) % 4]
                        psemv = s_pb[(4 * h + 2) % 4]
                        for th in range(2):
                            for k in range(KC):
                                w = []
                                if k == 0:
                                    w = [(psemv, piece_ld[2]),
                                         (s_act, proj_evac.get(th, 0))]
                                em.on("tensor", lambda e, k=k, th=th, bufv=bufv:
                                      e.matmul(pa[th][:], bufv[:, k, :],
                                               xn2[:, k, th * 512:(th + 1) * 512],
                                               start=(k == 0), stop=(k == KC - 1)),
                                      waits=w, inc=(s_pe if k == KC - 1 else None))
                                drain()
                            mm_done = s_pe.c
                            em.on("scalar", lambda e, th=th, h=h: e.activation(
                                      vf[h % 2][:, th * 512:(th + 1) * 512],
                                      pa[th][:], AF.Copy),
                                  waits=[(s_pe, mm_done)], inc=s_act)
                            proj_evac[th] = s_act.c
                        vf_act = s_act.c
                        for mt in range(NT):
                            em.on("tensor", lambda e, mt=mt, h=h: e.transpose(
                                      pkt[:],
                                      vf[h % 2][:, mt * 128:(mt + 1) * 128],
                                      c_ident[:]),
                                  waits=[(s_dve, state["pkt_guard"]),
                                         (s_act, vf_act)], inc=s_pe)
                            kt_pe = s_pe.c
                            em.on("vector", lambda e, mt=mt, h=h: e.tensor_copy(
                                      v_buf[h % 2][:, mt, :], pkt[:]),
                                  waits=[(s_pe, kt_pe)], inc=s_dve)
                            state["pkt_guard"] = s_dve.c
                            drain()
                        pe_piece_done[4 * h + 2] = s_pe.c
                    else:
                        bufu = pbuf[(4 * h + 3) % 4]
                        psemu = s_pb[(4 * h + 3) % 4]
                        for th in range(2):
                            for k in range(KC):
                                w = []
                                if k == 0:
                                    w = [(psemu, piece_ld[3]),
                                         (s_act, proj_evac.get(th, 0))]
                                em.on("tensor", lambda e, k=k, th=th, bufu=bufu:
                                      e.matmul(pa[th][:], bufu[:, k, :],
                                               xn2[:, k, th * 512:(th + 1) * 512],
                                               start=(k == 0), stop=(k == KC - 1)),
                                      waits=w, inc=(s_pe if k == KC - 1 else None))
                                drain()
                            mm_done = s_pe.c
                            em.on("scalar", lambda e, th=th, h=h: e.activation(
                                      u_fm[:, h, th * 512:(th + 1) * 512], pa[th][:],
                                      AF.Copy),
                                  waits=[(s_pe, mm_done)], inc=s_act)
                            proj_evac[th] = s_act.c
                        pe_piece_done[4 * h + 3] = s_pe.c
                        proj_act_h[h] = s_act.c

                def build_scan_block(h, i):
                    hh = h % 4
                    loc = {}
                    items = []

                    def it_qd():
                        decays_done = decays_done_h[h]
                        proj_act = proj_act_h[h]
                        em.on("vector", lambda e, h=h, hh=hh, i=i: e.tensor_mul(
                                  qd_scr[:], q_full[:, h, i * BLK:(i + 1) * BLK],
                                  qd4[:, hh, :]),
                              waits=[(s_act, max(decays_done, proj_act)),
                                     (s_dve, s_dve.c), (s_pe, state["prev_o_pe"])],
                              inc=s_dve)
                        loc["qd_done"] = s_dve.c
                    items.append(it_qd)

                    def it_kq(j):
                        proj_act = proj_act_h[h]
                        w = [(s_dve, pkq_guard[j])]
                        if i == 0 and j == 0:
                            w.append((s_act, proj_act))
                        em.on("tensor", lambda e, h=h, i=i, j=j: e.matmul(
                                  pkq[j][:],
                                  k_buf[h % 2][:, i * BLK + j * 128:
                                                i * BLK + (j + 1) * 128],
                                  q_full[:, h, i * BLK:(i + 1) * BLK],
                                  start=True, stop=True),
                              waits=w, inc=s_pe)
                        kq_pe = s_pe.c
                        em.on("vector", lambda e, hh=hh, j=j: e.tensor_mul(
                                  kqd[:, j, :], pkq[j][:], dg4[:, hh, j, :]),
                              waits=[(s_pe, kq_pe)], inc=s_dve)
                        pkq_guard[j] = s_dve.c
                        loc["kqd_done"] = s_dve.c
                    items.append(lambda: it_kq(0))
                    items.append(lambda: it_kq(1))

                    def it_po(j2):
                        kv_done = kv_done_h.get(h, 0)
                        first = True
                        if i > 0:
                            em.on("tensor", lambda e, j2=j2, h=h: e.matmul(
                                      po[j2][:],
                                      qd_scr[:, j2 * 128:(j2 + 1) * 128],
                                      kv_sb[:, h, :], start=True, stop=False),
                                  waits=[(s_dve, loc["qd_done"]),
                                         (s_dve, kv_done),
                                         (s_dve, po_guard[j2])])
                            first = False
                        for js in range(2):
                            w = []
                            if js == 0:
                                w = [(s_dve, loc["kqd_done"]),
                                     (s_dve, loc["qd_done"])]
                                if first:
                                    w.append((s_dve, po_guard[j2]))
                            em.on("tensor", lambda e, i=i, j2=j2, js=js, h=h,
                                  first=first: e.matmul(
                                      po[j2][:],
                                      kqd[:, js, j2 * 128:(j2 + 1) * 128],
                                      v_buf[h % 2][:, 2 * i + js, :],
                                      start=(first and js == 0),
                                      stop=(js == 1)),
                                  waits=w, inc=(s_pe if js == 1 else None))
                            first = False
                        if j2 == 1:
                            loc["o_pe"] = s_pe.c
                            state["prev_o_pe"] = loc["o_pe"]
                    items.append(lambda: it_po(0))
                    items.append(lambda: it_po(1))

                    def it_oevac():
                        for j2 in range(2):
                            em.on("vector", lambda e, i=i, j2=j2, h=h:
                                  e.tensor_copy(
                                      o_tm[:, 2 * i + j2, h * 128:(h + 1) * 128],
                                      po[j2][:]),
                                  waits=[(s_pe, loc["o_pe"])], inc=s_dve)
                            po_guard[j2] = s_dve.c
                    items.append(it_oevac)

                    def it_kv(j):
                        decays_done = decays_done_h[h]
                        em.on("tensor", lambda e, h=h, i=i, j=j: e.transpose(
                                  pkt[:],
                                  k_buf[h % 2][:, i * BLK + j * 128:
                                                i * BLK + (j + 1) * 128],
                                  c_ident[:]),
                              waits=[(s_dve, state["pkt_guard"])], inc=s_pe)
                        kt_pe = s_pe.c
                        em.on("vector", lambda e, hh=hh, j=j: e.tensor_scalar(
                                  kdT[:, j, :], pkt[:], kd4[:, hh, j:j + 1],
                                  None, ALU.mult),
                              waits=[(s_pe, kt_pe), (s_act, decays_done)],
                              inc=s_dve)
                        state["pkt_guard"] = s_dve.c
                        kdt_done = s_dve.c
                        em.on("tensor", lambda e, j=j, h=h, i=i: e.matmul(
                                  pkv[:], kdT[:, j, :],
                                  v_buf[h % 2][:, 2 * i + j, :],
                                  start=(j == 0), stop=(j == 1)),
                              waits=[(s_dve, kdt_done)] +
                                    ([(s_dve, state["pkv_guard"])]
                                     if j == 0 else []),
                              inc=(s_pe if j == 1 else None))
                        if j == 1:
                            loc["kv_pe"] = s_pe.c
                    items.append(lambda: it_kv(0))
                    items.append(lambda: it_kv(1))

                    def it_kvup():
                        decays_done = decays_done_h[h]
                        kv_pe = loc["kv_pe"]
                        if i == 0:
                            em.on("vector", lambda e, h=h: e.tensor_copy(
                                      kv_sb[:, h, :], pkv[:]),
                                  waits=[(s_pe, kv_pe),
                                         (s_dve, state["kv_last_dve"])],
                                  inc=s_dve)
                        else:
                            em.on("vector", lambda e, h=h, hh=hh:
                                  e.scalar_tensor_tensor(
                                      kv_sb[:, h, :], kv_sb[:, h, :],
                                      bd4[:, hh:hh + 1], pkv[:],
                                      ALU.mult, ALU.add),
                                  waits=[(s_pe, kv_pe), (s_act, decays_done),
                                         (s_dve, state["kv_last_dve"])],
                                  inc=s_dve)
                        state["pkv_guard"] = s_dve.c
                        state["kv_last_dve"] = s_dve.c
                        kv_done_h[h] = s_dve.c
                        if i == NB - 1 and stage == "p2kv":
                            em.on("scalar", lambda e, h=h: e.activation(
                                      kv_bf[:], kv_sb[:, h, :], AF.Copy),
                                  waits=[(s_dve, state["kv_last_dve"])],
                                  inc=s_act)
                    items.append(it_kvup)
                    return items

                state = {"prev_o_pe": 0, "pkt_guard": 0, "pkv_guard": 0,
                         "kv_last_dve": 0}
                emit_decays(0, [(s_x, consts_ld)])
                emit_pieces(0)
                for part in range(4):
                    emit_proj_part(0, part)
                for h in range(NH):
                    for i in range(NB):
                        items = build_scan_block(h, i)
                        nxt = h + 1
                        if nxt < NH:
                            if i == 0:
                                emit_pieces(nxt)
                            emit_proj_part(nxt, i, pending=items)
                        while items:
                            items.pop(0)()
                    if (h + 1) % 4 == 0 and h + 1 < NH:
                        emit_decays((h + 1) // 4, [(s_dve, s_dve.c)])
                kv_last_dve = state["kv_last_dve"]
                po_guard = dict(po_guard)

                if stage == "p2kv":
                    scan_dve, scan_act = s_dve.c, s_act.c
                    em.on("scalar", lambda e: e.copy(sq2[:], kv_sb.ap().rearrange(
                              "p h e -> p (h e)")),
                          waits=[(s_dve, scan_dve), (s_act, scan_act)], inc=s_act)
                    aa = s_act.c
                    em.on("sync", lambda e: e.dma_start(out_r[0], sq2[:]),
                          waits=[(s_act, aa)], inc=s_st[0], inc_n=16)
                    st0 = s_st[0].c
                    em.on("scalar", lambda e: e.copy(sq2[:, 0:HD], kv_bf[:]),
                          waits=[(s_st[0], st0)], inc=s_act)
                    aa = s_act.c
                    em.on("sync", lambda e: e.dma_start(out_r[1], sq2[:]),
                          waits=[(s_act, aa)], inc=s_st[0], inc_n=16)
                    em.wait("sync", [(s_st[0], s_st[0].c)])
                    em.flush()
                    return nc
                if stage == "p2s":
                    scan_dve, scan_act = s_dve.c, s_act.c
                    dst_done = {}
                    for m in range(NT):
                        em.on("scalar", lambda e, m=m: e.copy(sq2[:], o_tm[:, m, :]),
                              waits=[(s_dve, scan_dve), (s_act, scan_act),
                                     (s_st[0], dst_done.get(m - 1, 0))],
                              inc=s_act)
                        aa = s_act.c
                        em.on("sync", lambda e, m=m: e.dma_start(out_r[m], sq2[:]),
                              waits=[(s_act, aa)], inc=s_st[0], inc_n=16)
                        dst_done[m] = s_st[0].c
                    em.wait("sync", [(s_st[0], s_st[0].c)])
                    em.flush()
                    return nc
                skip_cc = (stage == "p2x")
                if not skip_cc:
                    pass
                em.on("sync", lambda e: e.dma_start(ccin_r, kv_sb[:]),
                      waits=[(s_dve, s_dve.c), (s_act, s_act.c)], inc=s_x, inc_n=16)
                if not skip_cc:
                    em.on("gpsimd", lambda e: e.collective_compute(
                              "AllGather", ALU.bypass,
                              replica_groups=[list(range(NCORES))],
                              ins=[d_ccin.opt()], outs=[d_ccout.opt()]),
                          waits=[(s_x, s_x.c)], inc=s_cc, inc_n=1)
                else:
                    em.on("gpsimd", lambda e: e.dma_start(
                              d_ccout.rearrange("(j n) e -> j n e", j=NCORES)[0],
                              d_ccin),
                          waits=[(s_x, s_x.c)], inc=s_cc, inc_n=16)

                gath = [p2b.enter_context(nc.sbuf_tensor(f"gath{i}",
                                                         [128, NCORES, HD], F32))
                        for i in range(2)]
                cf = p2b.enter_context(nc.sbuf_tensor("cf", [128, NCORES], F32))
                kvst = p2b.enter_context(nc.sbuf_tensor("kvst", [128, HD], F32))
                kvst_bf = p2b.enter_context(nc.sbuf_tensor("kvst_bf", [128, HD], BF16))
                posd = p2b.enter_context(nc.sbuf_tensor("posd", [128, T], F32))
                qpd = p2b.enter_context(nc.sbuf_tensor("qpd", [128, T], BF16))
                kvst_use = {}
                for h in (range(NH) if not skip_cc else []):
                    w = [(s_cc, 1)]
                    if h >= 2:
                        w.append((s_dve, kvst_use[h - 2]))
                    em.on("sync", lambda e, h=h: e.dma_start(gath[h % 2][:],
                                                             ccout_r[:, :, h, :]),
                          waits=w, inc=s_g[h % 2], inc_n=16)
                    g_ld = s_g[h % 2].c
                    em.on("scalar", lambda e, h=h: e.activation(cf[:], c_delta[:],
                                                                AF.Exp, scale=NS(h)),
                          waits=[(s_dve, s_dve.c)], inc=s_act)
                    cf_act = s_act.c
                    em.on("vector", lambda e: e.tensor_mul(cf[:], cf[:], c_maskg[:]),
                          waits=[(s_act, cf_act)], inc=s_dve)
                    cf_done = s_dve.c
                    em.on("vector", lambda e, h=h: e.tensor_scalar(
                              kvst[:], gath[h % 2][:, 0, :], cf[:, 0:1],
                              None, ALU.mult),
                          waits=[(s_g[h % 2], g_ld), (s_dve, cf_done)], inc=s_dve)
                    for j in range(1, NCORES):
                        em.on("vector", lambda e, j=j, h=h: e.scalar_tensor_tensor(
                                  kvst[:], gath[h % 2][:, j, :], cf[:, j:j + 1],
                                  kvst[:], ALU.mult, ALU.add),
                              waits=[(s_dve, s_dve.c)], inc=s_dve)
                    kvst_done = s_dve.c
                    em.on("scalar", lambda e: e.activation(kvst_bf[:], kvst[:],
                                                           AF.Copy),
                          waits=[(s_dve, kvst_done)], inc=s_act)
                    kvb_act = s_act.c
                    em.on("scalar", lambda e, h=h: e.activation(posd[:], c_pos[:],
                                                                AF.Exp, scale=NS(h)),
                          waits=[(s_dve, s_dve.c)], inc=s_act)
                    pd_act = s_act.c
                    em.on("vector", lambda e, h=h: e.tensor_mul(qpd[:],
                                                                q_full[:, h, :],
                                                                posd[:]),
                          waits=[(s_act, pd_act)], inc=s_dve)
                    qpd_done = s_dve.c
                    oadd = {}
                    for m in range(NT):
                        w = [(s_dve, qpd_done), (s_act, kvb_act),
                             (s_dve, po_guard[m % 2])]
                        if m >= 2:
                            w.append((s_dve, oadd[m - 2]))
                        em.on("tensor", lambda e, m=m: e.matmul(
                                  po[m % 2][:], qpd[:, m * 128:(m + 1) * 128],
                                  kvst_bf[:], start=True, stop=True),
                              waits=w, inc=s_pe)
                        c_pe = s_pe.c
                        em.on("vector", lambda e, m=m, h=h: e.tensor_add(
                                  o_tm[:, m, h * 128:(h + 1) * 128],
                                  o_tm[:, m, h * 128:(h + 1) * 128], po[m % 2][:]),
                              waits=[(s_pe, c_pe)], inc=s_dve)
                        oadd[m] = s_dve.c
                        po_guard[m % 2] = s_dve.c
                    kvst_use[h] = s_dve.c
                p2b_dve_end = s_dve.c
                if stage in ("p2o", "p2q"):
                    dcp = sq2
                    dst_done = {}
                    if stage == "p2o":
                        for m in range(NT):
                            em.on("scalar", lambda e, m=m: e.copy(dcp[:],
                                                                  o_tm[:, m, :]),
                                  waits=[(s_dve, p2b_dve_end),
                                         (s_st[0], dst_done.get(m - 1, 0))],
                                  inc=s_act)
                            aa = s_act.c
                            em.on("sync", lambda e, m=m: e.dma_start(out_r[m], dcp[:]),
                                  waits=[(s_act, aa)], inc=s_st[0], inc_n=16)
                            dst_done[m] = s_st[0].c
                    else:
                        out_flat = d_out.rearrange("(a p) z -> p a z", p=128)
                        qf = q_full.ap().rearrange("p h t -> p (h t)")
                        for a in range(8):
                            em.on("scalar", lambda e, a=a: e.copy(
                                      dcp[:], qf[:, a * D:(a + 1) * D]),
                                  waits=[(s_dve, p2b_dve_end),
                                         (s_st[0], dst_done.get(a - 1, 0))],
                                  inc=s_act)
                            aa = s_act.c
                            em.on("sync", lambda e, a=a: e.dma_start(out_flat[:, a, :],
                                                                     dcp[:]),
                                  waits=[(s_act, aa)], inc=s_st[0], inc_n=16)
                            dst_done[a] = s_st[0].c
                    em.wait("sync", [(s_st[0], s_st[0].c)])
                    em.flush()
                    return nc
                em.flush()

            em = Em(nc)
            phase_guard(em)
            p2d_entry = all_dma_totals()
            with contextlib.ExitStack() as p2d:
                uo_fm = p2d.enter_context(nc.sbuf_tensor("uo_fm", [128, KC, T], BF16))
                wout_bf = p2d.enter_context(nc.sbuf_tensor("wout_bf", [128, KC, D],
                                                           BF16))
                sq3 = p2d.enter_context(nc.sbuf_tensor("sq3", [128, D], F32))
                cols3 = p2d.enter_context(nc.sbuf_tensor("cols3", [128, 4], F32))
                xt3 = [p2d.enter_context(nc.sbuf_tensor("xt3s", [128, D], F32))] * 2
                x3t = [p2d.enter_context(nc.sbuf_tensor("x3ts", [128, D], F32))] * 2
                p_tr = [p2d.enter_context(nc.psum_tensor(f"p_trd{i}", [128, 128],
                                                         BF16))
                        for i in range(2)]
                pw = [p2d.enter_context(nc.psum_tensor(f"pw{i}", [128, 512], F32))
                      for i in range(2)]
                C3 = lambda i: cols3[:, i:i + 1]
                em.on("gpsimd", lambda e: e.dma_start(wout_bf[:], wo_r),
                      waits=p2d_entry, inc=s_w, inc_n=16)
                wout_ld = s_w.c
                uo_done = {}
                on_act = {}
                rinv_prev = 0
                for m in range(NT):
                    w = [(s_act, on_act[m - 1])] if m >= 1 else []
                    em.on("scalar", lambda e, m=m: e.activation(sq3[:], o_tm[:, m, :],
                                                                AF.Square,
                                                                accum_out=C3(0)),
                          waits=w + [(s_dve, max(rinv_prev, p2b_dve_end))], inc=s_act)
                    ssq_done = s_act.c
                    em.on("vector", lambda e: e.tensor_scalar(C3(3), C3(0), 1.0 / D,
                                                              EPS_RMS,
                                                              ALU.mult, ALU.add),
                          waits=[(s_act, ssq_done), (s_dve, s_dve.c)], inc=s_dve)
                    ms_done = s_dve.c
                    em.on("scalar", lambda e: e.activation(C3(1), C3(3), AF.Sqrt),
                          waits=[(s_dve, ms_done)], inc=s_act)
                    sd_done = s_act.c
                    em.on("vector", lambda e: e.reciprocal(C3(2), C3(1)),
                          waits=[(s_act, sd_done)], inc=s_dve)
                    rinv_done = s_dve.c
                    rinv_prev = rinv_done
                    em.on("scalar", lambda e, m=m: e.activation(o_tm[:, m, :],
                                                                o_tm[:, m, :],
                                                                AF.Identity,
                                                                scale=C3(2)),
                          waits=[(s_dve, rinv_done)], inc=s_act)
                    on_act[m] = s_act.c
                    cp_done = {}
                    for c in range(KC):
                        w = [(s_act, on_act[m])] if c < 2 else [(s_dve, cp_done[c - 2])]
                        em.on("tensor", lambda e, c=c, m=m: e.transpose(
                                  p_tr[c % 2][:], o_tm[:, m, c * 128:(c + 1) * 128],
                                  c_ident[:]),
                              waits=w, inc=s_pe)
                        tr = s_pe.c
                        em.on("vector", lambda e, c=c, m=m: e.tensor_mul(
                                  uo_fm[:, c, m * 128:(m + 1) * 128], p_tr[c % 2][:],
                                  u_fm[:, c, m * 128:(m + 1) * 128]),
                              waits=[(s_pe, tr)], inc=s_dve)
                        cp_done[c] = s_dve.c
                    uo_done[m] = s_dve.c
                pwg_done = {}
                tile_add = {}
                st_done = {}
                dst = out_r if stage in ("p2", "p2x") else x3_r
                for m in range(NT):
                    w = []
                    if m >= 1:
                        w.append((s_dve, tile_add[m - 1]))
                    em.on("sync", lambda e, m=m: e.dma_start(xt3[m % 2][:], x2_r[m]),
                          waits=w, inc=s_ld[m % 2], inc_n=16)
                    ld = s_ld[m % 2].c
                    for q in range(4):
                        gq = m * 4 + q
                        for k in range(KC):
                            w = []
                            if k == 0:
                                w = [(s_dve, uo_done[m]), (s_w, wout_ld)]
                                if gq >= 2:
                                    w.append((s_dve, pwg_done[gq - 2]))
                            em.on("tensor", lambda e, k=k, m=m, q=q: e.matmul(
                                      pw[q % 2][:], uo_fm[:, k, m * 128:(m + 1) * 128],
                                      wout_bf[:, k, q * 512:(q + 1) * 512],
                                      start=(k == 0), stop=(k == KC - 1)),
                                  waits=w, inc=(s_pe if k == KC - 1 else None))
                        w_pe = s_pe.c
                        ww = [(s_pe, w_pe), (s_ld[m % 2], ld)]
                        if m >= 1:
                            ww.append((s_st[(m - 1) % 2], st_done[m - 1]))
                        em.on("vector", lambda e, m=m, q=q: e.tensor_add(
                                  x3t[m % 2][:, q * 512:(q + 1) * 512], pw[q % 2][:],
                                  xt3[m % 2][:, q * 512:(q + 1) * 512]),
                              waits=ww, inc=s_dve)
                        pwg_done[gq] = s_dve.c
                    tile_add[m] = s_dve.c
                    em.on("sync", lambda e, m=m, dst=dst: e.dma_start(dst[m],
                                                                     x3t[m % 2][:]),
                          waits=[(s_dve, tile_add[m])], inc=s_st[m % 2], inc_n=16)
                    st_done[m] = s_st[m % 2].c
                if stage in ("p2", "p2x"):
                    em.wait("sync", [(s_st[0], s_st[0].c), (s_st[1], s_st[1].c)])
                em.flush()
        if stage in ("p2", "p2x"):
            return nc

        if SPARSE:
            ffn_pass_sparse(x3_r, out_r, final=True, tag="p3")
        else:
            ffn_pass(x3_r, out_r, final=True, tag="p3")
    return nc


_NC_CACHE = {}


def _get_nc(stage):
    if stage not in _NC_CACHE:
        _NC_CACHE[stage] = build_nc(stage)
    return _NC_CACHE[stage]


def _host_constants():
    import ml_dtypes
    arr = np.arange(1, BLK + 1, dtype=np.float32)
    arr_b = np.tile(arr[None, :], (128, 1))
    pos = np.arange(1, T + 1, dtype=np.float32)
    pos_b = np.tile(pos[None, :], (128, 1))
    s_idx = (np.arange(2)[:, None] * 128 + np.arange(128)[None, :])
    m_idx = np.arange(BLK)
    diff = m_idx[None, None, :] - s_idx[:, :, None]
    idxT = np.where(diff >= 0, diff, 1.0e6).astype(np.float32)
    idxT_b = np.ascontiguousarray(idxT.transpose(1, 0, 2))
    krev = np.ascontiguousarray((BLK - (s_idx + 1)).astype(np.float32).T)
    c256 = np.full((128, 1), float(BLK), dtype=np.float32)
    ident = np.eye(128).astype(ml_dtypes.bfloat16)
    return arr_b, pos_b, idxT_b, krev, c256, ident


def _host_sparse_consts():
    ltri = np.tril(np.ones((128, 128), np.float32), k=-1)
    erange_b = np.tile(np.arange(NE, dtype=np.float32)[None, :], (128, 1))
    p = np.arange(128, dtype=np.float32)
    g128p_b = np.stack([g * 128 + p for g in range(NG)], axis=1).astype(np.float32)
    tok = (np.arange(NT)[None, :] * 128 + np.arange(128)[:, None]).astype(np.int64)
    tokhi_b = (tok // 256).astype(np.float32)
    toklo_b = (tok % 256).astype(np.float32)
    ones2_b = np.zeros((128, 2), np.float32)
    ones2_b[:, 0] = 1.0
    onesrow_b = np.ones((1, 128), np.float32)
    ecap_b = (np.arange(NE, dtype=np.float32) * CAP).reshape(1, NE)
    return {"ltri": np.ascontiguousarray(ltri),
            "erange_b": np.ascontiguousarray(erange_b),
            "g128p_b": np.ascontiguousarray(g128p_b),
            "tokhi_b": np.ascontiguousarray(tokhi_b),
            "toklo_b": np.ascontiguousarray(toklo_b),
            "ones2_b": ones2_b, "onesrow_b": onesrow_b, "ecap_b": ecap_b}


def kernel(L0_attn_x, L0_attn_slope_rate, L0_attn_W_qkvu, L0_attn_W_out,
           L0_ffn_router_w, L0_ffn_w_gate, L0_ffn_w_up, L0_ffn_w_down,
           _stage="full"):
    x = np.ascontiguousarray(
        np.asarray(L0_attn_x, dtype=np.float32).reshape(2 * 4096, D))
    slope = np.asarray(L0_attn_slope_rate, dtype=np.float32).reshape(NH)
    wqkvuT = np.ascontiguousarray(np.asarray(L0_attn_W_qkvu, dtype=np.float32).T)
    woutT = np.ascontiguousarray(np.asarray(L0_attn_W_out, dtype=np.float32).T)
    rT = np.ascontiguousarray(np.asarray(L0_ffn_router_w, dtype=np.float32).T)
    wg = np.ascontiguousarray(np.asarray(L0_ffn_w_gate, dtype=np.float32))
    wu = np.ascontiguousarray(np.asarray(L0_ffn_w_up, dtype=np.float32))
    wd = np.ascontiguousarray(np.asarray(L0_ffn_w_down, dtype=np.float32))

    arr_b, pos_b, idxT_b, krev, c256, ident = _host_constants()
    slope_bc = np.ascontiguousarray(np.tile(slope[None, :], (128, 1)))

    in_maps = []
    for c in range(NCORES):
        base = (c // 4) * 4
        delta = np.zeros(NCORES, dtype=np.float32)
        maskg = np.zeros(NCORES, dtype=np.float32)
        for j in range(NCORES):
            if base <= j < c:
                maskg[j] = 1.0
                delta[j] = 1024.0 * (c - 1 - j)
        in_maps.append({
            "x_in": np.ascontiguousarray(x[c * T:(c + 1) * T]),
            "wqkvuT": wqkvuT,
            "woutT": woutT,
            "rT": rT,
            "w_gate": wg,
            "w_up": wu,
            "w_down": wd,
            "slope_bc": slope_bc,
            "arr_b": arr_b,
            "pos_b": pos_b,
            "idxT_b": idxT_b,
            "krev_b": krev,
            "c256_b": c256,
            "delta_b": np.ascontiguousarray(np.tile(delta[None, :], (128, 1))),
            "maskg_b": np.ascontiguousarray(np.tile(maskg[None, :], (128, 1))),
            "ident": ident,
            **_host_sparse_consts(),
        })

    nc = _get_nc(_stage)
    res = run_bass_kernel_spmd(nc, in_maps, core_ids=list(range(NCORES)))
    out = np.concatenate([np.asarray(res.results[c]["out"])
                          for c in range(NCORES)], axis=0)
    return out.reshape(2, 4096, D).astype(np.float32)


if __name__ == "__main__":
    import sys as _sys
    stg = _sys.argv[1] if len(_sys.argv) > 1 else "p1"
    nc = build_nc(stg)
    print("built", stg, "ok")
